# version 01
# baseline (speedup 1.0000x reference)
import numpy as np
import ml_dtypes
import concourse.bass as bass
import concourse.mybir as mybir
from concourse.ap import AP
from concourse.bass_utils import run_bass_kernel_spmd

F32 = mybir.dt.float32
BF16 = mybir.dt.bfloat16
AF = mybir.ActivationFunctionType
ALU = mybir.AluOpType
AXX = mybir.AxisListType.X
BF = ml_dtypes.bfloat16

B, S, D = 4, 4096, 1024
T = S // 2
NT = T // 128
NC = D // 128
NJ = S // 128
NIB = S // 512
DH = 64
MYH = 8
EPS = 1e-5
QSC = 1.0 / 32.0
ASC = 2.0 ** -10
BSC = 0.125


class Sem:
    def __init__(self, handle):
        self.h = handle
        self.n = 0

    def inc(self, instr, k=1):
        self.n += k
        instr.then_inc(self.h, k)
        return instr


def build_program(sublayers=(0, 1, 2, 3), aphase=9, bphase=9, dbg=False):
    nc = bass.Bass("TRN2", target_bir_lowering=False, debug=False,
                   detect_race_conditions=False)
    rg = [[0, 1], [2, 3], [4, 5], [6, 7]]
    _uid = [0]

    _pidc = {}

    def pid_pair(eng):
        k = str(eng.engine)
        if k not in _pidc:
            _pidc[k] = eng.partition_id() % 2
        return _pidc[k]

    def _nm(n):
        _uid[0] += 1
        return f"{n}_{_uid[0]}"

    ein = lambda n, s: nc.dram_tensor(n, s, F32, kind="ExternalInput").ap()
    x0 = ein("x0", [T, D])
    aWqT, aWkT, aWvT = ein("aWqT", [D, D]), ein("aWkT", [D, D]), ein("aWvT", [D, D])
    ab = ein("ab", [1, 3 * D])
    fWrT, fWkT, fWvT = ein("fWrT", [D, D]), ein("fWkT", [D, D]), ein("fWvT", [D, D])
    fmur, fmuk = ein("fmur", [128, NC]), ein("fmuk", [128, NC])
    bWq, bWk, bWv, bWo = ein("bWq", [D, D]), ein("bWk", [D, D]), ein("bWv", [D, D]), ein("bWo", [D, D])
    bqb, bkb, bvb, bob = ein("bqb", [1, D]), ein("bkb", [1, D]), ein("bvb", [1, D]), ein("bob", [1, D])
    ropeC, ropeS = ein("ropeC", [T, 64]), ein("ropeS", [T, 64])
    cmask = ein("cmask", [128, 4 * 512])
    msel = ein("msel", [128, 1])
    yout = nc.dram_tensor("y", [T, D], F32, kind="ExternalOutput").ap()
    if dbg:
        dqkv = nc.dram_tensor("dqkv", [3 * S, 512], BF16, kind="ExternalOutput").ap()
        dyt = nc.dram_tensor("dyt", [S, 512], BF16, kind="ExternalOutput").ap()
        dhb = nc.dram_tensor("dhb", [T, D], BF16, kind="ExternalOutput").ap()
        dqb = nc.dram_tensor("dqb", [3 * T, D], BF16, kind="ExternalOutput").ap()
        dzt = nc.dram_tensor("dzt", [S, 8], F32, kind="ExternalOutput").ap()

    dram = lambda n, s, dt=BF16: nc.dram_tensor(n, s, dt, kind="Internal").ap()
    xb0 = dram("xb0", [T, D], F32)
    xb1 = dram("xb1", [T, D], F32)
    h_b = dram("h_b", [T, D])
    akv = dram("akv", [2 * D, T])
    ag_akv = dram("ag_akv", [4 * D, T])
    azb = dram("azb", [128, NJ], F32)
    ag_az = dram("ag_az", [256, NJ], F32)
    a_spill = dram("a_spill", [NJ * 128, T])
    bnd_in = dram("bnd_in", [1, D])
    ag_bnd = dram("ag_bnd", [2, D])
    qkvb = dram("qkvb", [3 * T, D])
    bqkv = dram("bqkv", [3 * T, 512])
    ag_bqkv = dram("ag_bqkv", [6 * T, 512])
    qkvg = dram("qkvg", [3 * S, 512])
    yb_tok = dram("yb_tok", [S, 512])
    ybx = dram("ybx", [T, 512])
    ag_y2 = dram("ag_y2", [2 * T, 512])
    yg2 = dram("yg2", [T, D])

    from contextlib import ExitStack
    with ExitStack() as _es:
        ones = _es.enter_context(nc.sbuf_tensor("ones", [1, 512], BF16))
        onesc = _es.enter_context(nc.sbuf_tensor("onesc", [1, 128], BF16))
        mur = _es.enter_context(nc.sbuf_tensor("mur", [128, NC], F32))
        muk = _es.enter_context(nc.sbuf_tensor("muk", [128, NC], F32))
        masks = _es.enter_context(nc.sbuf_tensor("masks", [128, 4 * 512], BF16))
        ropeCs = _es.enter_context(nc.sbuf_tensor("ropeCs", [128, NT * 64], F32))
        ropeSs = _es.enter_context(nc.sbuf_tensor("ropeSs", [128, NT * 64], F32))
        stats = _es.enter_context(nc.sbuf_tensor("stats", [128, 96], F32))
        ident = _es.enter_context(nc.sbuf_tensor("ident", [128, 128], BF16))
        msl = _es.enter_context(nc.sbuf_tensor("msl", [128, 1], F32))
        zst = _es.enter_context(nc.sbuf_tensor("zst", [128, 3 * NJ], F32))
        _ldw = _es.enter_context(nc.semaphore("s_ldw"))
        _ldx = _es.enter_context(nc.semaphore("s_ldx"))
        _mm = _es.enter_context(nc.semaphore("s_mm"))
        _act = _es.enter_context(nc.semaphore("s_act"))
        _dve = _es.enter_context(nc.semaphore("s_dve"))
        _gp = _es.enter_context(nc.semaphore("s_gp"))
        _cc = _es.enter_context(nc.semaphore("s_cc"))
        _sp = _es.enter_context(nc.semaphore("s_sp"))
        _st = _es.enter_context(nc.semaphore("s_st"))
        _id = _es.enter_context(nc.semaphore("s_id"))
        block = _es.enter_context(nc.Block())

        ldw, ldx, mm, act, dve = Sem(_ldw), Sem(_ldx), Sem(_mm), Sem(_act), Sem(_dve)
        gp, cc, sp, st, idm = Sem(_gp), Sem(_cc), Sem(_sp), Sem(_st), Sem(_id)

        @block.vector
        def _(v):
            v.memset(ones[:, :], 1.0)
            v.memset(onesc[:, :], 1.0)

        @block.sync
        def _(sy):
            sp.inc(sy.dma_start(out=msl[:, :], in_=msel[:, :]), 16)
            sp.inc(sy.dma_start(out=mur[:, :], in_=fmur[:, :]), 16)
            sp.inc(sy.dma_start(out=muk[:, :], in_=fmuk[:, :]), 16)
            sp.inc(sy.dma_start(
                out=ropeCs[:, :].rearrange("p (n c) -> p n c", c=64),
                in_=ropeC.rearrange("(n p) c -> p n c", p=128)), 16)
            sp.inc(sy.dma_start(
                out=ropeSs[:, :].rearrange("p (n c) -> p n c", c=64),
                in_=ropeS.rearrange("(n p) c -> p n c", p=128)), 16)

        @block.gpsimd
        def _(g):
            gp.inc(g.dma_start(out=masks[:, :], in_=cmask[:, :]), 16)
            g.memset(ident[:, :], 1.0)
            idm.inc(g.affine_select(
                ident[:, :], ident[:, :],
                pattern=[[-1, 128]], compare_op=ALU.is_equal,
                fill=0.0, base=0, channel_multiplier=1), 1)


        def emit_ln(xin, xf, hsc, ev, nslots, sw):
            ldx0 = ldx.n

            @block.sync
            def _(sy):
                ldx.inc(sy.dma_start(
                    out=xf[:, :].rearrange("p (n d) -> p n d", d=1024),
                    in_=xin.rearrange("(n p) d -> p n d", p=128)), 16)

            dve0 = dve.n

            @block.vector
            def _(v):
                v.wait_ge(ldx.h, ldx0 + 16)
                for tt in range(NT):
                    xt = xf[:, tt * 1024:(tt + 1) * 1024]
                    v.reduce_sum(stats[:, tt:tt + 1], xt, axis=AXX)
                v.tensor_copy(stats[:, 90:91], stats[:, 0:1])
                dve.inc(v.tensor_scalar(
                    stats[:, 16:32], stats[:, 0:16], -1.0 / D, None, ALU.mult))

            act0 = act.n

            @block.scalar
            def _(s):
                s.wait_ge(dve.h, dve0 + 1)
                for tt in range(NT):
                    xt = xf[:, tt * 1024:(tt + 1) * 1024]
                    s.activation(
                        hsc[:, :], xt, AF.Square,
                        bias=stats[:, 16 + tt:17 + tt],
                        accum_out=stats[:, 32 + tt:33 + tt])
                act.inc(s.activation(hsc[:, 0:1], hsc[:, 0:1], AF.Square), 1)

            dve1 = dve.n

            @block.vector
            def _(v):
                v.wait_ge(act.h, act0 + 1)
                dve.inc(v.tensor_scalar(
                    stats[:, 48:64], stats[:, 32:48], 1.0 / D, EPS,
                    ALU.mult, ALU.add))

            act1 = act.n

            @block.scalar
            def _(s):
                s.wait_ge(dve.h, dve1 + 1)
                act.inc(s.activation(stats[:, 80:96], stats[:, 48:64], AF.Sqrt), 1)

            dve2 = dve.n

            @block.vector
            def _(v):
                v.wait_ge(act.h, act1 + 1)
                v.reciprocal(stats[:, 48:64], stats[:, 80:96])
                v.tensor_copy(stats[:, 90:91], stats[:, 48:49])
                v.tensor_mul(stats[:, 64:80], stats[:, 16:32], stats[:, 48:64])
                dve.inc(v.tensor_copy(stats[:, 90:91], stats[:, 64:65]))

            act2 = act.n
            sp0 = sp.n

            @block.scalar
            def _(s):
                s.wait_ge(dve.h, dve2 + 1)
                for tt in range(NT):
                    if tt >= nslots:
                        s.wait_ge(sp.h, sp0 + (tt - nslots + 1) * 16)
                    xt = xf[:, tt * 1024:(tt + 1) * 1024]
                    act.inc(s.activation(
                        ev[:, (tt % nslots) * sw:(tt % nslots) * sw + 1024],
                        xt, AF.Identity,
                        scale=stats[:, 48 + tt:49 + tt],
                        bias=stats[:, 64 + tt:65 + tt]), 1)

            @block.sync
            def _(sy):
                for tt in range(NT):
                    sy.wait_ge(act.h, act2 + tt + 1)
                    sp.inc(sy.dma_start(
                        out=h_b[tt * 128:(tt + 1) * 128, :],
                        in_=ev[:, (tt % nslots) * sw:(tt % nslots) * sw + 1024]), 16)
            return sp0

        def emit_attn_a(xin, xout):
            AP1, AP2, AP3 = aphase >= 2, aphase >= 3, aphase >= 4
            with (
                nc.sbuf_tensor(_nm("a_qT"), [128, NC * T], BF16) as qT,
                nc.sbuf_tensor(_nm("a_ev"), [128, 4 * 2048], BF16) as ev,
            ):
                _Wg = nc.sbuf_tensor(_nm("a_W"), [128, 3 * NC * 1024], BF16)
                _bg = nc.sbuf_tensor(_nm("a_b"), [1, 3 * D], BF16)
                W = _Wg.__enter__()
                abias = _bg.__enter__()
                ldw0 = ldw.n

                @block.gpsimd
                def _(g):
                    for wi, w in enumerate((aWkT, aWvT, aWqT)):
                        ldw.inc(g.dma_start(
                            out=W[:, wi * NC * 1024:(wi + 1) * NC * 1024]
                                .rearrange("p (c f) -> p c f", f=1024),
                            in_=w.rearrange("(c p) f -> p c f", p=128)), 16)
                    ldw.inc(g.dma_start(out=abias[:, :], in_=ab[:, :]), 16)

                with (
                    nc.sbuf_tensor(_nm("a_xf"), [128, NT * 1024], F32) as xf,
                    nc.sbuf_tensor(_nm("a_sc"), [128, 1024], F32) as hsc,
                ):
                    sp0 = emit_ln(xin, xf, hsc, ev, 4, 2048)

                if not AP1:
                    return
                with (
                    nc.sbuf_tensor(_nm("a_hT"), [128, NC * T], BF16) as hT,
                    nc.psum_tensor(_nm("a_pk0"), [128, 2048], F32) as pk0,
                    nc.psum_tensor(_nm("a_pk1"), [128, 2048], F32) as pk1,
                ):
                    ldx1 = ldx.n
                    sp_h_done = sp.n

                    @block.sync
                    def _(sy):
                        sy.wait_ge(sp.h, sp0 + NT * 16)
                        for dc in range(NC):
                            ldx.inc(sy.dma_start_transpose(
                                hT[:, dc * T:(dc + 1) * T],
                                h_b[:, dc * 128:(dc + 1) * 128]), 16)

                    def wsl(wi, dc, f0, fn):
                        base = wi * 8192 + dc * 1024 + f0
                        return W[:, base:base + fn]

                    mm0 = mm.n
                    act3 = act.n
                    sp1 = sp.n

                    @block.tensor
                    def _(t):
                        t.wait_ge(ldw.h, ldw0 + 64)
                        t.wait_ge(ldx.h, ldx1 + NC * 16)
                        for fc in range(NC):
                            ps = (pk0, pk1)[fc % 2]
                            if fc >= 2:
                                t.wait_ge(act.h, act3 + fc - 1)
                            for dc in range(NC):
                                for ib in range(4):
                                    t.matmul(ps[:, ib * 512:(ib + 1) * 512],
                                             wsl(0, dc, fc * 128, 128),
                                             hT[:, dc * T + ib * 512: dc * T + (ib + 1) * 512],
                                             start=(dc == 0), stop=False)
                            for ib in range(4):
                                last = ib == 3
                                m_ = t.matmul(
                                    ps[:, ib * 512:(ib + 1) * 512],
                                    abias[0:1, D + fc * 128: D + (fc + 1) * 128],
                                    ones[0:1, 0:512], start=False, stop=True)
                                if last:
                                    mm.inc(m_, 1)

                    @block.scalar
                    def _(s):
                        for fc in range(NC):
                            s.wait_ge(mm.h, mm0 + fc + 1)
                            if fc == 0:
                                s.wait_ge(sp.h, sp1)
                            if fc >= 4:
                                s.wait_ge(sp.h, sp1 + (fc - 3) * 16)
                            act.inc(s.activation(
                                ev[:, (fc % 4) * 2048:(fc % 4) * 2048 + 2048],
                                (pk0, pk1)[fc % 2][:, :], AF.Identity), 1)

                    @block.sync
                    def _(sy):
                        for fc in range(NC):
                            sy.wait_ge(act.h, act3 + fc + 1)
                            sp.inc(sy.dma_start(
                                out=akv[fc * 128:(fc + 1) * 128, :],
                                in_=ev[:, (fc % 4) * 2048:(fc % 4) * 2048 + 2048]), 16)

                    mm1 = mm.n
                    act4 = act.n
                    sp2 = sp.n

                    @block.tensor
                    def _(t):
                        for tt in range(NT):
                            ps = (pk0, pk1)[tt % 2]
                            pv = ps[:, 0:1024]
                            if tt >= 2:
                                t.wait_ge(act.h, act4 + tt - 1)
                            for dc in range(NC):
                                for fb in range(2):
                                    t.matmul(pv[:, fb * 512:(fb + 1) * 512],
                                             hT[:, dc * T + tt * 128: dc * T + (tt + 1) * 128],
                                             wsl(1, dc, fb * 512, 512),
                                             start=(dc == 0), stop=False)
                            for fb in range(2):
                                m_ = t.matmul(
                                    pv[:, fb * 512:(fb + 1) * 512],
                                    onesc[0:1, 0:128],
                                    abias[0:1, 2 * D + fb * 512: 2 * D + (fb + 1) * 512],
                                    start=False, stop=True)
                                if fb == 1:
                                    mm.inc(m_, 1)

                    @block.scalar
                    def _(s):
                        for tt in range(NT):
                            s.wait_ge(mm.h, mm1 + tt + 1)
                            if tt == 0:
                                s.wait_ge(sp.h, sp2)
                            if tt >= 4:
                                s.wait_ge(sp.h, sp2 + (tt - 3) * 16)
                            act.inc(s.activation(
                                ev[:, (tt % 4) * 2048:(tt % 4) * 2048 + 1024],
                                (pk0, pk1)[tt % 2][:, 0:1024], AF.Identity), 1)

                    @block.sync
                    def _(sy):
                        for tt in range(NT):
                            sy.wait_ge(act.h, act4 + tt + 1)
                            sp.inc(sy.dma_start(
                                out=AP(akv.tensor, D * T + tt * 128 * D,
                                       [[D, 128], [1, D]]),
                                in_=ev[:, (tt % 4) * 2048:(tt % 4) * 2048 + 1024]), 16)

                    cc0 = cc.n
                    kv_done = sp.n

                    @block.gpsimd
                    def _(g):
                        g.wait_ge(sp.h, kv_done)
                        cc.inc(g.collective_compute(
                            "AllGather", ALU.bypass, replica_groups=rg,
                            ins=[akv[:, :].opt()],
                            outs=[ag_akv[:, :].opt()]), 1)

                    mm2 = mm.n
                    act5 = act.n

                    @block.tensor
                    def _(t):
                        for fc in range(NC):
                            ps = (pk0, pk1)[fc % 2]
                            if fc >= 2:
                                t.wait_ge(act.h, act5 + fc - 1)
                            for dc in range(NC):
                                for ib in range(4):
                                    t.matmul(ps[:, ib * 512:(ib + 1) * 512],
                                             wsl(2, dc, fc * 128, 128),
                                             hT[:, dc * T + ib * 512: dc * T + (ib + 1) * 512],
                                             start=(dc == 0), stop=False)
                            for ib in range(4):
                                m_ = t.matmul(
                                    ps[:, ib * 512:(ib + 1) * 512],
                                    abias[0:1, fc * 128:(fc + 1) * 128],
                                    ones[0:1, 0:512], start=False, stop=True)
                                if ib == 3:
                                    mm.inc(m_, 1)

                    @block.scalar
                    def _(s):
                        for fc in range(NC):
                            s.wait_ge(mm.h, mm2 + fc + 1)
                            act.inc(s.activation(
                                qT[:, fc * T:(fc + 1) * T],
                                (pk0, pk1)[fc % 2][:, :], AF.Identity, scale=ASC), 1)

                _bg.__exit__(None, None, None)
                _Wg.__exit__(None, None, None)

                if not AP2:
                    return
                with nc.sbuf_tensor(_nm("a_V"), [128, NJ * 1024], BF16) as Vsb:
                    with (
                        nc.sbuf_tensor(_nm("a_kT"), [128, NC * S], BF16) as ktT,
                        nc.sbuf_tensor(_nm("a_A"), [128, 2 * 2048], BF16) as Asl,
                        nc.psum_tensor(_nm("a_ps0"), [128, 2048], F32) as ps0,
                        nc.psum_tensor(_nm("a_ps1"), [128, 2048], F32) as ps1,
                    ):
                        ldx2 = ldx.n
                        qdone = act.n

                        @block.sync
                        def _(sy):
                            sy.wait_ge(act.h, qdone)
                            sy.wait_ge(cc.h, cc0 + 1)
                            for r in range(2):
                                for dc in range(NC):
                                    ldx.inc(sy.dma_start(
                                        out=ktT[:, dc * S + r * T: dc * S + r * T + T],
                                        in_=ag_akv[2 * D * r + dc * 128: 2 * D * r + (dc + 1) * 128, :]), 16)

                        mm3 = mm.n
                        act6 = act.n
                        sp3 = sp.n

                        @block.tensor
                        def _(t):
                            t.wait_ge(act.h, qdone)
                            t.wait_ge(ldx.h, ldx2 + 256)
                            for jt in range(NJ):
                                ps = (ps0, ps1)[jt % 2]
                                if jt >= 2:
                                    t.wait_ge(act.h, act6 + jt - 1)
                                for dc in range(NC):
                                    for ib in range(4):
                                        m_ = t.matmul(
                                            ps[:, ib * 512:(ib + 1) * 512],
                                            ktT[:, dc * S + jt * 128: dc * S + (jt + 1) * 128],
                                            qT[:, dc * T + ib * 512: dc * T + (ib + 1) * 512],
                                            start=(dc == 0), stop=(dc == NC - 1))
                                        if dc == NC - 1 and ib == 3:
                                            mm.inc(m_, 1)

                        @block.scalar
                        def _(s):
                            for jt in range(NJ):
                                s.wait_ge(mm.h, mm3 + jt + 1)
                                if jt >= 2:
                                    s.wait_ge(sp.h, sp3 + (jt - 1) * 16)
                                act.inc(s.activation(
                                    Asl[:, (jt % 2) * 2048:(jt % 2) * 2048 + 2048],
                                    (ps0, ps1)[jt % 2][:, :], AF.Exp,
                                    accum_out=zst[:, jt:jt + 1]), 1)
                            act.inc(s.activation(stats[:, 80:81], stats[:, 80:81], AF.Square), 1)

                        @block.sync
                        def _(sy):
                            for jt in range(NJ):
                                sy.wait_ge(act.h, act6 + jt + 1)
                                sp.inc(sy.dma_start(
                                    out=a_spill[jt * 128:(jt + 1) * 128, :],
                                    in_=Asl[:, (jt % 2) * 2048:(jt % 2) * 2048 + 2048]), 16)
                            sy.wait_ge(act.h, act6 + NJ + 1)
                            sp.inc(sy.dma_start(out=azb[:, :], in_=zst[:, 0:NJ]), 16)

                        cc1 = cc.n
                        z_done = sp.n

                        @block.gpsimd
                        def _(g):
                            g.wait_ge(sp.h, z_done)
                            cc.inc(g.collective_compute(
                                "AllGather", ALU.bypass, replica_groups=rg,
                                ins=[azb[:, :].opt()], outs=[ag_az[:, :].opt()]), 1)

                        ldx3 = ldx.n

                        @block.sync
                        def _(sy):
                            sy.wait_ge(cc.h, cc1 + 1)
                            ldx.inc(sy.dma_start(
                                out=zst[:, NJ:3 * NJ].rearrange("p (r j) -> p r j", r=2),
                                in_=ag_az.rearrange("(r p) j -> p r j", p=128)), 16)

                        dve3 = dve.n

                        @block.vector
                        def _(v):
                            v.wait_ge(ldx.h, ldx3 + 16)
                            v.tensor_add(zst[:, 0:NJ], zst[:, NJ:2 * NJ], zst[:, 2 * NJ:3 * NJ])
                            dve.inc(v.reciprocal(zst[:, 0:NJ], zst[:, 0:NJ]))

                    if not AP3:
                        return
                    _ngrp = 1 if aphase == 5 else (2 if aphase == 6 else (3 if aphase == 7 else 4))
                    with (
                        nc.sbuf_tensor(_nm("a_Ar"), [128, NJ * 512], BF16) as Ar,
                        nc.sbuf_tensor(_nm("a_xr"), [128, 4 * 1024], F32) as xr,
                        nc.sbuf_tensor(_nm("a_xo"), [128, 4 * 1024], F32) as xo,
                        nc.psum_tensor(_nm("a_po0"), [128, 1024], F32) as po0,
                        nc.psum_tensor(_nm("a_po1"), [128, 1024], F32) as po1,
                        nc.psum_tensor(_nm("a_po2"), [128, 1024], F32) as po2,
                        nc.psum_tensor(_nm("a_po3"), [128, 1024], F32) as po3,
                    ):
                        pos = (po0, po1, po2, po3)
                        ldx4 = ldx.n

                        @block.sync
                        def _(sy):
                            sy.wait_ge(act.h, qdone)
                            sy.wait_ge(cc.h, cc0 + 1)
                            for half in range(2):
                                ldx.inc(sy.dma_start(
                                    out=Vsb[:, half * 16 * 1024:(half + 1) * 16 * 1024]
                                        .rearrange("p (j d) -> p j d", d=1024),
                                    in_=AP(ag_akv.tensor, (2 * D * half + D) * T,
                                           [[1024, 128], [128 * 1024, 16], [1, 1024]])), 16)

                        dve4 = dve.n

                        @block.vector
                        def _(v):
                            v.wait_ge(dve.h, dve3 + 1)
                            v.wait_ge(ldx.h, ldx4 + 32)
                            for jt in range(NJ):
                                dve.inc(v.tensor_scalar_mul(
                                    Vsb[:, jt * 1024:(jt + 1) * 1024],
                                    Vsb[:, jt * 1024:(jt + 1) * 1024],
                                    zst[:, jt:jt + 1]))

                        if aphase == 4:
                            return
                        ldx5 = ldx.n
                        mm4 = mm.n
                        dve5 = dve.n

                        st0 = st.n

                        @block.sync
                        def _(sy):
                            for grp in range(_ngrp):
                                if grp >= 1:
                                    sy.wait_ge(mm.h, mm4 + grp * 4)
                                for q in range(4):
                                    ldx.inc(sy.dma_start(
                                        out=Ar[:, q * 8 * 512:(q + 1) * 8 * 512]
                                            .rearrange("p (j i) -> p j i", i=512),
                                        in_=AP(a_spill.tensor,
                                               q * 8 * 128 * T + grp * 512,
                                               [[T, 128], [128 * T, 8], [1, 512]])), 16)
                                if grp >= 1:
                                    sy.wait_ge(dve.h, dve5 + grp * 4)
                                for it in range(4):
                                    tt = grp * 4 + it
                                    ldx.inc(sy.dma_start(
                                        out=xr[:, it * 1024:(it + 1) * 1024],
                                        in_=xin[tt * 128:(tt + 1) * 128, :]), 16)
                                if grp >= 1:
                                    pg = grp - 1
                                    for it in range(4):
                                        tt = pg * 4 + it
                                        sy.wait_ge(dve.h, dve5 + pg * 4 + it + 1)
                                        st.inc(sy.dma_start(
                                            out=xout[tt * 128:(tt + 1) * 128, :],
                                            in_=xo[:, it * 1024:(it + 1) * 1024]), 16)
                            pg = _ngrp - 1
                            for it in range(4):
                                tt = pg * 4 + it
                                sy.wait_ge(dve.h, dve5 + pg * 4 + it + 1)
                                st.inc(sy.dma_start(
                                    out=xout[tt * 128:(tt + 1) * 128, :],
                                    in_=xo[:, it * 1024:(it + 1) * 1024]), 16)

                        @block.tensor
                        def _(t):
                            t.wait_ge(dve.h, dve4 + NJ)
                            for grp in range(_ngrp):
                                t.wait_ge(ldx.h, ldx5 + grp * 8 * 16 + 4 * 16)
                                if grp >= 1:
                                    t.wait_ge(dve.h, dve5 + grp * 4)
                                for jt in range(NJ):
                                    for it in range(4):
                                        a_sl = Ar[:, jt * 512 + it * 128:
                                                  jt * 512 + (it + 1) * 128]
                                        for fb in range(2):
                                            m_ = t.matmul(
                                                pos[it][:, fb * 512:(fb + 1) * 512],
                                                a_sl,
                                                Vsb[:, jt * 1024 + fb * 512: jt * 1024 + (fb + 1) * 512],
                                                start=(jt == 0), stop=(jt == NJ - 1))
                                            if jt == NJ - 1 and fb == 1:
                                                mm.inc(m_, 1)

                        @block.vector
                        def _(v):
                            for grp in range(_ngrp):
                                if grp >= 1:
                                    v.wait_ge(st.h, st0 + grp * 4 * 16)
                                for it in range(4):
                                    v.wait_ge(mm.h, mm4 + grp * 4 + it + 1)
                                    v.wait_ge(ldx.h, ldx5 + (grp + 1) * 8 * 16)
                                    dve.inc(v.tensor_add(
                                        xo[:, it * 1024:(it + 1) * 1024],
                                        xr[:, it * 1024:(it + 1) * 1024],
                                        pos[it][:, :]))

        def emit_ffn(xin, xout):
            NH2 = NT // 2
            TH = NH2 * 128
            with (
                nc.sbuf_tensor(_nm("f_W"), [128, 3 * NC * 1024], BF16) as W,
                nc.sbuf_tensor(_nm("f_prevA"), [8, 128], BF16) as prevA,
                nc.sbuf_tensor(_nm("f_prev"), [128, 16], BF16) as prevT,
                nc.psum_tensor(_nm("f_pb"), [128, 8], BF16) as pbP,
                nc.psum_tensor(_nm("f_pp0"), [128, 1024], F32) as pp0,
                nc.psum_tensor(_nm("f_pp1"), [128, 1024], F32) as pp1,
            ):
                f_mm, f_dve, f_act, f_st, f_sp = mm.n, dve.n, act.n, st.n, sp.n
                ldw0 = ldw.n

                @block.gpsimd
                def _(g):
                    g.wait_ge(mm.h, f_mm)
                    g.wait_ge(dve.h, f_dve)
                    g.wait_ge(act.h, f_act)
                    g.wait_ge(st.h, f_st)
                    for wi, w in enumerate((fWrT, fWkT, fWvT)):
                        ldw.inc(g.dma_start(
                            out=W[:, wi * NC * 1024:(wi + 1) * NC * 1024]
                                .rearrange("p (c f) -> p c f", f=1024),
                            in_=w.rearrange("(c p) f -> p c f", p=128)), 16)

                with (
                    nc.sbuf_tensor(_nm("f_xf"), [128, NT * 1024], F32) as xf,
                    nc.sbuf_tensor(_nm("f_sc"), [128, 1024], F32) as hsc,
                    nc.sbuf_tensor(_nm("f_ev"), [128, 2 * 1024], BF16) as ev2,
                ):
                    @block.sync
                    def _(sy):
                        sy.wait_ge(st.h, f_st)
                    sp0 = emit_ln(xin, xf, hsc, ev2, 2, 1024)

                sp_b = sp.n

                @block.sync
                def _(sy):
                    sy.wait_ge(sp.h, sp0 + NT * 16)
                    sp.inc(sy.dma_start(out=bnd_in[0:1, :], in_=h_b[T - 1:T, :]), 16)

                cc_b = cc.n

                @block.gpsimd
                def _(g):
                    g.wait_ge(sp.h, sp_b + 16)
                    cc.inc(g.collective_compute(
                        "AllGather", ALU.bypass, replica_groups=rg,
                        ins=[bnd_in[:, :].opt()], outs=[ag_bnd[:, :].opt()]), 1)

                ldxb = ldx.n

                @block.sync
                def _(sy):
                    sy.wait_ge(cc.h, cc_b + 1)
                    ldx.inc(sy.dma_start(
                        out=prevA[:, :],
                        in_=AP(ag_bnd.tensor, 0, [[128, 8], [1, 128]])), 16)

                mm_b = mm.n

                @block.tensor
                def _(t):
                    t.wait_ge(ldx.h, ldxb + 16)
                    t.wait_ge(idm.h, idm.n)
                    mm.inc(t.transpose(pbP[:, :], prevA[:, :], ident[0:8, 0:8]), 1)

                dve_b = dve.n

                @block.vector
                def _(v):
                    v.wait_ge(mm.h, mm_b + 1)
                    v.wait_ge(sp.h, 16)
                    dve.inc(v.tensor_scalar_mul(prevT[:, 0:8], pbP[:, :], msl[:, 0:1]))

                for hh in range(2):
                    with (
                        nc.sbuf_tensor(_nm("f_hT"), [128, NC * 1024], BF16) as hT,
                        nc.sbuf_tensor(_nm("f_xrT"), [128, NC * 1024], BF16) as xrT,
                        nc.sbuf_tensor(_nm("f_xkT"), [128, NC * 1024], BF16) as xkT,
                        nc.sbuf_tensor(_nm("f_r"), [128, NH2 * 1024], BF16) as rti,
                        nc.sbuf_tensor(_nm("f_ka"), [128, NC * 1024], BF16) as kaT,
                        nc.sbuf_tensor(_nm("f_rs"), [128, 2 * 1024], F32) as rsc,
                        nc.sbuf_tensor(_nm("f_tmp"), [128, 1024], BF16) as tmpc,
                        nc.sbuf_tensor(_nm("f_t1"), [128, 1024], F32) as t1,
                        nc.sbuf_tensor(_nm("f_xr"), [128, 2 * 1024], F32) as xrr,
                        nc.sbuf_tensor(_nm("f_xo"), [128, 2 * 1024], F32) as xor_,
                    ):
                        pcol = 0 if hh == 0 else 8
                        ldxh = ldx.n
                        if hh == 1:
                            h0_mm, h0_dve, h0_act, h0_st = mm.n, dve.n, act.n, st.n
                        else:
                            h0_mm = h0_dve = h0_act = h0_st = 0

                        @block.sync
                        def _(sy):
                            if hh == 0:
                                sy.wait_ge(sp.h, sp0 + NT * 16)
                            else:
                                sy.wait_ge(dve.h, h0_dve)
                                sy.wait_ge(act.h, h0_act)
                            for dc in range(NC):
                                ldx.inc(sy.dma_start_transpose(
                                    hT[:, dc * 1024:(dc + 1) * 1024],
                                    h_b[hh * TH:(hh + 1) * TH,
                                        dc * 128:(dc + 1) * 128]), 16)

                        dve_m = dve.n

                        @block.vector
                        def _(v):
                            v.wait_ge(dve.h, dve_b + 1)
                            if hh == 1:
                                v.wait_ge(dve.h, dve_sv)
                                v.wait_ge(mm.h, h0_mm)
                            for dc in range(NC):
                                v.wait_ge(ldx.h, ldxh + (dc + 1) * 16)
                                hc = hT[:, dc * 1024:(dc + 1) * 1024]
                                pv = prevT[:, pcol + dc:pcol + dc + 1]
                                v.tensor_sub(tmpc[:, 1:1024], hc[:, 1:1024], hc[:, 0:1023])
                                v.tensor_sub(tmpc[:, 0:1], hc[:, 0:1], pv)
                                v.scalar_tensor_tensor(
                                    xrT[:, dc * 1024 + 1:(dc + 1) * 1024],
                                    tmpc[:, 1:1024], mur[:, dc:dc + 1], hc[:, 0:1023],
                                    ALU.mult, ALU.add)
                                v.scalar_tensor_tensor(
                                    xrT[:, dc * 1024:dc * 1024 + 1],
                                    tmpc[:, 0:1], mur[:, dc:dc + 1], pv,
                                    ALU.mult, ALU.add)
                                v.scalar_tensor_tensor(
                                    xkT[:, dc * 1024 + 1:(dc + 1) * 1024],
                                    tmpc[:, 1:1024], muk[:, dc:dc + 1], hc[:, 0:1023],
                                    ALU.mult, ALU.add)
                                dve.inc(v.scalar_tensor_tensor(
                                    xkT[:, dc * 1024:dc * 1024 + 1],
                                    tmpc[:, 0:1], muk[:, dc:dc + 1], pv,
                                    ALU.mult, ALU.add))
                            if hh == 0:
                                dve.inc(v.tensor_copy(
                                    prevT[:, 8:16],
                                    hT[:, :].rearrange("p (c t) -> p c t", t=1024)[:, :, 1023]))

                        if hh == 0:
                            dve_sv = dve.n

                        mm_r = mm.n
                        act_r = act.n

                        @block.tensor
                        def _(t):
                            t.wait_ge(ldw.h, ldw0 + 16)
                            for tt in range(NH2):
                                ps = (pp0, pp1)[tt % 2]
                                if tt == 0:
                                    t.wait_ge(dve.h, dve_m + NC)
                                if tt >= 2:
                                    t.wait_ge(act.h, act_r + tt - 1)
                                for dc in range(NC):
                                    for fb in range(2):
                                        m_ = t.matmul(
                                            ps[:, fb * 512:(fb + 1) * 512],
                                            xrT[:, dc * 1024 + tt * 128: dc * 1024 + (tt + 1) * 128],
                                            W[:, dc * 1024 + fb * 512: dc * 1024 + (fb + 1) * 512],
                                            start=(dc == 0), stop=(dc == NC - 1))
                                        if dc == NC - 1 and fb == 1:
                                            mm.inc(m_, 1)

                        @block.scalar
                        def _(s):
                            if hh == 1:
                                s.wait_ge(dve.h, h0_dve)
                            for tt in range(NH2):
                                s.wait_ge(mm.h, mm_r + tt + 1)
                                act.inc(s.activation(
                                    rti[:, tt * 1024:(tt + 1) * 1024],
                                    (pp0, pp1)[tt % 2][:, :], AF.Sigmoid), 1)

                        mm_k = mm.n
                        act_k = act.n
                        dve_k = dve.n

                        @block.tensor
                        def _(t):
                            t.wait_ge(ldw.h, ldw0 + 32)
                            for fc in range(NC):
                                ps = (pp0, pp1)[fc % 2]
                                if fc == 0:
                                    t.wait_ge(act.h, act_r + NH2)
                                if fc >= 2:
                                    t.wait_ge(act.h, act_k + fc - 1)
                                for dc in range(NC):
                                    for ib in range(2):
                                        m_ = t.matmul(
                                            ps[:, ib * 512:(ib + 1) * 512],
                                            W[:, 8192 + dc * 1024 + fc * 128: 8192 + dc * 1024 + fc * 128 + 128],
                                            xkT[:, dc * 1024 + ib * 512: dc * 1024 + (ib + 1) * 512],
                                            start=(dc == 0), stop=(dc == NC - 1))
                                        if dc == NC - 1 and ib == 1:
                                            mm.inc(m_, 1)

                        @block.scalar
                        def _(s):
                            for fc in range(NC):
                                s.wait_ge(mm.h, mm_k + fc + 1)
                                if fc >= 2:
                                    s.wait_ge(dve.h, dve_k + fc - 1)
                                act.inc(s.activation(
                                    rsc[:, (fc % 2) * 1024:(fc % 2) * 1024 + 1024],
                                    (pp0, pp1)[fc % 2][:, :], AF.Relu), 1)

                        @block.vector
                        def _(v):
                            for fc in range(NC):
                                v.wait_ge(act.h, act_k + fc + 1)
                                dve.inc(v.tensor_mul(
                                    kaT[:, fc * 1024:(fc + 1) * 1024],
                                    rsc[:, (fc % 2) * 1024:(fc % 2) * 1024 + 1024],
                                    rsc[:, (fc % 2) * 1024:(fc % 2) * 1024 + 1024]))

                        mm_v = mm.n
                        dve_v = dve.n
                        ldxv = ldx.n
                        st_v = st.n

                        @block.sync
                        def _(sy):
                            if hh == 1:
                                sy.wait_ge(dve.h, h0_dve)
                            for tt in range(NH2):
                                if tt >= 2:
                                    sy.wait_ge(dve.h, dve_v + tt - 1)
                                ldx.inc(sy.dma_start(
                                    out=xrr[:, (tt % 2) * 1024:(tt % 2) * 1024 + 1024],
                                    in_=xin[hh * TH + tt * 128: hh * TH + (tt + 1) * 128, :]), 16)
                                if tt >= 1:
                                    pt = tt - 1
                                    sy.wait_ge(dve.h, dve_v + pt + 1)
                                    st.inc(sy.dma_start(
                                        out=xout[hh * TH + pt * 128: hh * TH + (pt + 1) * 128, :],
                                        in_=xor_[:, (pt % 2) * 1024:(pt % 2) * 1024 + 1024]), 16)
                            pt = NH2 - 1
                            sy.wait_ge(dve.h, dve_v + pt + 1)
                            st.inc(sy.dma_start(
                                out=xout[hh * TH + pt * 128: hh * TH + (pt + 1) * 128, :],
                                in_=xor_[:, (pt % 2) * 1024:(pt % 2) * 1024 + 1024]), 16)

                        @block.tensor
                        def _(t):
                            t.wait_ge(ldw.h, ldw0 + 48)
                            for tt in range(NH2):
                                ps = (pp0, pp1)[tt % 2]
                                if tt == 0:
                                    t.wait_ge(act.h, act_k + NC)
                                    t.wait_ge(dve.h, dve_k + NC)
                                if tt >= 2:
                                    t.wait_ge(dve.h, dve_v + tt - 1)
                                for dc in range(NC):
                                    for fb in range(2):
                                        m_ = t.matmul(
                                            ps[:, fb * 512:(fb + 1) * 512],
                                            kaT[:, dc * 1024 + tt * 128: dc * 1024 + (tt + 1) * 128],
                                            W[:, 2 * 8192 + dc * 1024 + fb * 512: 2 * 8192 + dc * 1024 + (fb + 1) * 512],
                                            start=(dc == 0), stop=(dc == NC - 1))
                                        if dc == NC - 1 and fb == 1:
                                            mm.inc(m_, 1)

                        @block.vector
                        def _(v):
                            for tt in range(NH2):
                                v.wait_ge(mm.h, mm_v + tt + 1)
                                v.wait_ge(ldx.h, ldxv + (tt + 1) * 16)
                                if tt >= 2:
                                    v.wait_ge(st.h, st_v + (tt - 1) * 16)
                                v.tensor_mul(t1[:, :],
                                             (pp0, pp1)[tt % 2][:, :],
                                             rti[:, tt * 1024:(tt + 1) * 1024])
                                dve.inc(v.tensor_add(
                                    xor_[:, (tt % 2) * 1024:(tt % 2) * 1024 + 1024],
                                    xrr[:, (tt % 2) * 1024:(tt % 2) * 1024 + 1024],
                                    t1[:, :]), 1)

        def emit_attn_b(xin, xout):
            HPC = MYH // 2

            with (
                nc.sbuf_tensor(_nm("b_W"), [128, 3 * NC * 1024], BF16) as W,
                nc.sbuf_tensor(_nm("b_bias"), [1, 4 * D], BF16) as bbias,
            ):
                f_mm, f_dve, f_act, f_st = mm.n, dve.n, act.n, st.n
                ldw0 = ldw.n

                @block.gpsimd
                def _(g):
                    g.wait_ge(mm.h, f_mm)
                    g.wait_ge(dve.h, f_dve)
                    g.wait_ge(act.h, f_act)
                    g.wait_ge(st.h, f_st)
                    for wi, w in enumerate((bWq, bWk, bWv)):
                        ldw.inc(g.dma_start(
                            out=W[:, wi * NC * 1024:(wi + 1) * NC * 1024]
                                .rearrange("p (c f) -> p c f", f=1024),
                            in_=w.rearrange("(c p) f -> p c f", p=128)), 16)
                    for bi, b_ in enumerate((bqb, bkb, bvb, bob)):
                        ldw.inc(g.dma_start(
                            out=bbias[:, bi * D:(bi + 1) * D], in_=b_[:, :]), 16)

                with (
                    nc.sbuf_tensor(_nm("b_xf"), [128, NT * 1024], F32) as xf,
                    nc.sbuf_tensor(_nm("b_sc"), [128, 1024], F32) as hsc,
                    nc.sbuf_tensor(_nm("b_ev"), [128, 2 * 1024], BF16) as ev2,
                ):
                    @block.sync
                    def _(sy):
                        sy.wait_ge(st.h, f_st)
                    sp0 = emit_ln(xin, xf, hsc, ev2, 2, 1024)

                if bphase < 2:
                    return
                with (
                    nc.sbuf_tensor(_nm("b_hT"), [128, NC * T], BF16) as hT,
                    nc.sbuf_tensor(_nm("b_rt1"), [128, 1024], F32) as rt1,
                    nc.sbuf_tensor(_nm("b_rt2"), [128, 1024], F32) as rt2,
                    nc.sbuf_tensor(_nm("b_rt3"), [128, 1024], F32) as rt3,
                    nc.sbuf_tensor(_nm("b_rt4"), [128, 1024], F32) as rt4,
                    nc.sbuf_tensor(_nm("b_qr"), [128, 2 * 1024], BF16) as qring,
                    nc.sbuf_tensor(_nm("b_kr"), [128, 2 * 1024], BF16) as kring,
                    nc.sbuf_tensor(_nm("b_vr"), [128, 2 * 1024], BF16) as vring,
                    nc.psum_tensor(_nm("b_pq"), [128, 1024], F32) as pq,
                    nc.psum_tensor(_nm("b_pk"), [128, 1024], F32) as pk,
                    nc.psum_tensor(_nm("b_pv"), [128, 1024], F32) as pv,
                ):
                    ldx1 = ldx.n

                    @block.sync
                    def _(sy):
                        sy.wait_ge(sp.h, sp0 + NT * 16)
                        for dc in range(NC):
                            ldx.inc(sy.dma_start_transpose(
                                hT[:, dc * T:(dc + 1) * T],
                                h_b[:, dc * 128:(dc + 1) * 128]), 16)

                    mm_p = mm.n
                    dve_p = dve.n
                    act_p = act.n
                    sp_p = sp.n

                    @block.tensor
                    def _(t):
                        t.wait_ge(ldw.h, ldw0 + 7 * 16)
                        t.wait_ge(ldx.h, ldx1 + NC * 16)
                        for tt in range(NT):
                            if tt >= 1:
                                t.wait_ge(dve.h, dve_p + tt * 2)
                                t.wait_ge(act.h, act_p + tt)
                            for wi, ps in ((0, pq), (1, pk), (2, pv)):
                                for dc in range(NC):
                                    for fb in range(2):
                                        t.matmul(
                                            ps[:, fb * 512:(fb + 1) * 512],
                                            hT[:, dc * T + tt * 128: dc * T + (tt + 1) * 128],
                                            W[:, wi * 8192 + dc * 1024 + fb * 512:
                                              wi * 8192 + dc * 1024 + (fb + 1) * 512],
                                            start=(dc == 0), stop=False)
                                for fb in range(2):
                                    m_ = t.matmul(
                                        ps[:, fb * 512:(fb + 1) * 512],
                                        onesc[0:1, 0:128],
                                        bbias[0:1, wi * D + fb * 512: wi * D + (fb + 1) * 512],
                                        start=False, stop=True)
                                    if fb == 1:
                                        mm.inc(m_, 1)

                    def rope_pair(v, tt):
                        for src, rt in ((pq, rt1), (pk, rt2)):
                            v.tensor_mul(
                                AP(rt, 0, [[1024, 128], [64, 16], [2, 32]]),
                                AP(src, 1, [[1024, 128], [64, 16], [2, 32]]),
                                AP(ropeSs, tt * 64, [[NT * 64, 128], [0, 16], [2, 32]]))
                            v.tensor_mul(
                                AP(rt, 1, [[1024, 128], [64, 16], [2, 32]]),
                                AP(src, 0, [[1024, 128], [64, 16], [2, 32]]),
                                AP(ropeSs, tt * 64 + 1, [[NT * 64, 128], [0, 16], [2, 32]]))
                        for src, rt, rq in ((pq, rt1, rt3), (pk, rt2, rt4)):
                            v.tensor_mul(
                                AP(rq, 0, [[1024, 128], [64, 16], [1, 64]]),
                                AP(src, 0, [[1024, 128], [64, 16], [1, 64]]),
                                AP(ropeCs, tt * 64, [[NT * 64, 128], [0, 16], [1, 64]]))
                        for src, rt, rq, ring in ((pq, rt1, rt3, qring), (pk, rt2, rt4, kring)):
                            dve.inc(v.tensor_add(
                                ring[:, (tt % 2) * 1024:(tt % 2) * 1024 + 1024],
                                rq[:, :], rt[:, :]))

                    def rope(v, src, ring, tt):
                        v.tensor_mul(
                            AP(rt1, 0, [[1024, 128], [64, 16], [2, 32]]),
                            AP(src, 1, [[1024, 128], [64, 16], [2, 32]]),
                            AP(ropeSs, tt * 64, [[NT * 64, 128], [0, 16], [2, 32]]))
                        v.tensor_mul(
                            AP(rt1, 1, [[1024, 128], [64, 16], [2, 32]]),
                            AP(src, 0, [[1024, 128], [64, 16], [2, 32]]),
                            AP(ropeSs, tt * 64 + 1, [[NT * 64, 128], [0, 16], [2, 32]]))
                        v.tensor_mul(
                            AP(rt2, 0, [[1024, 128], [64, 16], [1, 64]]),
                            AP(src, 0, [[1024, 128], [64, 16], [1, 64]]),
                            AP(ropeCs, tt * 64, [[NT * 64, 128], [0, 16], [1, 64]]))
                        dve.inc(v.tensor_add(
                            ring[:, (tt % 2) * 1024:(tt % 2) * 1024 + 1024],
                            rt2[:, :], rt1[:, :]))

                    @block.vector
                    def _(v):
                        for tt in range(NT):
                            v.wait_ge(mm.h, mm_p + tt * 3 + 2)
                            if tt >= 2:
                                v.wait_ge(sp.h, sp_p + (tt - 2) * 48 + 32)
                            rope_pair(v, tt)

                    @block.scalar
                    def _(s):
                        for tt in range(NT):
                            s.wait_ge(mm.h, mm_p + tt * 3 + 3)
                            if tt >= 2:
                                s.wait_ge(sp.h, sp_p + (tt - 2) * 48 + 48)
                            act.inc(s.activation(
                                vring[:, (tt % 2) * 1024:(tt % 2) * 1024 + 1024],
                                pv[:, :], AF.Identity), 1)

                    @block.sync
                    def _(sy):
                        for tt in range(NT):
                            sy.wait_ge(dve.h, dve_p + tt * 2 + 1)
                            sp.inc(sy.dma_start(
                                out=qkvb[tt * 128:(tt + 1) * 128, :],
                                in_=qring[:, (tt % 2) * 1024:(tt % 2) * 1024 + 1024]), 16)
                            sy.wait_ge(dve.h, dve_p + tt * 2 + 2)
                            sp.inc(sy.dma_start(
                                out=qkvb[T + tt * 128: T + (tt + 1) * 128, :],
                                in_=kring[:, (tt % 2) * 1024:(tt % 2) * 1024 + 1024]), 16)
                            sy.wait_ge(act.h, act_p + tt + 1)
                            sp.inc(sy.dma_start(
                                out=qkvb[2 * T + tt * 128: 2 * T + (tt + 1) * 128, :],
                                in_=vring[:, (tt % 2) * 1024:(tt % 2) * 1024 + 1024]), 16)

                if bphase < 3:
                    return
                sp_x = sp.n
                roff = {}

                @block.sync
                def _(sy):
                    sy.wait_ge(sp.h, sp_x)
                    if dbg:
                        st.inc(sy.dma_start(out=dhb[:, :], in_=h_b[:, :]), 16)
                        st.inc(sy.dma_start(out=dqb[:, :], in_=qkvb[:, :]), 16)
                    pair = pid_pair(sy)
                    roff["own"] = sy.snap(pair * (T * 512), donate=True)
                    roff["oth"] = sy.snap((1 - pair) * (T * 512), donate=True)
                    roff["oth3"] = sy.snap((1 - pair) * (3 * T * 512), donate=True)
                    sp.inc(sy.dma_start(
                        out=AP(bqkv.tensor, 0, [[512, 3 * T], [1, 512]]),
                        in_=AP(qkvb.tensor, 512, [[1024, 3 * T], [1, 512]])), 16)
                    for xi in range(3):
                        sp.inc(sy.dma_start(
                            out=AP(qkvg.tensor, roff["own"] + xi * (S * 512),
                                   [[512, T], [1, 512]]),
                            in_=AP(qkvb.tensor, xi * T * 1024,
                                   [[1024, T], [1, 512]])), 16)

                cc_x = cc.n

                @block.gpsimd
                def _(g):
                    g.wait_ge(sp.h, sp_x + 4 * 16)
                    cc.inc(g.collective_compute(
                        "AllGather", ALU.bypass, replica_groups=rg,
                        ins=[AP(bqkv.tensor, 0, [[512, 3 * T], [1, 512]]).opt()],
                        outs=[AP(ag_bqkv.tensor, 0, [[512, 6 * T], [1, 512]]).opt()]), 1)

                sp_y = sp.n

                @block.sync
                def _(sy):
                    sy.wait_ge(cc.h, cc_x + 1)
                    for xi in range(3):
                        sp.inc(sy.dma_start(
                            out=AP(qkvg.tensor, roff["oth"] + xi * (S * 512),
                                   [[512, T], [1, 512]]),
                            in_=AP(ag_bqkv.tensor, roff["oth3"] + xi * (T * 512),
                                   [[512, T], [1, 512]])), 16)

                if bphase < 4:
                    return
                with (
                    nc.sbuf_tensor(_nm("b_qT"), [128, HPC * S], BF16) as qT,
                    nc.sbuf_tensor(_nm("b_kT"), [128, HPC * S], BF16) as kT,
                    nc.sbuf_tensor(_nm("b_va"), [128, NJ * 520], BF16) as Vau,
                    nc.sbuf_tensor(_nm("b_ea"), [128, 3 * 1024], BF16) as eA,
                    nc.sbuf_tensor(_nm("b_yt"), [128, 2 * 2048], BF16) as ytk,
                    nc.sbuf_tensor(_nm("b_rz"), [128, 8], F32) as rz,
                    nc.sbuf_tensor(_nm("b_dz"), [128, 2 * 32], F32) as dzsb,
                    nc.psum_tensor(_nm("b_ps0"), [128, 1024], F32) as bs0,
                    nc.psum_tensor(_nm("b_ps1"), [128, 1024], F32) as bs1,
                    nc.psum_tensor(_nm("b_av0"), [128, 1024], F32) as av0,
                    nc.psum_tensor(_nm("b_av1"), [128, 1024], F32) as av1,
                ):
                    ldx2 = ldx.n
                    prj_dve, prj_act, prj_mm = dve.n, act.n, mm.n

                    @block.sync
                    def _(sy):
                        sy.wait_ge(dve.h, prj_dve)
                        sy.wait_ge(act.h, prj_act)
                        sy.wait_ge(sp.h, sp_y + 3 * 16)
                        if dbg:
                            st.inc(sy.dma_start(out=dqkv[:, :],
                                                in_=AP(qkvg.tensor, 0, [[512, 3 * S], [1, 512]])), 16)
                        for hp in range(HPC):
                            ldx.inc(sy.dma_start_transpose(
                                qT[:, hp * S:(hp + 1) * S],
                                AP(qkvg.tensor, hp * 128, [[512, S], [1, 128]])), 16)
                            ldx.inc(sy.dma_start_transpose(
                                kT[:, hp * S:(hp + 1) * S],
                                AP(qkvg.tensor, S * 512 + hp * 128, [[512, S], [1, 128]])), 16)
                        for jt in range(NJ):
                            ldx.inc(sy.dma_start(
                                out=Vau[:, jt * 520:(jt + 1) * 520]
                                    .rearrange("p (h c) -> p h c", c=65)[:, :, 0:64],
                                in_=AP(qkvg.tensor, 2 * S * 512 + jt * 128 * 512,
                                       [[512, 128], [64, 8], [1, 64]])), 16)

                    dve_o = dve.n

                    @block.vector
                    def _(v):
                        v.wait_ge(mm.h, prj_mm)
                        dve.inc(v.memset(
                            Vau[:, :].rearrange("p (j h c) -> p j h c", h=8, c=65)
                               [:, :, :, 64], 1.0))

                    sched = []
                    for ib in range(NIB):
                        for hp in range(HPC):
                            jt_hi = 4 * (ib + 1)
                            for jt in range(jt_hi):
                                sched.append((ib, hp, jt, jt == jt_hi - 1))
                    NU = len(sched)
                    NG = NIB * HPC
                    grp_of = []
                    g = -1
                    for u, (ib, hp, jt, lastj) in enumerate(sched):
                        if jt == 0:
                            g += 1
                        grp_of.append(g)
                    diag = [jt >= 4 * ib for (ib, hp, jt, _l) in sched]

                    mm_s = mm.n
                    act_s = act.n
                    dve_s = dve.n
                    st_s = st.n

                    dve_after_mask = [0] * NU
                    dve_after_evac = [0] * NG
                    c = dve_s + 0
                    for u in range(NU):
                        if diag[u]:
                            c += 1
                        dve_after_mask[u] = c
                        if sched[u][3]:
                            c += 1
                            dve_after_evac[grp_of[u]] = c

                    sc_mm = [0] * NU
                    av_mm = [0] * NU

                    @block.tensor
                    def _(t):
                        t.wait_ge(dve.h, dve_o + 1)
                        t.wait_ge(act.h, prj_act)
                        t.wait_ge(ldx.h, ldx2 + (8 + NJ) * 16)

                        def emit_scores(u):
                            ib, hp, jt, _l = sched[u]
                            ps = (bs0, bs1)[u % 2]
                            if u >= 2:
                                t.wait_ge(act.h, act_s + u - 1)
                            for h2 in range(2):
                                m_ = t.matmul(
                                    ps[:, h2 * 512:(h2 + 1) * 512],
                                    kT[64 * h2:64 * h2 + 64,
                                       hp * S + jt * 128: hp * S + (jt + 1) * 128],
                                    qT[64 * h2:64 * h2 + 64,
                                       hp * S + ib * 512: hp * S + (ib + 1) * 512],
                                    start=True, stop=True,
                                    tile_position=(64 * h2, 0))
                            mm.inc(m_, 1)
                            sc_mm[u] = mm.n

                        def emit_av(u):
                            ib, hp, jt, lastj = sched[u]
                            g_ = grp_of[u]
                            av = (av0, av1)[g_ % 2]
                            t.wait_ge(act.h, act_s + u + 1)
                            if diag[u]:
                                t.wait_ge(dve.h, dve_after_mask[u])
                            if jt == 0 and g_ >= 2:
                                t.wait_ge(dve.h, dve_after_evac[g_ - 2])
                            sl = (u % 3) * 1024
                            for h2 in range(2):
                                for q in range(4):
                                    m_ = t.matmul(
                                        av[:, (h2 * 4 + q) * 128:(h2 * 4 + q) * 128 + 65],
                                        eA[:, sl + h2 * 512 + q * 128: sl + h2 * 512 + (q + 1) * 128],
                                        Vau[:, jt * 520 + (hp * 2 + h2) * 65:
                                            jt * 520 + (hp * 2 + h2) * 65 + 65],
                                        start=(jt == 0 and q == 0),
                                        stop=(lastj and q == 3))
                            mm.inc(m_, 1)
                            av_mm[u] = mm.n

                        for u in range(NU):
                            emit_scores(u)
                            if u >= 1:
                                emit_av(u - 1)
                        emit_av(NU - 1)

                    @block.scalar
                    def _(s):
                        for u in range(NU):
                            s.wait_ge(mm.h, sc_mm[u])
                            if u >= 3:
                                s.wait_ge(mm.h, av_mm[u - 3])
                            act.inc(s.activation(
                                eA[:, (u % 3) * 1024:(u % 3) * 1024 + 1024],
                                (bs0, bs1)[u % 2][:, :], AF.Exp, scale=BSC), 1)

                    @block.vector
                    def _(v):
                        for u in range(NU):
                            ib, hp, jt, lastj = sched[u]
                            if diag[u]:
                                v.wait_ge(act.h, act_s + u + 1)
                                k_ = jt - 4 * ib
                                sl = (u % 3) * 1024
                                v.tensor_mul(eA[:, sl:sl + 512], eA[:, sl:sl + 512],
                                             masks[:, k_ * 512:(k_ + 1) * 512])
                                dve.inc(v.tensor_mul(
                                    eA[:, sl + 512:sl + 1024], eA[:, sl + 512:sl + 1024],
                                    masks[:, k_ * 512:(k_ + 1) * 512]))
                            if lastj:
                                g_ = grp_of[u]
                                av = (av0, av1)[g_ % 2]
                                v.wait_ge(mm.h, av_mm[u])
                                if ib >= 2 and hp == 0:
                                    v.wait_ge(st.h, st_s + (ib - 1) * 4 * 16)
                                for h2 in range(2):
                                    for q in range(4):
                                        base = (h2 * 4 + q) * 128
                                        if dbg:
                                            v.tensor_copy(
                                                dzsb[:, (ib % 2) * 32 + q * 8 + hp * 2 + h2:
                                                     (ib % 2) * 32 + q * 8 + hp * 2 + h2 + 1],
                                                av[:, base + 64:base + 65])
                                        v.reciprocal(rz[:, h2 * 4 + q:h2 * 4 + q + 1],
                                                     av[:, base + 64:base + 65])
                                for h2 in range(2):
                                    for q in range(4):
                                        base = (h2 * 4 + q) * 128
                                        m2 = v.tensor_scalar_mul(
                                            ytk[:, (ib % 2) * 2048 + q * 512 +
                                                (hp * 2 + h2) * 64:
                                                (ib % 2) * 2048 + q * 512 +
                                                (hp * 2 + h2) * 64 + 64],
                                            av[:, base:base + 64],
                                            rz[:, h2 * 4 + q:h2 * 4 + q + 1])
                                dve.inc(m2)

                    @block.sync
                    def _(sy):
                        for ib in range(NIB):
                            gend = dve_after_evac[ib * HPC + HPC - 1]
                            sy.wait_ge(dve.h, gend)
                            for q in range(4):
                                st.inc(sy.dma_start(
                                    out=yb_tok[(ib * 4 + q) * 128:(ib * 4 + q + 1) * 128, :],
                                    in_=ytk[:, (ib % 2) * 2048 + q * 512:
                                            (ib % 2) * 2048 + (q + 1) * 512]), 16)
                                if dbg:
                                    st.inc(sy.dma_start(
                                        out=dzt[(ib * 4 + q) * 128:(ib * 4 + q + 1) * 128, :],
                                        in_=dzsb[:, (ib % 2) * 32 + q * 8:
                                                 (ib % 2) * 32 + (q + 1) * 8]), 16)

                if bphase < 5:
                    return
                sp_ye = sp.n
                st_att = st.n

                @block.sync
                def _(sy):
                    sy.wait_ge(st.h, st_att)
                    if dbg:
                        st.inc(sy.dma_start(out=dyt[:, :],
                                            in_=AP(yb_tok.tensor, 0, [[512, S], [1, 512]])), 16)
                    sp.inc(sy.dma_start(
                        out=AP(ybx.tensor, 0, [[512, T], [1, 512]]),
                        in_=AP(yb_tok.tensor, roff["oth"], [[512, T], [1, 512]])), 16)
                    sp.inc(sy.dma_start(
                        out=AP(yg2.tensor, 0, [[1024, T], [1, 512]]),
                        in_=AP(yb_tok.tensor, roff["own"], [[512, T], [1, 512]])), 16)

                cc_y = cc.n

                @block.gpsimd
                def _(g):
                    g.wait_ge(sp.h, sp_ye + 16)
                    cc.inc(g.collective_compute(
                        "AllGather", ALU.bypass, replica_groups=rg,
                        ins=[AP(ybx.tensor, 0, [[512, T], [1, 512]]).opt()],
                        outs=[AP(ag_y2.tensor, 0, [[512, 2 * T], [1, 512]]).opt()]), 1)

                sp_ya = sp.n

                @block.sync
                def _(sy):
                    sy.wait_ge(cc.h, cc_y + 1)
                    sp.inc(sy.dma_start(
                        out=AP(yg2.tensor, 512, [[1024, T], [1, 512]]),
                        in_=AP(ag_y2.tensor, roff["oth"], [[512, T], [1, 512]])), 16)

                with (
                    nc.sbuf_tensor(_nm("b_yTo"), [128, NC * T], BF16) as yTo,
                    nc.sbuf_tensor(_nm("b_Wo"), [128, NC * 1024], BF16) as Wo,
                    nc.sbuf_tensor(_nm("b_xr"), [128, 2 * 1024], F32) as xrr,
                    nc.sbuf_tensor(_nm("b_xo"), [128, 2 * 1024], F32) as xor_,
                    nc.psum_tensor(_nm("b_po0"), [128, 1024], F32) as po0,
                    nc.psum_tensor(_nm("b_po1"), [128, 1024], F32) as po1,
                ):
                    att_mm, att_dve = mm.n, dve.n
                    ldw1 = ldw.n

                    @block.gpsimd
                    def _(g):
                        g.wait_ge(mm.h, att_mm)
                        ldw.inc(g.dma_start(
                            out=Wo[:, :].rearrange("p (c f) -> p c f", f=1024),
                            in_=bWo.rearrange("(c p) f -> p c f", p=128)), 16)

                    ldx3 = ldx.n

                    @block.sync
                    def _(sy):
                        sy.wait_ge(mm.h, att_mm)
                        sy.wait_ge(dve.h, att_dve)
                        sy.wait_ge(sp.h, sp_ya + 16)
                        for dc in range(NC):
                            ldx.inc(sy.dma_start_transpose(
                                yTo[:, dc * T:(dc + 1) * T],
                                AP(yg2.tensor, dc * 128, [[1024, T], [1, 128]])), 16)

                    mm_o = mm.n
                    dve_v = dve.n
                    ldxv = ldx.n
                    st_v = st.n

                    @block.sync
                    def _(sy):
                        for tt in range(NT):
                            if tt >= 2:
                                sy.wait_ge(dve.h, dve_v + tt - 1)
                            ldx.inc(sy.dma_start(
                                out=xrr[:, (tt % 2) * 1024:(tt % 2) * 1024 + 1024],
                                in_=xin[tt * 128:(tt + 1) * 128, :]), 16)
                            if tt >= 1:
                                pt = tt - 1
                                sy.wait_ge(dve.h, dve_v + pt + 1)
                                st.inc(sy.dma_start(
                                    out=xout[pt * 128:(pt + 1) * 128, :],
                                    in_=xor_[:, (pt % 2) * 1024:(pt % 2) * 1024 + 1024]), 16)
                        pt = NT - 1
                        sy.wait_ge(dve.h, dve_v + pt + 1)
                        st.inc(sy.dma_start(
                            out=xout[pt * 128:(pt + 1) * 128, :],
                            in_=xor_[:, (pt % 2) * 1024:(pt % 2) * 1024 + 1024]), 16)

                    @block.tensor
                    def _(t):
                        t.wait_ge(ldw.h, ldw1 + 16)
                        t.wait_ge(ldx.h, ldx3 + NC * 16)
                        for tt in range(NT):
                            ps = (po0, po1)[tt % 2]
                            if tt >= 2:
                                t.wait_ge(dve.h, dve_v + tt - 1)
                            for dc in range(NC):
                                for fb in range(2):
                                    t.matmul(
                                        ps[:, fb * 512:(fb + 1) * 512],
                                        yTo[:, dc * T + tt * 128: dc * T + (tt + 1) * 128],
                                        Wo[:, dc * 1024 + fb * 512: dc * 1024 + (fb + 1) * 512],
                                        start=(dc == 0), stop=False)
                            for fb in range(2):
                                m_ = t.matmul(
                                    ps[:, fb * 512:(fb + 1) * 512],
                                    onesc[0:1, 0:128],
                                    bbias[0:1, 3 * D + fb * 512: 3 * D + (fb + 1) * 512],
                                    start=False, stop=True)
                                if fb == 1:
                                    mm.inc(m_, 1)

                    @block.vector
                    def _(v):
                        for tt in range(NT):
                            v.wait_ge(mm.h, mm_o + tt + 1)
                            v.wait_ge(ldx.h, ldxv + (tt + 1) * 16)
                            if tt >= 2:
                                v.wait_ge(st.h, st_v + (tt - 1) * 16)
                            dve.inc(v.tensor_add(
                                xor_[:, (tt % 2) * 1024:(tt % 2) * 1024 + 1024],
                                xrr[:, (tt % 2) * 1024:(tt % 2) * 1024 + 1024],
                                (po0, po1)[tt % 2][:, :]))

        cur = x0
        last = max(sublayers)
        if 0 in sublayers:
            nxt = yout if last == 0 else xb0
            emit_attn_a(cur, nxt)
            cur = nxt
        if 1 in sublayers:
            nxt = yout if last == 1 else xb1
            emit_ffn(cur, nxt)
            cur = nxt
        if 2 in sublayers:
            nxt = yout if last == 2 else xb0
            emit_attn_b(cur, nxt)
            cur = nxt
        if 3 in sublayers:
            emit_ffn(cur, yout)

        @block.sync
        def _(sy):
            sy.wait_ge(st.h, st.n)
            sy.wait_ge(sp.h, sp.n)
            sy.wait_ge(ldx.h, ldx.n)
            sy.wait_ge(ldw.h, ldw.n)

    return nc



def make_inputs(full):
    x = np.asarray(full["L0_attn_a_x"], np.float32)
    W = np.asarray(full["L0_attn_a_W"], np.float32)
    b = np.asarray(full["L0_attn_a_b"], np.float32)
    Wr = np.asarray(full["L0_ffn_W_r"], np.float32)
    Wk = np.asarray(full["L0_ffn_W_k"], np.float32)
    Wv = np.asarray(full["L0_ffn_W_v"], np.float32)
    mur = np.asarray(full["L0_ffn_mu_r"], np.float32)
    muk = np.asarray(full["L0_ffn_mu_k"], np.float32)
    Wq = np.asarray(full["L0_attn_b_Wq"], np.float32)
    Wkb = np.asarray(full["L0_attn_b_Wk"], np.float32)
    Wvb = np.asarray(full["L0_attn_b_Wv"], np.float32)
    Wo = np.asarray(full["L0_attn_b_Wo"], np.float32)
    bq = np.asarray(full["L0_attn_b_bq"], np.float32)
    bk = np.asarray(full["L0_attn_b_bk"], np.float32)
    bv = np.asarray(full["L0_attn_b_bv"], np.float32)
    bo = np.asarray(full["L0_attn_b_bo"], np.float32)

    aWqT = np.ascontiguousarray(W[0:D].T)
    aWkT = np.ascontiguousarray(W[D:2 * D].T)
    aWvT = np.ascontiguousarray(W[2 * D:3 * D].T)
    ab = np.ascontiguousarray(b.reshape(1, 3 * D))
    fWrT = np.ascontiguousarray(Wr.T)
    fWkT = np.ascontiguousarray(Wk.T)
    fWvT = np.ascontiguousarray(Wv.T)
    fmur = np.ascontiguousarray(mur.reshape(NC, 128).T)
    fmuk = np.ascontiguousarray(muk.reshape(NC, 128).T)

    pos = np.arange(S, dtype=np.float32)
    freq = np.arange(0, DH, 2, dtype=np.float32) / DH
    inv_freq = (1.0 / 10000.0 ** freq).astype(np.float32)
    ang = np.outer(pos, inv_freq)
    cosT = np.repeat(np.cos(ang).astype(np.float32), 2, axis=1)
    sinT = np.sin(ang).astype(np.float32)
    sgn = np.empty((S, DH), np.float32)
    sgn[:, 0::2] = -sinT
    sgn[:, 1::2] = sinT

    m = np.zeros((128, 4 * 512), np.float32)
    for k in range(4):
        jj = 128 * k + np.arange(128)[:, None]
        ii = np.arange(512)[None, :]
        m[:, k * 512:(k + 1) * 512] = (jj <= ii).astype(np.float32)

    ins = []
    for c in range(8):
        bsel, hh = c // 2, c % 2
        t0 = hh * T
        myh = np.arange(hh * MYH, hh * MYH + MYH)
        oth = np.arange((1 - hh) * MYH, (1 - hh) * MYH + MYH)
        perm = np.concatenate([myh, oth])
        colp = (perm[:, None] * DH + np.arange(DH)[None, :]).reshape(-1)
        d = {
            "x0": np.ascontiguousarray(x[bsel, t0:t0 + T]),
            "aWqT": aWqT, "aWkT": aWkT, "aWvT": aWvT, "ab": ab,
            "fWrT": fWrT, "fWkT": fWkT, "fWvT": fWvT,
            "fmur": fmur, "fmuk": fmuk,
            "bWq": np.ascontiguousarray(Wq[:, colp]),
            "bWk": np.ascontiguousarray(Wkb[:, colp]),
            "bWv": np.ascontiguousarray(Wvb[:, colp]),
            "bWo": np.ascontiguousarray(Wo[colp, :]),
            "bqb": np.ascontiguousarray(bq[colp].reshape(1, D)),
            "bkb": np.ascontiguousarray(bk[colp].reshape(1, D)),
            "bvb": np.ascontiguousarray(bv[colp].reshape(1, D)),
            "bob": np.ascontiguousarray(bo.reshape(1, D)),
            "ropeC": np.ascontiguousarray(cosT[t0:t0 + T]),
            "ropeS": np.ascontiguousarray(sgn[t0:t0 + T]),
            "cmask": m,
            "msel": np.full((128, 1), float(hh), np.float32),
        }
        ins.append(d)
    return ins


def kernel(**inputs):
    nc = build_program()
    in_maps = make_inputs(inputs)
    res = run_bass_kernel_spmd(nc, in_maps, core_ids=list(range(8)))
    out = np.empty((B, S, D), np.float32)
    for c in range(8):
        bsel, hh = c // 2, c % 2
        out[bsel, hh * T:(hh + 1) * T] = res.results[c]["y"]
    return out


# version 06
# speedup vs baseline: 1.0474x; 1.0474x over previous
import numpy as np
import ml_dtypes
import concourse.bass as bass
import concourse.mybir as mybir
from concourse.ap import AP
from concourse.bass_utils import run_bass_kernel_spmd

F32 = mybir.dt.float32
BF16 = mybir.dt.bfloat16
AF = mybir.ActivationFunctionType
ALU = mybir.AluOpType
AXX = mybir.AxisListType.X
BF = ml_dtypes.bfloat16

B, S, D = 4, 4096, 1024
T = S // 2
NT = T // 128
NC = D // 128
NJ = S // 128
NIB = S // 512
DH = 64
MYH = 8
EPS = 1e-5
QSC = 1.0 / 32.0
ASC = 2.0 ** -10
BSC = 0.125


class Sem:
    def __init__(self, handle):
        self.h = handle
        self.n = 0

    def inc(self, instr, k=1):
        self.n += k
        instr.then_inc(self.h, k)
        return instr


def build_program(sublayers=(0, 1, 2, 3), aphase=9, bphase=9, dbg=False, lnchase=True):
    nc = bass.Bass("TRN2", target_bir_lowering=False, debug=False,
                   detect_race_conditions=False)
    rg = [[0, 1], [2, 3], [4, 5], [6, 7]]
    _uid = [0]

    _pidc = {}

    def pid_pair(eng):
        k = str(eng.engine)
        if k not in _pidc:
            _pidc[k] = eng.partition_id() % 2
        return _pidc[k]

    def _nm(n):
        _uid[0] += 1
        return f"{n}_{_uid[0]}"

    ein = lambda n, s: nc.dram_tensor(n, s, F32, kind="ExternalInput").ap()
    x0 = ein("x0", [T, D])
    aWqT, aWkT, aWvT = ein("aWqT", [D, D]), ein("aWkT", [D, D]), ein("aWvT", [D, D])
    ab = ein("ab", [1, 3 * D])
    fWrT, fWkT, fWvT = ein("fWrT", [D, D]), ein("fWkT", [D, D]), ein("fWvT", [D, D])
    fmur, fmuk = ein("fmur", [128, NC]), ein("fmuk", [128, NC])
    bWq, bWk, bWv, bWo = ein("bWq", [D, D]), ein("bWk", [D, D]), ein("bWv", [D, D]), ein("bWo", [D, D])
    bqb, bkb, bvb, bob = ein("bqb", [1, D]), ein("bkb", [1, D]), ein("bvb", [1, D]), ein("bob", [1, D])
    ropeC, ropeS = ein("ropeC", [T, 64]), ein("ropeS", [T, 64])
    cmask = ein("cmask", [128, 4 * 512])
    msel = ein("msel", [128, 1])
    yout = nc.dram_tensor("y", [T, D], F32, kind="ExternalOutput").ap()
    if dbg:
        dqkv = nc.dram_tensor("dqkv", [3 * S, 512], BF16, kind="ExternalOutput").ap()
        dyt = nc.dram_tensor("dyt", [S, 512], BF16, kind="ExternalOutput").ap()
        dhb = nc.dram_tensor("dhb", [T, D], BF16, kind="ExternalOutput").ap()
        dqb = nc.dram_tensor("dqb", [3 * T, D], BF16, kind="ExternalOutput").ap()
        dzt = nc.dram_tensor("dzt", [S, 8], F32, kind="ExternalOutput").ap()

    dram = lambda n, s, dt=BF16: nc.dram_tensor(n, s, dt, kind="Internal").ap()
    xb0 = dram("xb0", [T, D], F32)
    xb1 = dram("xb1", [T, D], F32)
    h_b = dram("h_b", [T, D])
    akv = dram("akv", [2 * D, T])
    ag_k = dram("ag_k", [2 * D, T])
    ag_v2 = dram("ag_v2", [2 * T, D])
    azb = dram("azb", [128, NJ], F32)
    ag_az = dram("ag_az", [256, NJ], F32)
    a_spill = dram("a_spill", [NJ * 128, T])
    bnd_in = dram("bnd_in", [1, D])
    ag_bnd = dram("ag_bnd", [2, D])
    qkvo = dram("qkvo", [3 * T, 512])
    bqkv = dram("bqkv", [3 * T, 512])
    ag_bqkv = dram("ag_bqkv", [6 * T, 512])
    qkvg = dram("qkvg", [3 * S, 512])
    yb_tok = dram("yb_tok", [S, 512])
    ybx = dram("ybx", [T, 512])
    ag_y2 = dram("ag_y2", [2 * T, 512])
    yg2 = dram("yg2", [T, D])

    from contextlib import ExitStack
    with ExitStack() as _es:
        ones = _es.enter_context(nc.sbuf_tensor("ones", [1, 512], BF16))
        onesc = _es.enter_context(nc.sbuf_tensor("onesc", [1, 128], BF16))
        mur = _es.enter_context(nc.sbuf_tensor("mur", [128, NC], F32))
        muk = _es.enter_context(nc.sbuf_tensor("muk", [128, NC], F32))
        masks = _es.enter_context(nc.sbuf_tensor("masks", [128, 4 * 512], BF16))
        ropeCs = _es.enter_context(nc.sbuf_tensor("ropeCs", [128, NT * 64], F32))
        ropeSs = _es.enter_context(nc.sbuf_tensor("ropeSs", [128, NT * 64], F32))
        stats = _es.enter_context(nc.sbuf_tensor("stats", [128, 96], F32))
        ident = _es.enter_context(nc.sbuf_tensor("ident", [128, 128], BF16))
        msl = _es.enter_context(nc.sbuf_tensor("msl", [128, 1], F32))
        zst = _es.enter_context(nc.sbuf_tensor("zst", [128, 3 * NJ], F32))
        _ldw = _es.enter_context(nc.semaphore("s_ldw"))
        _ldx = _es.enter_context(nc.semaphore("s_ldx"))
        _mm = _es.enter_context(nc.semaphore("s_mm"))
        _act = _es.enter_context(nc.semaphore("s_act"))
        _dve = _es.enter_context(nc.semaphore("s_dve"))
        _gp = _es.enter_context(nc.semaphore("s_gp"))
        _cc = _es.enter_context(nc.semaphore("s_cc"))
        _sp = _es.enter_context(nc.semaphore("s_sp"))
        _st = _es.enter_context(nc.semaphore("s_st"))
        _id = _es.enter_context(nc.semaphore("s_id"))
        block = _es.enter_context(nc.Block())

        ldw, ldx, mm, act, dve = Sem(_ldw), Sem(_ldx), Sem(_mm), Sem(_act), Sem(_dve)
        gp, cc, sp, st, idm = Sem(_gp), Sem(_cc), Sem(_sp), Sem(_st), Sem(_id)

        @block.vector
        def _(v):
            v.memset(ones[:, :], 1.0)
            v.memset(onesc[:, :], 1.0)

        @block.sync
        def _(sy):
            sp.inc(sy.dma_start(out=msl[:, :], in_=msel[:, :]), 16)
            sp.inc(sy.dma_start(out=mur[:, :], in_=fmur[:, :]), 16)
            sp.inc(sy.dma_start(out=muk[:, :], in_=fmuk[:, :]), 16)
            sp.inc(sy.dma_start(
                out=ropeCs[:, :].rearrange("p (n c) -> p n c", c=64),
                in_=ropeC.rearrange("(n p) c -> p n c", p=128)), 16)
            sp.inc(sy.dma_start(
                out=ropeSs[:, :].rearrange("p (n c) -> p n c", c=64),
                in_=ropeS.rearrange("(n p) c -> p n c", p=128)), 16)

        @block.gpsimd
        def _(g):
            gp.inc(g.dma_start(out=masks[:, :], in_=cmask[:, :]), 16)
            g.memset(ident[:, :], 1.0)
            idm.inc(g.affine_select(
                ident[:, :], ident[:, :],
                pattern=[[-1, 128]], compare_op=ALU.is_equal,
                fill=0.0, base=0, channel_multiplier=1), 1)


        def emit_ln(xin, xf, hsc, ev, nslots, sw, st_base=None, fence=()):
            ldx0 = ldx.n

            st_all = st.n

            @block.sync
            def _(sy):
                for s_, c_ in fence:
                    sy.wait_ge(s_.h, c_)
                if st_base is None:
                    sy.wait_ge(st.h, st_all)
                else:
                    sy.wait_ge(st.h, st_base + NT * 16)
                ldx.inc(sy.dma_start(
                    out=xf[:, :].rearrange("p (n d) -> p n d", d=1024),
                    in_=xin.rearrange("(n p) d -> p n d", p=128)), 16)

            dve0 = dve.n

            @block.vector
            def _(v):
                v.wait_ge(ldx.h, ldx0 + 16)
                for tt in range(NT):
                    xt = xf[:, tt * 1024:(tt + 1) * 1024]
                    v.reduce_sum(stats[:, tt:tt + 1], xt, axis=AXX)
                v.tensor_copy(stats[:, 90:91], stats[:, 0:1])
                dve.inc(v.tensor_scalar(
                    stats[:, 16:32], stats[:, 0:16], -1.0 / D, None, ALU.mult))

            act0 = act.n

            @block.scalar
            def _(s):
                s.wait_ge(dve.h, dve0 + 1)
                for tt in range(NT):
                    xt = xf[:, tt * 1024:(tt + 1) * 1024]
                    s.activation(
                        hsc[:, :], xt, AF.Square,
                        bias=stats[:, 16 + tt:17 + tt],
                        accum_out=stats[:, 32 + tt:33 + tt])
                act.inc(s.activation(hsc[:, 0:1], hsc[:, 0:1], AF.Square), 1)

            dve1 = dve.n

            @block.vector
            def _(v):
                v.wait_ge(act.h, act0 + 1)
                dve.inc(v.tensor_scalar(
                    stats[:, 48:64], stats[:, 32:48], 1.0 / D, EPS,
                    ALU.mult, ALU.add))

            act1 = act.n

            @block.scalar
            def _(s):
                s.wait_ge(dve.h, dve1 + 1)
                act.inc(s.activation(stats[:, 80:96], stats[:, 48:64], AF.Sqrt), 1)

            dve2 = dve.n

            @block.vector
            def _(v):
                v.wait_ge(act.h, act1 + 1)
                v.reciprocal(stats[:, 48:64], stats[:, 80:96])
                v.tensor_copy(stats[:, 90:91], stats[:, 48:49])
                v.tensor_mul(stats[:, 64:80], stats[:, 16:32], stats[:, 48:64])
                dve.inc(v.tensor_copy(stats[:, 90:91], stats[:, 64:65]))

            act2 = act.n
            sp0 = sp.n

            @block.scalar
            def _(s):
                s.wait_ge(dve.h, dve2 + 1)
                for tt in range(NT):
                    if tt >= nslots:
                        s.wait_ge(sp.h, sp0 + (tt - nslots + 1) * 16)
                    xt = xf[:, tt * 1024:(tt + 1) * 1024]
                    act.inc(s.activation(
                        ev[:, (tt % nslots) * sw:(tt % nslots) * sw + 1024],
                        xt, AF.Identity,
                        scale=stats[:, 48 + tt:49 + tt],
                        bias=stats[:, 64 + tt:65 + tt]), 1)

            @block.sync
            def _(sy):
                for tt in range(NT):
                    sy.wait_ge(act.h, act2 + tt + 1)
                    sp.inc(sy.dma_start(
                        out=h_b[tt * 128:(tt + 1) * 128, :],
                        in_=ev[:, (tt % nslots) * sw:(tt % nslots) * sw + 1024]), 16)
            return sp0

        def emit_attn_a(xin, xout):
            AP1, AP2, AP3 = aphase >= 2, aphase >= 3, aphase >= 4
            ret_st = [None]
            with (
                nc.sbuf_tensor(_nm("a_qT"), [128, NC * T], BF16) as qT,
                nc.sbuf_tensor(_nm("a_ev"), [128, 4 * 2048], BF16) as ev,
            ):
                _Wg = nc.sbuf_tensor(_nm("a_W"), [128, 3 * NC * 1024], BF16)
                _bg = nc.sbuf_tensor(_nm("a_b"), [1, 3 * D], BF16)
                W = _Wg.__enter__()
                abias = _bg.__enter__()
                ldw0 = ldw.n

                @block.gpsimd
                def _(g):
                    for wi, w in enumerate((aWkT, aWvT, aWqT)):
                        ldw.inc(g.dma_start(
                            out=W[:, wi * NC * 1024:(wi + 1) * NC * 1024]
                                .rearrange("p (c f) -> p c f", f=1024),
                            in_=w.rearrange("(c p) f -> p c f", p=128)), 16)
                    ldw.inc(g.dma_start(out=abias[:, :], in_=ab[:, :]), 16)

                with (
                    nc.sbuf_tensor(_nm("a_xf"), [128, NT * 1024], F32) as xf,
                    nc.sbuf_tensor(_nm("a_sc"), [128, 1024], F32) as hsc,
                ):
                    sp0 = emit_ln(xin, xf, hsc, ev, 4, 2048)

                if not AP1:
                    return
                with (
                    nc.sbuf_tensor(_nm("a_hT"), [128, NC * T], BF16) as hT,
                    nc.psum_tensor(_nm("a_pk0"), [128, 2048], F32) as pk0,
                    nc.psum_tensor(_nm("a_pk1"), [128, 2048], F32) as pk1,
                ):
                    ldx1 = ldx.n
                    sp_h_done = sp.n

                    @block.sync
                    def _(sy):
                        sy.wait_ge(sp.h, sp0 + NT * 16)
                        for dc in range(NC):
                            ldx.inc(sy.dma_start_transpose(
                                hT[:, dc * T:(dc + 1) * T],
                                h_b[:, dc * 128:(dc + 1) * 128]), 16)

                    def wsl(wi, dc, f0, fn):
                        base = wi * 8192 + dc * 1024 + f0
                        return W[:, base:base + fn]

                    mm0 = mm.n
                    act3 = act.n
                    sp1 = sp.n

                    @block.tensor
                    def _(t):
                        t.wait_ge(ldw.h, ldw0 + 64)
                        t.wait_ge(ldx.h, ldx1 + NC * 16)
                        for fc in range(NC):
                            ps = (pk0, pk1)[fc % 2]
                            if fc >= 2:
                                t.wait_ge(act.h, act3 + fc - 1)
                            for dc in range(NC):
                                for ib in range(4):
                                    t.matmul(ps[:, ib * 512:(ib + 1) * 512],
                                             wsl(0, dc, fc * 128, 128),
                                             hT[:, dc * T + ib * 512: dc * T + (ib + 1) * 512],
                                             start=(dc == 0), stop=False)
                            for ib in range(4):
                                last = ib == 3
                                m_ = t.matmul(
                                    ps[:, ib * 512:(ib + 1) * 512],
                                    abias[0:1, D + fc * 128: D + (fc + 1) * 128],
                                    ones[0:1, 0:512], start=False, stop=True)
                                if last:
                                    mm.inc(m_, 1)

                    @block.scalar
                    def _(s):
                        for fc in range(NC):
                            s.wait_ge(mm.h, mm0 + fc + 1)
                            if fc == 0:
                                s.wait_ge(sp.h, sp1)
                            if fc >= 4:
                                s.wait_ge(sp.h, sp1 + (fc - 3) * 16)
                            act.inc(s.activation(
                                ev[:, (fc % 4) * 2048:(fc % 4) * 2048 + 2048],
                                (pk0, pk1)[fc % 2][:, :], AF.Identity), 1)

                    @block.sync
                    def _(sy):
                        for fc in range(NC):
                            sy.wait_ge(act.h, act3 + fc + 1)
                            sp.inc(sy.dma_start(
                                out=akv[fc * 128:(fc + 1) * 128, :],
                                in_=ev[:, (fc % 4) * 2048:(fc % 4) * 2048 + 2048]), 16)

                    cc0 = cc.n
                    k_done = sp.n

                    @block.gpsimd
                    def _(g):
                        g.wait_ge(sp.h, k_done)
                        cc.inc(g.collective_compute(
                            "AllGather", ALU.bypass, replica_groups=rg,
                            ins=[akv[0:D, :].opt()], outs=[ag_k[:, :].opt()]), 1)

                    mm1 = mm.n
                    act4 = act.n
                    sp2 = sp.n

                    @block.tensor
                    def _(t):
                        for tt in range(NT):
                            ps = (pk0, pk1)[tt % 2]
                            pv = ps[:, 0:1024]
                            if tt >= 2:
                                t.wait_ge(act.h, act4 + tt - 1)
                            for dc in range(NC):
                                for fb in range(2):
                                    t.matmul(pv[:, fb * 512:(fb + 1) * 512],
                                             hT[:, dc * T + tt * 128: dc * T + (tt + 1) * 128],
                                             wsl(1, dc, fb * 512, 512),
                                             start=(dc == 0), stop=False)
                            for fb in range(2):
                                m_ = t.matmul(
                                    pv[:, fb * 512:(fb + 1) * 512],
                                    onesc[0:1, 0:128],
                                    abias[0:1, 2 * D + fb * 512: 2 * D + (fb + 1) * 512],
                                    start=False, stop=True)
                                if fb == 1:
                                    mm.inc(m_, 1)

                    @block.scalar
                    def _(s):
                        for tt in range(NT):
                            s.wait_ge(mm.h, mm1 + tt + 1)
                            if tt == 0:
                                s.wait_ge(sp.h, sp2)
                            if tt >= 4:
                                s.wait_ge(sp.h, sp2 + (tt - 3) * 16)
                            act.inc(s.activation(
                                ev[:, (tt % 4) * 2048:(tt % 4) * 2048 + 1024],
                                (pk0, pk1)[tt % 2][:, 0:1024], AF.Identity), 1)

                    @block.sync
                    def _(sy):
                        for tt in range(NT):
                            sy.wait_ge(act.h, act4 + tt + 1)
                            sp.inc(sy.dma_start(
                                out=AP(akv.tensor, D * T + tt * 128 * D,
                                       [[D, 128], [1, D]]),
                                in_=ev[:, (tt % 4) * 2048:(tt % 4) * 2048 + 1024]), 16)

                    v_done = sp.n

                    @block.gpsimd
                    def _(g):
                        g.wait_ge(sp.h, v_done)
                        cc.inc(g.collective_compute(
                            "AllGather", ALU.bypass, replica_groups=rg,
                            ins=[AP(akv.tensor, D * T, [[D, T], [1, D]]).opt()],
                            outs=[ag_v2[:, :].opt()]), 1)

                    mm2 = mm.n
                    act5 = act.n

                    @block.tensor
                    def _(t):
                        for fc in range(NC):
                            ps = (pk0, pk1)[fc % 2]
                            if fc >= 2:
                                t.wait_ge(act.h, act5 + fc - 1)
                            for dc in range(NC):
                                for ib in range(4):
                                    t.matmul(ps[:, ib * 512:(ib + 1) * 512],
                                             wsl(2, dc, fc * 128, 128),
                                             hT[:, dc * T + ib * 512: dc * T + (ib + 1) * 512],
                                             start=(dc == 0), stop=False)
                            for ib in range(4):
                                m_ = t.matmul(
                                    ps[:, ib * 512:(ib + 1) * 512],
                                    abias[0:1, fc * 128:(fc + 1) * 128],
                                    ones[0:1, 0:512], start=False, stop=True)
                                if ib == 3:
                                    mm.inc(m_, 1)

                    @block.scalar
                    def _(s):
                        for fc in range(NC):
                            s.wait_ge(mm.h, mm2 + fc + 1)
                            act.inc(s.activation(
                                qT[:, fc * T:(fc + 1) * T],
                                (pk0, pk1)[fc % 2][:, :], AF.Identity, scale=ASC), 1)

                _bg.__exit__(None, None, None)
                _Wg.__exit__(None, None, None)

                if not AP2:
                    return ret_st[0]
                with nc.sbuf_tensor(_nm("a_V"), [128, NJ * 1024], BF16) as Vsb:
                    with (
                        nc.sbuf_tensor(_nm("a_kT"), [128, NC * S], BF16) as ktT,
                        nc.sbuf_tensor(_nm("a_A"), [128, 2 * 2048], BF16) as Asl,
                        nc.psum_tensor(_nm("a_ps0"), [128, 2048], F32) as ps0,
                        nc.psum_tensor(_nm("a_ps1"), [128, 2048], F32) as ps1,
                    ):
                        ldx2 = ldx.n
                        qdone = act.n

                        @block.sync
                        def _(sy):
                            sy.wait_ge(act.h, qdone)
                            sy.wait_ge(cc.h, cc0 + 1)
                            for r in range(2):
                                for dc in range(NC):
                                    ldx.inc(sy.dma_start(
                                        out=ktT[:, dc * S + r * T: dc * S + r * T + T],
                                        in_=ag_k[D * r + dc * 128: D * r + (dc + 1) * 128, :]), 16)

                        mm3 = mm.n
                        act6 = act.n
                        sp3 = sp.n

                        @block.tensor
                        def _(t):
                            t.wait_ge(act.h, qdone)
                            t.wait_ge(ldx.h, ldx2 + 256)
                            for jt in range(NJ):
                                ps = (ps0, ps1)[jt % 2]
                                if jt >= 2:
                                    t.wait_ge(act.h, act6 + jt - 1)
                                for dc in range(NC):
                                    for ib in range(4):
                                        m_ = t.matmul(
                                            ps[:, ib * 512:(ib + 1) * 512],
                                            ktT[:, dc * S + jt * 128: dc * S + (jt + 1) * 128],
                                            qT[:, dc * T + ib * 512: dc * T + (ib + 1) * 512],
                                            start=(dc == 0), stop=(dc == NC - 1))
                                        if dc == NC - 1 and ib == 3:
                                            mm.inc(m_, 1)

                        @block.scalar
                        def _(s):
                            for jt in range(NJ):
                                s.wait_ge(mm.h, mm3 + jt + 1)
                                if jt >= 2:
                                    s.wait_ge(sp.h, sp3 + (jt - 1) * 16)
                                act.inc(s.activation(
                                    Asl[:, (jt % 2) * 2048:(jt % 2) * 2048 + 2048],
                                    (ps0, ps1)[jt % 2][:, :], AF.Exp,
                                    accum_out=zst[:, jt:jt + 1]), 1)
                            act.inc(s.activation(stats[:, 80:81], stats[:, 80:81], AF.Square), 1)

                        @block.sync
                        def _(sy):
                            for jt in range(NJ):
                                sy.wait_ge(act.h, act6 + jt + 1)
                                sp.inc(sy.dma_start(
                                    out=a_spill[jt * 128:(jt + 1) * 128, :],
                                    in_=Asl[:, (jt % 2) * 2048:(jt % 2) * 2048 + 2048]), 16)
                            sy.wait_ge(act.h, act6 + NJ + 1)
                            sp.inc(sy.dma_start(out=azb[:, :], in_=zst[:, 0:NJ]), 16)

                        cc1 = cc.n
                        z_done = sp.n

                        @block.gpsimd
                        def _(g):
                            g.wait_ge(sp.h, z_done)
                            cc.inc(g.collective_compute(
                                "AllGather", ALU.bypass, replica_groups=rg,
                                ins=[azb[:, :].opt()], outs=[ag_az[:, :].opt()]), 1)

                        ldx3 = ldx.n

                        @block.sync
                        def _(sy):
                            sy.wait_ge(cc.h, cc1 + 1)
                            ldx.inc(sy.dma_start(
                                out=zst[:, NJ:3 * NJ].rearrange("p (r j) -> p r j", r=2),
                                in_=ag_az.rearrange("(r p) j -> p r j", p=128)), 16)

                        dve3 = dve.n

                        @block.vector
                        def _(v):
                            v.wait_ge(ldx.h, ldx3 + 16)
                            v.tensor_add(zst[:, 0:NJ], zst[:, NJ:2 * NJ], zst[:, 2 * NJ:3 * NJ])
                            dve.inc(v.reciprocal(zst[:, 0:NJ], zst[:, 0:NJ]))

                    if not AP3:
                        return
                    _ngrp = 1 if aphase == 5 else (2 if aphase == 6 else (3 if aphase == 7 else 4))
                    with (
                        nc.sbuf_tensor(_nm("a_Ar"), [128, NJ * 512], BF16) as Ar,
                        nc.sbuf_tensor(_nm("a_xr"), [128, 4 * 1024], F32) as xr,
                        nc.sbuf_tensor(_nm("a_xo"), [128, 4 * 1024], F32) as xo,
                        nc.psum_tensor(_nm("a_po0"), [128, 1024], F32) as po0,
                        nc.psum_tensor(_nm("a_po1"), [128, 1024], F32) as po1,
                        nc.psum_tensor(_nm("a_po2"), [128, 1024], F32) as po2,
                        nc.psum_tensor(_nm("a_po3"), [128, 1024], F32) as po3,
                    ):
                        pos = (po0, po1, po2, po3)
                        ldx4 = ldx.n

                        @block.sync
                        def _(sy):
                            sy.wait_ge(act.h, qdone)
                            sy.wait_ge(cc.h, cc0 + 2)
                            for half in range(2):
                                ldx.inc(sy.dma_start(
                                    out=Vsb[:, half * 16 * 1024:(half + 1) * 16 * 1024]
                                        .rearrange("p (j d) -> p j d", d=1024),
                                    in_=AP(ag_v2.tensor, half * T * D,
                                           [[1024, 128], [128 * 1024, 16], [1, 1024]])), 16)

                        dve4 = dve.n

                        @block.vector
                        def _(v):
                            v.wait_ge(dve.h, dve3 + 1)
                            v.wait_ge(ldx.h, ldx4 + 32)
                            for jt in range(NJ):
                                dve.inc(v.tensor_scalar_mul(
                                    Vsb[:, jt * 1024:(jt + 1) * 1024],
                                    Vsb[:, jt * 1024:(jt + 1) * 1024],
                                    zst[:, jt:jt + 1]))

                        if aphase == 4:
                            return
                        ldx5 = ldx.n
                        mm4 = mm.n
                        dve5 = dve.n

                        st0 = st.n
                        ret_st[0] = st0

                        @block.sync
                        def _(sy):
                            for grp in range(_ngrp):
                                if grp >= 1:
                                    sy.wait_ge(mm.h, mm4 + grp * 4)
                                for q in range(4):
                                    ldx.inc(sy.dma_start(
                                        out=Ar[:, q * 8 * 512:(q + 1) * 8 * 512]
                                            .rearrange("p (j i) -> p j i", i=512),
                                        in_=AP(a_spill.tensor,
                                               q * 8 * 128 * T + grp * 512,
                                               [[T, 128], [128 * T, 8], [1, 512]])), 16)
                                if grp >= 1:
                                    sy.wait_ge(dve.h, dve5 + grp * 4)
                                for it in range(4):
                                    tt = grp * 4 + it
                                    ldx.inc(sy.dma_start(
                                        out=xr[:, it * 1024:(it + 1) * 1024],
                                        in_=xin[tt * 128:(tt + 1) * 128, :]), 16)
                                if grp >= 1:
                                    pg = grp - 1
                                    for it in range(4):
                                        tt = pg * 4 + it
                                        sy.wait_ge(dve.h, dve5 + pg * 4 + it + 1)
                                        st.inc(sy.dma_start(
                                            out=xout[tt * 128:(tt + 1) * 128, :],
                                            in_=xo[:, it * 1024:(it + 1) * 1024]), 16)
                            pg = _ngrp - 1
                            for it in range(4):
                                tt = pg * 4 + it
                                sy.wait_ge(dve.h, dve5 + pg * 4 + it + 1)
                                st.inc(sy.dma_start(
                                    out=xout[tt * 128:(tt + 1) * 128, :],
                                    in_=xo[:, it * 1024:(it + 1) * 1024]), 16)

                        @block.tensor
                        def _(t):
                            t.wait_ge(dve.h, dve4 + NJ)
                            for grp in range(_ngrp):
                                t.wait_ge(ldx.h, ldx5 + grp * 8 * 16 + 4 * 16)
                                if grp >= 1:
                                    t.wait_ge(dve.h, dve5 + grp * 4)
                                for jt in range(NJ):
                                    for it in range(4):
                                        a_sl = Ar[:, jt * 512 + it * 128:
                                                  jt * 512 + (it + 1) * 128]
                                        for fb in range(2):
                                            m_ = t.matmul(
                                                pos[it][:, fb * 512:(fb + 1) * 512],
                                                a_sl,
                                                Vsb[:, jt * 1024 + fb * 512: jt * 1024 + (fb + 1) * 512],
                                                start=(jt == 0), stop=(jt == NJ - 1))
                                            if jt == NJ - 1 and fb == 1:
                                                mm.inc(m_, 1)

                        @block.vector
                        def _(v):
                            for grp in range(_ngrp):
                                if grp >= 1:
                                    v.wait_ge(st.h, st0 + grp * 4 * 16)
                                for it in range(4):
                                    v.wait_ge(mm.h, mm4 + grp * 4 + it + 1)
                                    v.wait_ge(ldx.h, ldx5 + (grp + 1) * 8 * 16)
                                    dve.inc(v.tensor_add(
                                        xo[:, it * 1024:(it + 1) * 1024],
                                        xr[:, it * 1024:(it + 1) * 1024],
                                        pos[it][:, :]))

            return ret_st[0]

        def emit_ffn(xin, xout, st_base=None):
            ret_st = [None]
            NH2 = NT // 2
            TH = NH2 * 128
            with (
                nc.sbuf_tensor(_nm("f_W"), [128, 3 * NC * 1024], BF16) as W,
                nc.sbuf_tensor(_nm("f_prevA"), [8, 128], BF16) as prevA,
                nc.sbuf_tensor(_nm("f_prev"), [128, 16], BF16) as prevT,
                nc.psum_tensor(_nm("f_pb"), [128, 8], BF16) as pbP,
                nc.psum_tensor(_nm("f_pp0"), [128, 1024], F32) as pp0,
                nc.psum_tensor(_nm("f_pp1"), [128, 1024], F32) as pp1,
            ):
                f_mm, f_dve, f_act, f_st, f_sp = mm.n, dve.n, act.n, st.n, sp.n
                ldw0 = ldw.n

                @block.gpsimd
                def _(g):
                    g.wait_ge(mm.h, f_mm)
                    g.wait_ge(dve.h, f_dve)
                    g.wait_ge(act.h, f_act)
                    g.wait_ge(st.h, f_st)
                    for wi, w in enumerate((fWrT, fWkT, fWvT)):
                        ldw.inc(g.dma_start(
                            out=W[:, wi * NC * 1024:(wi + 1) * NC * 1024]
                                .rearrange("p (c f) -> p c f", f=1024),
                            in_=w.rearrange("(c p) f -> p c f", p=128)), 16)

                with (
                    nc.sbuf_tensor(_nm("f_xf"), [128, NT * 1024], F32) as xf,
                    nc.sbuf_tensor(_nm("f_sc"), [128, 1024], F32) as hsc,
                    nc.sbuf_tensor(_nm("f_ev"), [128, 2 * 1024], BF16) as ev2,
                ):
                    sp0 = emit_ln(xin, xf, hsc, ev2, 2, 1024, st_base=st_base,
                                  fence=((mm, f_mm), (dve, f_dve), (act, f_act)))

                sp_b = sp.n

                @block.sync
                def _(sy):
                    sy.wait_ge(sp.h, sp0 + NT * 16)
                    sp.inc(sy.dma_start(out=bnd_in[0:1, :], in_=h_b[T - 1:T, :]), 16)

                cc_b = cc.n

                @block.gpsimd
                def _(g):
                    g.wait_ge(sp.h, sp_b + 16)
                    cc.inc(g.collective_compute(
                        "AllGather", ALU.bypass, replica_groups=rg,
                        ins=[bnd_in[:, :].opt()], outs=[ag_bnd[:, :].opt()]), 1)

                ldxb = ldx.n

                @block.sync
                def _(sy):
                    sy.wait_ge(cc.h, cc_b + 1)
                    ldx.inc(sy.dma_start(
                        out=prevA[:, :],
                        in_=AP(ag_bnd.tensor, 0, [[128, 8], [1, 128]])), 16)

                mm_b = mm.n

                @block.tensor
                def _(t):
                    t.wait_ge(ldx.h, ldxb + 16)
                    t.wait_ge(idm.h, idm.n)
                    mm.inc(t.transpose(pbP[:, :], prevA[:, :], ident[0:8, 0:8]), 1)

                dve_b = dve.n

                @block.vector
                def _(v):
                    v.wait_ge(mm.h, mm_b + 1)
                    v.wait_ge(sp.h, 16)
                    dve.inc(v.tensor_scalar_mul(prevT[:, 0:8], pbP[:, :], msl[:, 0:1]))

                for hh in range(2):
                    with (
                        nc.sbuf_tensor(_nm("f_hT"), [128, NC * 1024], BF16) as hT,
                        nc.sbuf_tensor(_nm("f_xrT"), [128, NC * 1024], BF16) as xrT,
                        nc.sbuf_tensor(_nm("f_xkT"), [128, NC * 1024], BF16) as xkT,
                        nc.sbuf_tensor(_nm("f_r"), [128, NH2 * 1024], BF16) as rti,
                        nc.sbuf_tensor(_nm("f_ka"), [128, NC * 1024], BF16) as kaT,
                        nc.sbuf_tensor(_nm("f_rs"), [128, 2 * 1024], F32) as rsc,
                        nc.sbuf_tensor(_nm("f_tmp"), [128, 1024], BF16) as tmpc,
                        nc.sbuf_tensor(_nm("f_t1"), [128, 1024], F32) as t1,
                        nc.sbuf_tensor(_nm("f_xr"), [128, 2 * 1024], F32) as xrr,
                        nc.sbuf_tensor(_nm("f_xo"), [128, 2 * 1024], F32) as xor_,
                    ):
                        pcol = 0 if hh == 0 else 8
                        ldxh = ldx.n
                        if hh == 1:
                            h0_mm, h0_dve, h0_act, h0_st = mm.n, dve.n, act.n, st.n
                        else:
                            h0_mm = h0_dve = h0_act = h0_st = 0

                        @block.sync
                        def _(sy):
                            if hh == 0:
                                sy.wait_ge(sp.h, sp0 + NT * 16)
                            else:
                                sy.wait_ge(dve.h, h0_dve)
                                sy.wait_ge(act.h, h0_act)
                            for dc in range(NC):
                                ldx.inc(sy.dma_start_transpose(
                                    hT[:, dc * 1024:(dc + 1) * 1024],
                                    h_b[hh * TH:(hh + 1) * TH,
                                        dc * 128:(dc + 1) * 128]), 16)

                        dve_m = dve.n

                        @block.vector
                        def _(v):
                            v.wait_ge(dve.h, dve_b + 1)
                            if hh == 1:
                                v.wait_ge(dve.h, dve_sv)
                                v.wait_ge(mm.h, h0_mm)
                            for dc in range(NC):
                                v.wait_ge(ldx.h, ldxh + (dc + 1) * 16)
                                hc = hT[:, dc * 1024:(dc + 1) * 1024]
                                pv = prevT[:, pcol + dc:pcol + dc + 1]
                                v.tensor_sub(tmpc[:, 1:1024], hc[:, 1:1024], hc[:, 0:1023])
                                v.tensor_sub(tmpc[:, 0:1], hc[:, 0:1], pv)
                                v.scalar_tensor_tensor(
                                    xrT[:, dc * 1024 + 1:(dc + 1) * 1024],
                                    tmpc[:, 1:1024], mur[:, dc:dc + 1], hc[:, 0:1023],
                                    ALU.mult, ALU.add)
                                v.scalar_tensor_tensor(
                                    xrT[:, dc * 1024:dc * 1024 + 1],
                                    tmpc[:, 0:1], mur[:, dc:dc + 1], pv,
                                    ALU.mult, ALU.add)
                                v.scalar_tensor_tensor(
                                    xkT[:, dc * 1024 + 1:(dc + 1) * 1024],
                                    tmpc[:, 1:1024], muk[:, dc:dc + 1], hc[:, 0:1023],
                                    ALU.mult, ALU.add)
                                dve.inc(v.scalar_tensor_tensor(
                                    xkT[:, dc * 1024:dc * 1024 + 1],
                                    tmpc[:, 0:1], muk[:, dc:dc + 1], pv,
                                    ALU.mult, ALU.add))
                            if hh == 0:
                                dve.inc(v.tensor_copy(
                                    prevT[:, 8:16],
                                    hT[:, :].rearrange("p (c t) -> p c t", t=1024)[:, :, 1023]))

                        if hh == 0:
                            dve_sv = dve.n

                        mm_r = mm.n
                        act_r = act.n

                        @block.tensor
                        def _(t):
                            t.wait_ge(ldw.h, ldw0 + 16)
                            for tt in range(NH2):
                                ps = (pp0, pp1)[tt % 2]
                                if tt == 0:
                                    t.wait_ge(dve.h, dve_m + NC)
                                if tt >= 2:
                                    t.wait_ge(act.h, act_r + tt - 1)
                                for dc in range(NC):
                                    for fb in range(2):
                                        m_ = t.matmul(
                                            ps[:, fb * 512:(fb + 1) * 512],
                                            xrT[:, dc * 1024 + tt * 128: dc * 1024 + (tt + 1) * 128],
                                            W[:, dc * 1024 + fb * 512: dc * 1024 + (fb + 1) * 512],
                                            start=(dc == 0), stop=(dc == NC - 1))
                                        if dc == NC - 1 and fb == 1:
                                            mm.inc(m_, 1)

                        @block.scalar
                        def _(s):
                            if hh == 1:
                                s.wait_ge(dve.h, h0_dve)
                            for tt in range(NH2):
                                s.wait_ge(mm.h, mm_r + tt + 1)
                                act.inc(s.activation(
                                    rti[:, tt * 1024:(tt + 1) * 1024],
                                    (pp0, pp1)[tt % 2][:, :], AF.Sigmoid), 1)

                        mm_k = mm.n
                        act_k = act.n
                        dve_k = dve.n

                        @block.tensor
                        def _(t):
                            t.wait_ge(ldw.h, ldw0 + 32)
                            for fc in range(NC):
                                ps = (pp0, pp1)[fc % 2]
                                if fc == 0:
                                    t.wait_ge(act.h, act_r + NH2)
                                if fc >= 2:
                                    t.wait_ge(act.h, act_k + fc - 1)
                                for dc in range(NC):
                                    for ib in range(2):
                                        m_ = t.matmul(
                                            ps[:, ib * 512:(ib + 1) * 512],
                                            W[:, 8192 + dc * 1024 + fc * 128: 8192 + dc * 1024 + fc * 128 + 128],
                                            xkT[:, dc * 1024 + ib * 512: dc * 1024 + (ib + 1) * 512],
                                            start=(dc == 0), stop=(dc == NC - 1))
                                        if dc == NC - 1 and ib == 1:
                                            mm.inc(m_, 1)

                        @block.scalar
                        def _(s):
                            for fc in range(NC):
                                s.wait_ge(mm.h, mm_k + fc + 1)
                                if fc >= 2:
                                    s.wait_ge(dve.h, dve_k + fc - 1)
                                act.inc(s.activation(
                                    rsc[:, (fc % 2) * 1024:(fc % 2) * 1024 + 1024],
                                    (pp0, pp1)[fc % 2][:, :], AF.Relu), 1)

                        @block.vector
                        def _(v):
                            for fc in range(NC):
                                v.wait_ge(act.h, act_k + fc + 1)
                                dve.inc(v.tensor_mul(
                                    kaT[:, fc * 1024:(fc + 1) * 1024],
                                    rsc[:, (fc % 2) * 1024:(fc % 2) * 1024 + 1024],
                                    rsc[:, (fc % 2) * 1024:(fc % 2) * 1024 + 1024]))

                        mm_v = mm.n
                        dve_v = dve.n
                        ldxv = ldx.n
                        st_v = st.n
                        if hh == 0:
                            ret_st[0] = st_v

                        @block.sync
                        def _(sy):
                            if hh == 1:
                                sy.wait_ge(dve.h, h0_dve)
                            for tt in range(NH2):
                                if tt >= 2:
                                    sy.wait_ge(dve.h, dve_v + tt - 1)
                                ldx.inc(sy.dma_start(
                                    out=xrr[:, (tt % 2) * 1024:(tt % 2) * 1024 + 1024],
                                    in_=xin[hh * TH + tt * 128: hh * TH + (tt + 1) * 128, :]), 16)
                                if tt >= 1:
                                    pt = tt - 1
                                    sy.wait_ge(dve.h, dve_v + pt + 1)
                                    st.inc(sy.dma_start(
                                        out=xout[hh * TH + pt * 128: hh * TH + (pt + 1) * 128, :],
                                        in_=xor_[:, (pt % 2) * 1024:(pt % 2) * 1024 + 1024]), 16)
                            pt = NH2 - 1
                            sy.wait_ge(dve.h, dve_v + pt + 1)
                            st.inc(sy.dma_start(
                                out=xout[hh * TH + pt * 128: hh * TH + (pt + 1) * 128, :],
                                in_=xor_[:, (pt % 2) * 1024:(pt % 2) * 1024 + 1024]), 16)

                        @block.tensor
                        def _(t):
                            t.wait_ge(ldw.h, ldw0 + 48)
                            for tt in range(NH2):
                                ps = (pp0, pp1)[tt % 2]
                                if tt == 0:
                                    t.wait_ge(act.h, act_k + NC)
                                    t.wait_ge(dve.h, dve_k + NC)
                                if tt >= 2:
                                    t.wait_ge(dve.h, dve_v + tt - 1)
                                for dc in range(NC):
                                    for fb in range(2):
                                        m_ = t.matmul(
                                            ps[:, fb * 512:(fb + 1) * 512],
                                            kaT[:, dc * 1024 + tt * 128: dc * 1024 + (tt + 1) * 128],
                                            W[:, 2 * 8192 + dc * 1024 + fb * 512: 2 * 8192 + dc * 1024 + (fb + 1) * 512],
                                            start=(dc == 0), stop=(dc == NC - 1))
                                        if dc == NC - 1 and fb == 1:
                                            mm.inc(m_, 1)

                        @block.vector
                        def _(v):
                            for tt in range(NH2):
                                v.wait_ge(mm.h, mm_v + tt + 1)
                                v.wait_ge(ldx.h, ldxv + (tt + 1) * 16)
                                if tt >= 2:
                                    v.wait_ge(st.h, st_v + (tt - 1) * 16)
                                v.tensor_mul(t1[:, :],
                                             (pp0, pp1)[tt % 2][:, :],
                                             rti[:, tt * 1024:(tt + 1) * 1024])
                                dve.inc(v.tensor_add(
                                    xor_[:, (tt % 2) * 1024:(tt % 2) * 1024 + 1024],
                                    xrr[:, (tt % 2) * 1024:(tt % 2) * 1024 + 1024],
                                    t1[:, :]), 1)

            return ret_st[0]

        def emit_attn_b(xin, xout, st_base=None):
            ret_stb = [None]
            HPC = MYH // 2

            with (
                nc.sbuf_tensor(_nm("b_W"), [128, 3 * NC * 1024], BF16) as W,
                nc.sbuf_tensor(_nm("b_bias"), [1, 4 * D], BF16) as bbias,
            ):
                f_mm, f_dve, f_act, f_st = mm.n, dve.n, act.n, st.n
                ldw0 = ldw.n

                @block.gpsimd
                def _(g):
                    g.wait_ge(mm.h, f_mm)
                    g.wait_ge(dve.h, f_dve)
                    g.wait_ge(act.h, f_act)
                    g.wait_ge(st.h, f_st)
                    for wi, w in enumerate((bWq, bWk, bWv)):
                        ldw.inc(g.dma_start(
                            out=W[:, wi * NC * 1024:(wi + 1) * NC * 1024]
                                .rearrange("p (c f) -> p c f", f=1024),
                            in_=w.rearrange("(c p) f -> p c f", p=128)), 16)
                    for bi, b_ in enumerate((bqb, bkb, bvb, bob)):
                        ldw.inc(g.dma_start(
                            out=bbias[:, bi * D:(bi + 1) * D], in_=b_[:, :]), 16)

                with (
                    nc.sbuf_tensor(_nm("b_xf"), [128, NT * 1024], F32) as xf,
                    nc.sbuf_tensor(_nm("b_sc"), [128, 1024], F32) as hsc,
                    nc.sbuf_tensor(_nm("b_ev"), [128, 2 * 1024], BF16) as ev2,
                ):
                    sp0 = emit_ln(xin, xf, hsc, ev2, 2, 1024, st_base=st_base,
                                  fence=((mm, f_mm), (dve, f_dve), (act, f_act)))

                if bphase < 2:
                    return
                with (
                    nc.sbuf_tensor(_nm("b_hT"), [128, NC * T], BF16) as hT,
                    nc.sbuf_tensor(_nm("b_rt1"), [128, 1024], F32) as rt1,
                    nc.sbuf_tensor(_nm("b_rt2"), [128, 1024], F32) as rt2,
                    nc.sbuf_tensor(_nm("b_rt3"), [128, 1024], F32) as rt3,
                    nc.sbuf_tensor(_nm("b_rt4"), [128, 1024], F32) as rt4,
                    nc.sbuf_tensor(_nm("b_qr"), [128, 2 * 1024], BF16) as qring,
                    nc.sbuf_tensor(_nm("b_kr"), [128, 2 * 1024], BF16) as kring,
                    nc.sbuf_tensor(_nm("b_vr"), [128, 2 * 1024], BF16) as vring,
                    nc.psum_tensor(_nm("b_pq"), [128, 1024], F32) as pq,
                    nc.psum_tensor(_nm("b_pk"), [128, 1024], F32) as pk,
                    nc.psum_tensor(_nm("b_pv"), [128, 1024], F32) as pv,
                ):
                    ldx1 = ldx.n

                    @block.sync
                    def _(sy):
                        sy.wait_ge(sp.h, sp0 + NT * 16)
                        for dc in range(NC):
                            ldx.inc(sy.dma_start_transpose(
                                hT[:, dc * T:(dc + 1) * T],
                                h_b[:, dc * 128:(dc + 1) * 128]), 16)

                    mm_p = mm.n
                    dve_p = dve.n
                    act_p = act.n
                    sp_p = sp.n

                    @block.tensor
                    def _(t):
                        t.wait_ge(ldw.h, ldw0 + 7 * 16)
                        t.wait_ge(ldx.h, ldx1 + NC * 16)
                        for tt in range(NT):
                            if tt >= 1:
                                t.wait_ge(dve.h, dve_p + tt * 2)
                                t.wait_ge(act.h, act_p + tt)
                            for wi, ps in ((0, pq), (1, pk), (2, pv)):
                                for dc in range(NC):
                                    for fb in range(2):
                                        t.matmul(
                                            ps[:, fb * 512:(fb + 1) * 512],
                                            hT[:, dc * T + tt * 128: dc * T + (tt + 1) * 128],
                                            W[:, wi * 8192 + dc * 1024 + fb * 512:
                                              wi * 8192 + dc * 1024 + (fb + 1) * 512],
                                            start=(dc == 0), stop=False)
                                for fb in range(2):
                                    m_ = t.matmul(
                                        ps[:, fb * 512:(fb + 1) * 512],
                                        onesc[0:1, 0:128],
                                        bbias[0:1, wi * D + fb * 512: wi * D + (fb + 1) * 512],
                                        start=False, stop=True)
                                    if fb == 1:
                                        mm.inc(m_, 1)

                    def rope_pair(v, tt):
                        for src, rt in ((pq, rt1), (pk, rt2)):
                            v.tensor_mul(
                                AP(rt, 0, [[1024, 128], [64, 16], [2, 32]]),
                                AP(src, 1, [[1024, 128], [64, 16], [2, 32]]),
                                AP(ropeSs, tt * 64, [[NT * 64, 128], [0, 16], [2, 32]]))
                            v.tensor_mul(
                                AP(rt, 1, [[1024, 128], [64, 16], [2, 32]]),
                                AP(src, 0, [[1024, 128], [64, 16], [2, 32]]),
                                AP(ropeSs, tt * 64 + 1, [[NT * 64, 128], [0, 16], [2, 32]]))
                        for src, rt, rq in ((pq, rt1, rt3), (pk, rt2, rt4)):
                            v.tensor_mul(
                                AP(rq, 0, [[1024, 128], [64, 16], [1, 64]]),
                                AP(src, 0, [[1024, 128], [64, 16], [1, 64]]),
                                AP(ropeCs, tt * 64, [[NT * 64, 128], [0, 16], [1, 64]]))
                        for src, rt, rq, ring in ((pq, rt1, rt3, qring), (pk, rt2, rt4, kring)):
                            dve.inc(v.tensor_add(
                                ring[:, (tt % 2) * 1024:(tt % 2) * 1024 + 1024],
                                rq[:, :], rt[:, :]))

                    def rope(v, src, ring, tt):
                        v.tensor_mul(
                            AP(rt1, 0, [[1024, 128], [64, 16], [2, 32]]),
                            AP(src, 1, [[1024, 128], [64, 16], [2, 32]]),
                            AP(ropeSs, tt * 64, [[NT * 64, 128], [0, 16], [2, 32]]))
                        v.tensor_mul(
                            AP(rt1, 1, [[1024, 128], [64, 16], [2, 32]]),
                            AP(src, 0, [[1024, 128], [64, 16], [2, 32]]),
                            AP(ropeSs, tt * 64 + 1, [[NT * 64, 128], [0, 16], [2, 32]]))
                        v.tensor_mul(
                            AP(rt2, 0, [[1024, 128], [64, 16], [1, 64]]),
                            AP(src, 0, [[1024, 128], [64, 16], [1, 64]]),
                            AP(ropeCs, tt * 64, [[NT * 64, 128], [0, 16], [1, 64]]))
                        dve.inc(v.tensor_add(
                            ring[:, (tt % 2) * 1024:(tt % 2) * 1024 + 1024],
                            rt2[:, :], rt1[:, :]))

                    @block.vector
                    def _(v):
                        for tt in range(NT):
                            v.wait_ge(mm.h, mm_p + tt * 3 + 2)
                            if tt >= 2:
                                v.wait_ge(sp.h, sp_p + (tt - 2) * 96 + 64)
                            rope_pair(v, tt)

                    @block.scalar
                    def _(s):
                        for tt in range(NT):
                            s.wait_ge(mm.h, mm_p + tt * 3 + 3)
                            if tt >= 2:
                                s.wait_ge(sp.h, sp_p + (tt - 2) * 96 + 96)
                            act.inc(s.activation(
                                vring[:, (tt % 2) * 1024:(tt % 2) * 1024 + 1024],
                                pv[:, :], AF.Identity), 1)

                    @block.sync
                    def _(sy):
                        for tt in range(NT):
                            sy.wait_ge(dve.h, dve_p + tt * 2 + 1)
                            sp.inc(sy.dma_start(
                                out=qkvo[tt * 128:(tt + 1) * 128, :],
                                in_=qring[:, (tt % 2) * 1024:(tt % 2) * 1024 + 512]), 16)
                            sp.inc(sy.dma_start(
                                out=bqkv[tt * 128:(tt + 1) * 128, :],
                                in_=qring[:, (tt % 2) * 1024 + 512:(tt % 2) * 1024 + 1024]), 16)
                            sy.wait_ge(dve.h, dve_p + tt * 2 + 2)
                            sp.inc(sy.dma_start(
                                out=qkvo[T + tt * 128: T + (tt + 1) * 128, :],
                                in_=kring[:, (tt % 2) * 1024:(tt % 2) * 1024 + 512]), 16)
                            sp.inc(sy.dma_start(
                                out=bqkv[T + tt * 128: T + (tt + 1) * 128, :],
                                in_=kring[:, (tt % 2) * 1024 + 512:(tt % 2) * 1024 + 1024]), 16)
                            sy.wait_ge(act.h, act_p + tt + 1)
                            sp.inc(sy.dma_start(
                                out=qkvo[2 * T + tt * 128: 2 * T + (tt + 1) * 128, :],
                                in_=vring[:, (tt % 2) * 1024:(tt % 2) * 1024 + 512]), 16)
                            sp.inc(sy.dma_start(
                                out=bqkv[2 * T + tt * 128: 2 * T + (tt + 1) * 128, :],
                                in_=vring[:, (tt % 2) * 1024 + 512:(tt % 2) * 1024 + 1024]), 16)

                if bphase < 3:
                    return
                sp_x = sp.n
                roff = {}

                @block.sync
                def _(sy):
                    sy.wait_ge(sp.h, sp_x)
                    if dbg:
                        st.inc(sy.dma_start(out=dhb[:, :], in_=h_b[:, :]), 16)
                        st.inc(sy.dma_start(out=dqb[0:3*T, 0:512], in_=qkvo[:, :]), 16)
                    pair = pid_pair(sy)
                    roff["own"] = sy.snap(pair * (T * 512), donate=True)
                    roff["oth"] = sy.snap((1 - pair) * (T * 512), donate=True)
                    roff["oth3"] = sy.snap((1 - pair) * (3 * T * 512), donate=True)
                    for xi in range(3):
                        sp.inc(sy.dma_start(
                            out=AP(qkvg.tensor, roff["own"] + xi * (S * 512),
                                   [[512, T], [1, 512]]),
                            in_=AP(qkvo.tensor, xi * T * 512,
                                   [[512, T], [1, 512]])), 16)

                cc_x = cc.n

                @block.gpsimd
                def _(g):
                    g.wait_ge(sp.h, sp_x)
                    cc.inc(g.collective_compute(
                        "AllGather", ALU.bypass, replica_groups=rg,
                        ins=[AP(bqkv.tensor, 0, [[512, 3 * T], [1, 512]]).opt()],
                        outs=[AP(ag_bqkv.tensor, 0, [[512, 6 * T], [1, 512]]).opt()]), 1)

                sp_y = sp.n

                @block.sync
                def _(sy):
                    sy.wait_ge(cc.h, cc_x + 1)
                    for xi in range(3):
                        sp.inc(sy.dma_start(
                            out=AP(qkvg.tensor, roff["oth"] + xi * (S * 512),
                                   [[512, T], [1, 512]]),
                            in_=AP(ag_bqkv.tensor, roff["oth3"] + xi * (T * 512),
                                   [[512, T], [1, 512]])), 16)

                if bphase < 4:
                    return
                _Wog = nc.sbuf_tensor(_nm("b_Wo"), [128, NC * 1024], BF16)
                Wo = _Wog.__enter__()
                ldw1 = ldw.n

                @block.gpsimd
                def _(g):
                    ldw.inc(g.dma_start(
                        out=Wo[:, :].rearrange("p (c f) -> p c f", f=1024),
                        in_=bWo.rearrange("(c p) f -> p c f", p=128)), 16)

                with (
                    nc.sbuf_tensor(_nm("b_qT"), [128, HPC * S], BF16) as qT,
                    nc.sbuf_tensor(_nm("b_kT"), [128, HPC * S], BF16) as kT,
                    nc.sbuf_tensor(_nm("b_va"), [128, NJ * 520], BF16) as Vau,
                    nc.sbuf_tensor(_nm("b_ea"), [128, 3 * 1024], BF16) as eA,
                    nc.sbuf_tensor(_nm("b_yt"), [128, 2 * 2048], BF16) as ytk,
                    nc.sbuf_tensor(_nm("b_rz"), [128, 8], F32) as rz,
                    nc.sbuf_tensor(_nm("b_dz"), [128, 2 * 32], F32) as dzsb,
                    nc.psum_tensor(_nm("b_ps0"), [128, 1024], F32) as bs0,
                    nc.psum_tensor(_nm("b_ps1"), [128, 1024], F32) as bs1,
                    nc.psum_tensor(_nm("b_av0"), [128, 1024], F32) as av0,
                    nc.psum_tensor(_nm("b_av1"), [128, 1024], F32) as av1,
                ):
                    ldx2 = ldx.n
                    prj_dve, prj_act, prj_mm = dve.n, act.n, mm.n

                    @block.sync
                    def _(sy):
                        sy.wait_ge(dve.h, prj_dve)
                        sy.wait_ge(act.h, prj_act)
                        sy.wait_ge(sp.h, sp_y + 3 * 16)
                        if dbg:
                            st.inc(sy.dma_start(out=dqkv[:, :],
                                                in_=AP(qkvg.tensor, 0, [[512, 3 * S], [1, 512]])), 16)
                        for hp in range(HPC):
                            ldx.inc(sy.dma_start_transpose(
                                qT[:, hp * S:(hp + 1) * S],
                                AP(qkvg.tensor, hp * 128, [[512, S], [1, 128]])), 16)
                            ldx.inc(sy.dma_start_transpose(
                                kT[:, hp * S:(hp + 1) * S],
                                AP(qkvg.tensor, S * 512 + hp * 128, [[512, S], [1, 128]])), 16)
                        for jt in range(NJ):
                            ldx.inc(sy.dma_start(
                                out=Vau[:, jt * 520:(jt + 1) * 520]
                                    .rearrange("p (h c) -> p h c", c=65)[:, :, 0:64],
                                in_=AP(qkvg.tensor, 2 * S * 512 + jt * 128 * 512,
                                       [[512, 128], [64, 8], [1, 64]])), 16)

                    dve_o = dve.n

                    @block.vector
                    def _(v):
                        v.wait_ge(mm.h, prj_mm)
                        dve.inc(v.memset(
                            Vau[:, :].rearrange("p (j h c) -> p j h c", h=8, c=65)
                               [:, :, :, 64], 1.0))

                    sched = []
                    for ib in range(NIB):
                        for hp in range(HPC):
                            jt_hi = 4 * (ib + 1)
                            for jt in range(jt_hi):
                                sched.append((ib, hp, jt, jt == jt_hi - 1))
                    NU = len(sched)
                    NG = NIB * HPC
                    grp_of = []
                    g = -1
                    for u, (ib, hp, jt, lastj) in enumerate(sched):
                        if jt == 0:
                            g += 1
                        grp_of.append(g)
                    diag = [jt >= 4 * ib for (ib, hp, jt, _l) in sched]

                    mm_s = mm.n
                    act_s = act.n
                    dve_s = dve.n
                    st_s = st.n

                    dve_after_mask = [0] * NU
                    dve_after_evac = [0] * NG
                    c = dve_s + 0
                    for u in range(NU):
                        if diag[u]:
                            c += 1
                        dve_after_mask[u] = c
                        if sched[u][3]:
                            c += 1
                            dve_after_evac[grp_of[u]] = c

                    sc_mm = [0] * NU
                    av_mm = [0] * NU

                    @block.tensor
                    def _(t):
                        t.wait_ge(dve.h, dve_o + 1)
                        t.wait_ge(act.h, prj_act)
                        t.wait_ge(ldx.h, ldx2 + (8 + NJ) * 16)

                        def emit_scores(u):
                            ib, hp, jt, _l = sched[u]
                            ps = (bs0, bs1)[u % 2]
                            if u >= 2:
                                t.wait_ge(act.h, act_s + u - 1)
                            for h2 in range(2):
                                m_ = t.matmul(
                                    ps[:, h2 * 512:(h2 + 1) * 512],
                                    kT[64 * h2:64 * h2 + 64,
                                       hp * S + jt * 128: hp * S + (jt + 1) * 128],
                                    qT[64 * h2:64 * h2 + 64,
                                       hp * S + ib * 512: hp * S + (ib + 1) * 512],
                                    start=True, stop=True,
                                    tile_position=(64 * h2, 0))
                            mm.inc(m_, 1)
                            sc_mm[u] = mm.n

                        def emit_av(u):
                            ib, hp, jt, lastj = sched[u]
                            g_ = grp_of[u]
                            av = (av0, av1)[g_ % 2]
                            t.wait_ge(act.h, act_s + u + 1)
                            if diag[u]:
                                t.wait_ge(dve.h, dve_after_mask[u])
                            if jt == 0 and g_ >= 2:
                                t.wait_ge(dve.h, dve_after_evac[g_ - 2])
                            sl = (u % 3) * 1024
                            for h2 in range(2):
                                for q in range(4):
                                    m_ = t.matmul(
                                        av[:, (h2 * 4 + q) * 128:(h2 * 4 + q) * 128 + 65],
                                        eA[:, sl + h2 * 512 + q * 128: sl + h2 * 512 + (q + 1) * 128],
                                        Vau[:, jt * 520 + (hp * 2 + h2) * 65:
                                            jt * 520 + (hp * 2 + h2) * 65 + 65],
                                        start=(jt == 0 and q == 0),
                                        stop=(lastj and q == 3))
                            mm.inc(m_, 1)
                            av_mm[u] = mm.n

                        for u in range(NU):
                            emit_scores(u)
                            if u >= 1:
                                emit_av(u - 1)
                        emit_av(NU - 1)

                    @block.scalar
                    def _(s):
                        for u in range(NU):
                            s.wait_ge(mm.h, sc_mm[u])
                            if u >= 3:
                                s.wait_ge(mm.h, av_mm[u - 3])
                            act.inc(s.activation(
                                eA[:, (u % 3) * 1024:(u % 3) * 1024 + 1024],
                                (bs0, bs1)[u % 2][:, :], AF.Exp, scale=BSC), 1)

                    @block.vector
                    def _(v):
                        for u in range(NU):
                            ib, hp, jt, lastj = sched[u]
                            if diag[u]:
                                v.wait_ge(act.h, act_s + u + 1)
                                k_ = jt - 4 * ib
                                sl = (u % 3) * 1024
                                v.tensor_mul(eA[:, sl:sl + 512], eA[:, sl:sl + 512],
                                             masks[:, k_ * 512:(k_ + 1) * 512])
                                dve.inc(v.tensor_mul(
                                    eA[:, sl + 512:sl + 1024], eA[:, sl + 512:sl + 1024],
                                    masks[:, k_ * 512:(k_ + 1) * 512]))
                            if lastj:
                                g_ = grp_of[u]
                                av = (av0, av1)[g_ % 2]
                                v.wait_ge(mm.h, av_mm[u])
                                if ib >= 2 and hp == 0:
                                    v.wait_ge(st.h, st_s + (ib - 1) * 4 * 16)
                                for h2 in range(2):
                                    for q in range(4):
                                        base = (h2 * 4 + q) * 128
                                        if dbg:
                                            v.tensor_copy(
                                                dzsb[:, (ib % 2) * 32 + q * 8 + hp * 2 + h2:
                                                     (ib % 2) * 32 + q * 8 + hp * 2 + h2 + 1],
                                                av[:, base + 64:base + 65])
                                        v.reciprocal(rz[:, h2 * 4 + q:h2 * 4 + q + 1],
                                                     av[:, base + 64:base + 65])
                                for h2 in range(2):
                                    for q in range(4):
                                        base = (h2 * 4 + q) * 128
                                        m2 = v.tensor_scalar_mul(
                                            ytk[:, (ib % 2) * 2048 + q * 512 +
                                                (hp * 2 + h2) * 64:
                                                (ib % 2) * 2048 + q * 512 +
                                                (hp * 2 + h2) * 64 + 64],
                                            av[:, base:base + 64],
                                            rz[:, h2 * 4 + q:h2 * 4 + q + 1])
                                dve.inc(m2)

                    @block.sync
                    def _(sy):
                        for ib in range(NIB):
                            gend = dve_after_evac[ib * HPC + HPC - 1]
                            sy.wait_ge(dve.h, gend)
                            for q in range(4):
                                st.inc(sy.dma_start(
                                    out=yb_tok[(ib * 4 + q) * 128:(ib * 4 + q + 1) * 128, :],
                                    in_=ytk[:, (ib % 2) * 2048 + q * 512:
                                            (ib % 2) * 2048 + (q + 1) * 512]), 16)
                                if dbg:
                                    st.inc(sy.dma_start(
                                        out=dzt[(ib * 4 + q) * 128:(ib * 4 + q + 1) * 128, :],
                                        in_=dzsb[:, (ib % 2) * 32 + q * 8:
                                                 (ib % 2) * 32 + (q + 1) * 8]), 16)

                if bphase < 5:
                    return
                sp_ye = sp.n
                st_att = st.n

                @block.sync
                def _(sy):
                    sy.wait_ge(st.h, st_att)
                    if dbg:
                        st.inc(sy.dma_start(out=dyt[:, :],
                                            in_=AP(yb_tok.tensor, 0, [[512, S], [1, 512]])), 16)
                    sp.inc(sy.dma_start(
                        out=AP(ybx.tensor, 0, [[512, T], [1, 512]]),
                        in_=AP(yb_tok.tensor, roff["oth"], [[512, T], [1, 512]])), 16)
                    sp.inc(sy.dma_start(
                        out=AP(yg2.tensor, 0, [[1024, T], [1, 512]]),
                        in_=AP(yb_tok.tensor, roff["own"], [[512, T], [1, 512]])), 16)

                cc_y = cc.n

                @block.gpsimd
                def _(g):
                    g.wait_ge(sp.h, sp_ye + 16)
                    cc.inc(g.collective_compute(
                        "AllGather", ALU.bypass, replica_groups=rg,
                        ins=[AP(ybx.tensor, 0, [[512, T], [1, 512]]).opt()],
                        outs=[AP(ag_y2.tensor, 0, [[512, 2 * T], [1, 512]]).opt()]), 1)

                sp_ya = sp.n

                @block.sync
                def _(sy):
                    sy.wait_ge(cc.h, cc_y + 1)
                    sp.inc(sy.dma_start(
                        out=AP(yg2.tensor, 512, [[1024, T], [1, 512]]),
                        in_=AP(ag_y2.tensor, roff["oth"], [[512, T], [1, 512]])), 16)

                with (
                    nc.sbuf_tensor(_nm("b_yTo"), [128, NC * T], BF16) as yTo,
                    nc.sbuf_tensor(_nm("b_xr"), [128, 2 * 1024], F32) as xrr,
                    nc.sbuf_tensor(_nm("b_xo"), [128, 2 * 1024], F32) as xor_,
                    nc.psum_tensor(_nm("b_po0"), [128, 1024], F32) as po0,
                    nc.psum_tensor(_nm("b_po1"), [128, 1024], F32) as po1,
                ):
                    att_mm, att_dve = mm.n, dve.n

                    ldx3 = ldx.n

                    @block.sync
                    def _(sy):
                        sy.wait_ge(mm.h, att_mm)
                        sy.wait_ge(dve.h, att_dve)
                        sy.wait_ge(sp.h, sp_ye + 2 * 16)
                        for dc in range(4):
                            ldx.inc(sy.dma_start_transpose(
                                yTo[:, dc * T:(dc + 1) * T],
                                AP(yg2.tensor, dc * 128, [[1024, T], [1, 128]])), 16)
                        sy.wait_ge(sp.h, sp_ya + 16)
                        for dc in range(4, NC):
                            ldx.inc(sy.dma_start_transpose(
                                yTo[:, dc * T:(dc + 1) * T],
                                AP(yg2.tensor, dc * 128, [[1024, T], [1, 128]])), 16)

                    mm_o = mm.n
                    dve_v = dve.n
                    ldxv = ldx.n
                    st_v = st.n
                    ret_stb[0] = st_v

                    @block.sync
                    def _(sy):
                        for tt in range(NT):
                            if tt >= 2:
                                sy.wait_ge(dve.h, dve_v + tt - 1)
                            ldx.inc(sy.dma_start(
                                out=xrr[:, (tt % 2) * 1024:(tt % 2) * 1024 + 1024],
                                in_=xin[tt * 128:(tt + 1) * 128, :]), 16)
                            if tt >= 1:
                                pt = tt - 1
                                sy.wait_ge(dve.h, dve_v + pt + 1)
                                st.inc(sy.dma_start(
                                    out=xout[pt * 128:(pt + 1) * 128, :],
                                    in_=xor_[:, (pt % 2) * 1024:(pt % 2) * 1024 + 1024]), 16)
                        pt = NT - 1
                        sy.wait_ge(dve.h, dve_v + pt + 1)
                        st.inc(sy.dma_start(
                            out=xout[pt * 128:(pt + 1) * 128, :],
                            in_=xor_[:, (pt % 2) * 1024:(pt % 2) * 1024 + 1024]), 16)

                    @block.tensor
                    def _(t):
                        t.wait_ge(ldw.h, ldw1 + 16)
                        for tt in range(NT):
                            ps = (po0, po1)[tt % 2]
                            if tt >= 2:
                                t.wait_ge(dve.h, dve_v + tt - 1)
                            for dc in range(NC):
                                if tt == 0:
                                    t.wait_ge(ldx.h, ldx3 + (dc + 1) * 16)
                                for fb in range(2):
                                    t.matmul(
                                        ps[:, fb * 512:(fb + 1) * 512],
                                        yTo[:, dc * T + tt * 128: dc * T + (tt + 1) * 128],
                                        Wo[:, dc * 1024 + fb * 512: dc * 1024 + (fb + 1) * 512],
                                        start=(dc == 0), stop=False)
                            for fb in range(2):
                                m_ = t.matmul(
                                    ps[:, fb * 512:(fb + 1) * 512],
                                    onesc[0:1, 0:128],
                                    bbias[0:1, 3 * D + fb * 512: 3 * D + (fb + 1) * 512],
                                    start=False, stop=True)
                                if fb == 1:
                                    mm.inc(m_, 1)

                    @block.vector
                    def _(v):
                        for tt in range(NT):
                            v.wait_ge(mm.h, mm_o + tt + 1)
                            v.wait_ge(ldx.h, ldxv + (tt + 1) * 16)
                            if tt >= 2:
                                v.wait_ge(st.h, st_v + (tt - 1) * 16)
                            dve.inc(v.tensor_add(
                                xor_[:, (tt % 2) * 1024:(tt % 2) * 1024 + 1024],
                                xrr[:, (tt % 2) * 1024:(tt % 2) * 1024 + 1024],
                                (po0, po1)[tt % 2][:, :]))

                _Wog.__exit__(None, None, None)
            return ret_stb[0]

        cur = x0
        stb = None
        last = max(sublayers)
        _stb = lambda s: s if lnchase else None
        if 0 in sublayers:
            nxt = yout if last == 0 else xb0
            stb = emit_attn_a(cur, nxt)
            cur = nxt
        if 1 in sublayers:
            nxt = yout if last == 1 else xb1
            stb = emit_ffn(cur, nxt, st_base=_stb(stb))
            cur = nxt
        if 2 in sublayers:
            nxt = yout if last == 2 else xb0
            stb = emit_attn_b(cur, nxt, st_base=_stb(stb))
            cur = nxt
        if 3 in sublayers:
            emit_ffn(cur, yout, st_base=_stb(stb))

        @block.sync
        def _(sy):
            sy.wait_ge(st.h, st.n)
            sy.wait_ge(sp.h, sp.n)
            sy.wait_ge(ldx.h, ldx.n)
            sy.wait_ge(ldw.h, ldw.n)

    return nc



def make_inputs(full):
    x = np.asarray(full["L0_attn_a_x"], np.float32)
    W = np.asarray(full["L0_attn_a_W"], np.float32)
    b = np.asarray(full["L0_attn_a_b"], np.float32)
    Wr = np.asarray(full["L0_ffn_W_r"], np.float32)
    Wk = np.asarray(full["L0_ffn_W_k"], np.float32)
    Wv = np.asarray(full["L0_ffn_W_v"], np.float32)
    mur = np.asarray(full["L0_ffn_mu_r"], np.float32)
    muk = np.asarray(full["L0_ffn_mu_k"], np.float32)
    Wq = np.asarray(full["L0_attn_b_Wq"], np.float32)
    Wkb = np.asarray(full["L0_attn_b_Wk"], np.float32)
    Wvb = np.asarray(full["L0_attn_b_Wv"], np.float32)
    Wo = np.asarray(full["L0_attn_b_Wo"], np.float32)
    bq = np.asarray(full["L0_attn_b_bq"], np.float32)
    bk = np.asarray(full["L0_attn_b_bk"], np.float32)
    bv = np.asarray(full["L0_attn_b_bv"], np.float32)
    bo = np.asarray(full["L0_attn_b_bo"], np.float32)

    aWqT = np.ascontiguousarray(W[0:D].T)
    aWkT = np.ascontiguousarray(W[D:2 * D].T)
    aWvT = np.ascontiguousarray(W[2 * D:3 * D].T)
    ab = np.ascontiguousarray(b.reshape(1, 3 * D))
    fWrT = np.ascontiguousarray(Wr.T)
    fWkT = np.ascontiguousarray(Wk.T)
    fWvT = np.ascontiguousarray(Wv.T)
    fmur = np.ascontiguousarray(mur.reshape(NC, 128).T)
    fmuk = np.ascontiguousarray(muk.reshape(NC, 128).T)

    pos = np.arange(S, dtype=np.float32)
    freq = np.arange(0, DH, 2, dtype=np.float32) / DH
    inv_freq = (1.0 / 10000.0 ** freq).astype(np.float32)
    ang = np.outer(pos, inv_freq)
    cosT = np.repeat(np.cos(ang).astype(np.float32), 2, axis=1)
    sinT = np.sin(ang).astype(np.float32)
    sgn = np.empty((S, DH), np.float32)
    sgn[:, 0::2] = -sinT
    sgn[:, 1::2] = sinT

    m = np.zeros((128, 4 * 512), np.float32)
    for k in range(4):
        jj = 128 * k + np.arange(128)[:, None]
        ii = np.arange(512)[None, :]
        m[:, k * 512:(k + 1) * 512] = (jj <= ii).astype(np.float32)

    ins = []
    for c in range(8):
        bsel, hh = c // 2, c % 2
        t0 = hh * T
        myh = np.arange(hh * MYH, hh * MYH + MYH)
        oth = np.arange((1 - hh) * MYH, (1 - hh) * MYH + MYH)
        perm = np.concatenate([myh, oth])
        colp = (perm[:, None] * DH + np.arange(DH)[None, :]).reshape(-1)
        d = {
            "x0": np.ascontiguousarray(x[bsel, t0:t0 + T]),
            "aWqT": aWqT, "aWkT": aWkT, "aWvT": aWvT, "ab": ab,
            "fWrT": fWrT, "fWkT": fWkT, "fWvT": fWvT,
            "fmur": fmur, "fmuk": fmuk,
            "bWq": np.ascontiguousarray(Wq[:, colp]),
            "bWk": np.ascontiguousarray(Wkb[:, colp]),
            "bWv": np.ascontiguousarray(Wvb[:, colp]),
            "bWo": np.ascontiguousarray(Wo[colp, :]),
            "bqb": np.ascontiguousarray(bq[colp].reshape(1, D)),
            "bkb": np.ascontiguousarray(bk[colp].reshape(1, D)),
            "bvb": np.ascontiguousarray(bv[colp].reshape(1, D)),
            "bob": np.ascontiguousarray(bo.reshape(1, D)),
            "ropeC": np.ascontiguousarray(cosT[t0:t0 + T]),
            "ropeS": np.ascontiguousarray(sgn[t0:t0 + T]),
            "cmask": m,
            "msel": np.full((128, 1), float(hh), np.float32),
        }
        ins.append(d)
    return ins


def kernel(**inputs):
    nc = build_program()
    in_maps = make_inputs(inputs)
    res = run_bass_kernel_spmd(nc, in_maps, core_ids=list(range(8)))
    out = np.empty((B, S, D), np.float32)
    for c in range(8):
        bsel, hh = c // 2, c % 2
        out[bsel, hh * T:(hh + 1) * T] = res.results[c]["y"]
    return out


# version 07
# speedup vs baseline: 1.0891x; 1.0398x over previous
import numpy as np
import ml_dtypes
import concourse.bass as bass
import concourse.mybir as mybir
from concourse.ap import AP
from concourse.bass_utils import run_bass_kernel_spmd

F32 = mybir.dt.float32
BF16 = mybir.dt.bfloat16
AF = mybir.ActivationFunctionType
ALU = mybir.AluOpType
AXX = mybir.AxisListType.X
BF = ml_dtypes.bfloat16

B, S, D = 4, 4096, 1024
T = S // 2
NT = T // 128
NC = D // 128
NJ = S // 128
NIB = S // 512
DH = 64
MYH = 8
EPS = 1e-5
QSC = 1.0 / 32.0
ASC = 2.0 ** -10
BSC = 0.125


class Sem:
    def __init__(self, handle):
        self.h = handle
        self.n = 0

    def inc(self, instr, k=1):
        self.n += k
        instr.then_inc(self.h, k)
        return instr


def build_program(sublayers=(0, 1, 2, 3), aphase=9, bphase=9, dbg=False, lnchase=True):
    nc = bass.Bass("TRN2", target_bir_lowering=False, debug=False,
                   detect_race_conditions=False)
    rg = [[0, 1], [2, 3], [4, 5], [6, 7]]
    _uid = [0]

    _pidc = {}

    def pid_pair(eng):
        k = str(eng.engine)
        if k not in _pidc:
            _pidc[k] = eng.partition_id() % 2
        return _pidc[k]

    def _nm(n):
        _uid[0] += 1
        return f"{n}_{_uid[0]}"

    ein = lambda n, s: nc.dram_tensor(n, s, F32, kind="ExternalInput").ap()
    x0 = ein("x0", [T, D])
    aWqT, aWkT, aWvT = ein("aWqT", [D, D]), ein("aWkT", [D, D]), ein("aWvT", [D, D])
    ab = ein("ab", [1, 3 * D])
    fWrT, fWkT, fWvT = ein("fWrT", [D, D]), ein("fWkT", [D, D]), ein("fWvT", [D, D])
    fmur, fmuk = ein("fmur", [128, NC]), ein("fmuk", [128, NC])
    bWq, bWk, bWv, bWo = ein("bWq", [D, D]), ein("bWk", [D, D]), ein("bWv", [D, D]), ein("bWo", [D, D])
    bqb, bkb, bvb, bob = ein("bqb", [1, D]), ein("bkb", [1, D]), ein("bvb", [1, D]), ein("bob", [1, D])
    ropeC, ropeS = ein("ropeC", [T, 64]), ein("ropeS", [T, 64])
    cmask = ein("cmask", [128, 4 * 512])
    msel = ein("msel", [128, 1])
    yout = nc.dram_tensor("y", [T, D], F32, kind="ExternalOutput").ap()
    if dbg:
        dqkv = nc.dram_tensor("dqkv", [3 * S, 512], BF16, kind="ExternalOutput").ap()
        dyt = nc.dram_tensor("dyt", [S, 512], BF16, kind="ExternalOutput").ap()
        dhb = nc.dram_tensor("dhb", [T, D], BF16, kind="ExternalOutput").ap()
        dqb = nc.dram_tensor("dqb", [3 * T, D], BF16, kind="ExternalOutput").ap()
        dzt = nc.dram_tensor("dzt", [S, 8], F32, kind="ExternalOutput").ap()

    dram = lambda n, s, dt=BF16: nc.dram_tensor(n, s, dt, kind="Internal").ap()
    xb0 = dram("xb0", [T, D], F32)
    xb1 = dram("xb1", [T, D], F32)
    h_b = dram("h_b", [T, D])
    akv = dram("akv", [2 * D, T])
    ag_k = dram("ag_k", [2 * D, T])
    ag_v2 = dram("ag_v2", [2 * T, D])
    azb = dram("azb", [128, NJ], F32)
    ag_az = dram("ag_az", [256, NJ], F32)
    a_spill = dram("a_spill", [NJ * 128, T])
    bnd_in = dram("bnd_in", [1, D])
    ag_bnd = dram("ag_bnd", [2, D])
    qkvo = dram("qkvo", [3 * T, 512])
    bqkv = dram("bqkv", [3 * T, 512])
    ag_bqkv = dram("ag_bqkv", [6 * T, 512])
    qkvg = dram("qkvg", [3 * S, 512])
    yb_tok = dram("yb_tok", [S, 512])
    ybx = dram("ybx", [T, 512])
    ag_y2 = dram("ag_y2", [2 * T, 512])
    yg2 = dram("yg2", [T, D])

    from contextlib import ExitStack
    with ExitStack() as _es:
        ones = _es.enter_context(nc.sbuf_tensor("ones", [1, 512], BF16))
        onesc = _es.enter_context(nc.sbuf_tensor("onesc", [1, 128], BF16))
        mur = _es.enter_context(nc.sbuf_tensor("mur", [128, NC], F32))
        muk = _es.enter_context(nc.sbuf_tensor("muk", [128, NC], F32))
        masks = _es.enter_context(nc.sbuf_tensor("masks", [128, 4 * 512], BF16))
        ropeCs = _es.enter_context(nc.sbuf_tensor("ropeCs", [128, NT * 64], F32))
        ropeSs = _es.enter_context(nc.sbuf_tensor("ropeSs", [128, NT * 64], F32))
        stats = _es.enter_context(nc.sbuf_tensor("stats", [128, 96], F32))
        ident = _es.enter_context(nc.sbuf_tensor("ident", [128, 128], BF16))
        msl = _es.enter_context(nc.sbuf_tensor("msl", [128, 1], F32))
        zst = _es.enter_context(nc.sbuf_tensor("zst", [128, 3 * NJ], F32))
        _ldw = _es.enter_context(nc.semaphore("s_ldw"))
        _ldx = _es.enter_context(nc.semaphore("s_ldx"))
        _mm = _es.enter_context(nc.semaphore("s_mm"))
        _act = _es.enter_context(nc.semaphore("s_act"))
        _dve = _es.enter_context(nc.semaphore("s_dve"))
        _gp = _es.enter_context(nc.semaphore("s_gp"))
        _cc = _es.enter_context(nc.semaphore("s_cc"))
        _sp = _es.enter_context(nc.semaphore("s_sp"))
        _st = _es.enter_context(nc.semaphore("s_st"))
        _id = _es.enter_context(nc.semaphore("s_id"))
        block = _es.enter_context(nc.Block())

        ldw, ldx, mm, act, dve = Sem(_ldw), Sem(_ldx), Sem(_mm), Sem(_act), Sem(_dve)
        gp, cc, sp, st, idm = Sem(_gp), Sem(_cc), Sem(_sp), Sem(_st), Sem(_id)

        @block.vector
        def _(v):
            v.memset(ones[:, :], 1.0)
            v.memset(onesc[:, :], 1.0)

        @block.sync
        def _(sy):
            sp.inc(sy.dma_start(out=msl[:, :], in_=msel[:, :]), 16)
            sp.inc(sy.dma_start(out=mur[:, :], in_=fmur[:, :]), 16)
            sp.inc(sy.dma_start(out=muk[:, :], in_=fmuk[:, :]), 16)
            sp.inc(sy.dma_start(
                out=ropeCs[:, :].rearrange("p (n c) -> p n c", c=64),
                in_=ropeC.rearrange("(n p) c -> p n c", p=128)), 16)
            sp.inc(sy.dma_start(
                out=ropeSs[:, :].rearrange("p (n c) -> p n c", c=64),
                in_=ropeS.rearrange("(n p) c -> p n c", p=128)), 16)

        @block.gpsimd
        def _(g):
            gp.inc(g.dma_start(out=masks[:, :], in_=cmask[:, :]), 16)
            g.memset(ident[:, :], 1.0)
            idm.inc(g.affine_select(
                ident[:, :], ident[:, :],
                pattern=[[-1, 128]], compare_op=ALU.is_equal,
                fill=0.0, base=0, channel_multiplier=1), 1)


        def emit_ln(xin, xf, hsc, ev, nslots, sw, st_base=None, fence=()):
            ldx0 = ldx.n

            st_all = st.n

            @block.sync
            def _(sy):
                for s_, c_ in fence:
                    sy.wait_ge(s_.h, c_)
                if st_base is None:
                    sy.wait_ge(st.h, st_all)
                else:
                    sy.wait_ge(st.h, st_base + NT * 16)
                ldx.inc(sy.dma_start(
                    out=xf[:, :].rearrange("p (n d) -> p n d", d=1024),
                    in_=xin.rearrange("(n p) d -> p n d", p=128)), 16)

            dve0 = dve.n

            @block.vector
            def _(v):
                v.wait_ge(ldx.h, ldx0 + 16)
                for tt in range(NT):
                    xt = xf[:, tt * 1024:(tt + 1) * 1024]
                    v.reduce_sum(stats[:, tt:tt + 1], xt, axis=AXX)
                v.tensor_copy(stats[:, 90:91], stats[:, 0:1])
                dve.inc(v.tensor_scalar(
                    stats[:, 16:32], stats[:, 0:16], -1.0 / D, None, ALU.mult))

            act0 = act.n

            @block.scalar
            def _(s):
                s.wait_ge(dve.h, dve0 + 1)
                for tt in range(NT):
                    xt = xf[:, tt * 1024:(tt + 1) * 1024]
                    s.activation(
                        hsc[:, :], xt, AF.Square,
                        bias=stats[:, 16 + tt:17 + tt],
                        accum_out=stats[:, 32 + tt:33 + tt])
                act.inc(s.activation(hsc[:, 0:1], hsc[:, 0:1], AF.Square), 1)

            dve1 = dve.n

            @block.vector
            def _(v):
                v.wait_ge(act.h, act0 + 1)
                dve.inc(v.tensor_scalar(
                    stats[:, 48:64], stats[:, 32:48], 1.0 / D, EPS,
                    ALU.mult, ALU.add))

            act1 = act.n

            @block.scalar
            def _(s):
                s.wait_ge(dve.h, dve1 + 1)
                act.inc(s.activation(stats[:, 80:96], stats[:, 48:64], AF.Sqrt), 1)

            dve2 = dve.n

            @block.vector
            def _(v):
                v.wait_ge(act.h, act1 + 1)
                v.reciprocal(stats[:, 48:64], stats[:, 80:96])
                v.tensor_copy(stats[:, 90:91], stats[:, 48:49])
                v.tensor_mul(stats[:, 64:80], stats[:, 16:32], stats[:, 48:64])
                dve.inc(v.tensor_copy(stats[:, 90:91], stats[:, 64:65]))

            act2 = act.n
            sp0 = sp.n

            @block.scalar
            def _(s):
                s.wait_ge(dve.h, dve2 + 1)
                for tt in range(NT):
                    if tt >= nslots:
                        s.wait_ge(sp.h, sp0 + (tt - nslots + 1) * 16)
                    xt = xf[:, tt * 1024:(tt + 1) * 1024]
                    act.inc(s.activation(
                        ev[:, (tt % nslots) * sw:(tt % nslots) * sw + 1024],
                        xt, AF.Identity,
                        scale=stats[:, 48 + tt:49 + tt],
                        bias=stats[:, 64 + tt:65 + tt]), 1)

            @block.sync
            def _(sy):
                for tt in range(NT):
                    sy.wait_ge(act.h, act2 + tt + 1)
                    sp.inc(sy.dma_start(
                        out=h_b[tt * 128:(tt + 1) * 128, :],
                        in_=ev[:, (tt % nslots) * sw:(tt % nslots) * sw + 1024]), 16)
            return sp0

        def emit_attn_a(xin, xout):
            AP1, AP2, AP3 = aphase >= 2, aphase >= 3, aphase >= 4
            ret_st = [None]
            with (
                nc.sbuf_tensor(_nm("a_qT"), [128, NC * T], BF16) as qT,
                nc.sbuf_tensor(_nm("a_ev"), [128, 4 * 2048], BF16) as ev,
            ):
                _Wg = nc.sbuf_tensor(_nm("a_W"), [128, 3 * NC * 1024], BF16)
                _bg = nc.sbuf_tensor(_nm("a_b"), [1, 3 * D], BF16)
                W = _Wg.__enter__()
                abias = _bg.__enter__()
                ldw0 = ldw.n

                @block.gpsimd
                def _(g):
                    for wi, w in enumerate((aWkT, aWvT, aWqT)):
                        ldw.inc(g.dma_start(
                            out=W[:, wi * NC * 1024:(wi + 1) * NC * 1024]
                                .rearrange("p (c f) -> p c f", f=1024),
                            in_=w.rearrange("(c p) f -> p c f", p=128)), 16)
                    ldw.inc(g.dma_start(out=abias[:, :], in_=ab[:, :]), 16)

                with (
                    nc.sbuf_tensor(_nm("a_xf"), [128, NT * 1024], F32) as xf,
                    nc.sbuf_tensor(_nm("a_sc"), [128, 1024], F32) as hsc,
                ):
                    sp0 = emit_ln(xin, xf, hsc, ev, 4, 2048)

                if not AP1:
                    return
                with (
                    nc.sbuf_tensor(_nm("a_hT"), [128, NC * T], BF16) as hT,
                    nc.psum_tensor(_nm("a_pk0"), [128, 2048], F32) as pk0,
                    nc.psum_tensor(_nm("a_pk1"), [128, 2048], F32) as pk1,
                ):
                    ldx1 = ldx.n
                    sp_h_done = sp.n

                    @block.sync
                    def _(sy):
                        sy.wait_ge(sp.h, sp0 + NT * 16)
                        for dc in range(NC):
                            ldx.inc(sy.dma_start_transpose(
                                hT[:, dc * T:(dc + 1) * T],
                                h_b[:, dc * 128:(dc + 1) * 128]), 16)

                    def wsl(wi, dc, f0, fn):
                        base = wi * 8192 + dc * 1024 + f0
                        return W[:, base:base + fn]

                    mm0 = mm.n
                    act3 = act.n
                    sp1 = sp.n

                    @block.tensor
                    def _(t):
                        t.wait_ge(ldw.h, ldw0 + 64)
                        t.wait_ge(ldx.h, ldx1 + NC * 16)
                        for fc in range(NC):
                            ps = (pk0, pk1)[fc % 2]
                            if fc >= 2:
                                t.wait_ge(act.h, act3 + fc - 1)
                            for dc in range(NC):
                                for ib in range(4):
                                    t.matmul(ps[:, ib * 512:(ib + 1) * 512],
                                             wsl(0, dc, fc * 128, 128),
                                             hT[:, dc * T + ib * 512: dc * T + (ib + 1) * 512],
                                             start=(dc == 0), stop=False)
                            for ib in range(4):
                                last = ib == 3
                                m_ = t.matmul(
                                    ps[:, ib * 512:(ib + 1) * 512],
                                    abias[0:1, D + fc * 128: D + (fc + 1) * 128],
                                    ones[0:1, 0:512], start=False, stop=True)
                                if last:
                                    mm.inc(m_, 1)

                    @block.scalar
                    def _(s):
                        for fc in range(NC):
                            s.wait_ge(mm.h, mm0 + fc + 1)
                            if fc == 0:
                                s.wait_ge(sp.h, sp1)
                            if fc >= 4:
                                s.wait_ge(sp.h, sp1 + (fc - 3) * 16)
                            act.inc(s.activation(
                                ev[:, (fc % 4) * 2048:(fc % 4) * 2048 + 2048],
                                (pk0, pk1)[fc % 2][:, :], AF.Identity), 1)

                    @block.sync
                    def _(sy):
                        for fc in range(NC):
                            sy.wait_ge(act.h, act3 + fc + 1)
                            sp.inc(sy.dma_start(
                                out=akv[fc * 128:(fc + 1) * 128, :],
                                in_=ev[:, (fc % 4) * 2048:(fc % 4) * 2048 + 2048]), 16)

                    cc0 = cc.n
                    k_done = sp.n

                    @block.gpsimd
                    def _(g):
                        g.wait_ge(sp.h, k_done)
                        cc.inc(g.collective_compute(
                            "AllGather", ALU.bypass, replica_groups=rg,
                            ins=[akv[0:D, :].opt()], outs=[ag_k[:, :].opt()]), 1)

                    mm1 = mm.n
                    act4 = act.n
                    sp2 = sp.n

                    @block.tensor
                    def _(t):
                        for tt in range(NT):
                            ps = (pk0, pk1)[tt % 2]
                            pv = ps[:, 0:1024]
                            if tt >= 2:
                                t.wait_ge(act.h, act4 + tt - 1)
                            for dc in range(NC):
                                for fb in range(2):
                                    t.matmul(pv[:, fb * 512:(fb + 1) * 512],
                                             hT[:, dc * T + tt * 128: dc * T + (tt + 1) * 128],
                                             wsl(1, dc, fb * 512, 512),
                                             start=(dc == 0), stop=False)
                            for fb in range(2):
                                m_ = t.matmul(
                                    pv[:, fb * 512:(fb + 1) * 512],
                                    onesc[0:1, 0:128],
                                    abias[0:1, 2 * D + fb * 512: 2 * D + (fb + 1) * 512],
                                    start=False, stop=True)
                                if fb == 1:
                                    mm.inc(m_, 1)

                    @block.scalar
                    def _(s):
                        for tt in range(NT):
                            s.wait_ge(mm.h, mm1 + tt + 1)
                            if tt == 0:
                                s.wait_ge(sp.h, sp2)
                            if tt >= 4:
                                s.wait_ge(sp.h, sp2 + (tt - 3) * 16)
                            act.inc(s.activation(
                                ev[:, (tt % 4) * 2048:(tt % 4) * 2048 + 1024],
                                (pk0, pk1)[tt % 2][:, 0:1024], AF.Identity), 1)

                    @block.sync
                    def _(sy):
                        for tt in range(NT):
                            sy.wait_ge(act.h, act4 + tt + 1)
                            sp.inc(sy.dma_start(
                                out=AP(akv.tensor, D * T + tt * 128 * D,
                                       [[D, 128], [1, D]]),
                                in_=ev[:, (tt % 4) * 2048:(tt % 4) * 2048 + 1024]), 16)

                    v_done = sp.n

                    @block.gpsimd
                    def _(g):
                        g.wait_ge(sp.h, v_done)
                        cc.inc(g.collective_compute(
                            "AllGather", ALU.bypass, replica_groups=rg,
                            ins=[AP(akv.tensor, D * T, [[D, T], [1, D]]).opt()],
                            outs=[ag_v2[:, :].opt()]), 1)

                    mm2 = mm.n
                    act5 = act.n

                    @block.tensor
                    def _(t):
                        for fc in range(NC):
                            ps = (pk0, pk1)[fc % 2]
                            if fc >= 2:
                                t.wait_ge(act.h, act5 + fc - 1)
                            for dc in range(NC):
                                for ib in range(4):
                                    t.matmul(ps[:, ib * 512:(ib + 1) * 512],
                                             wsl(2, dc, fc * 128, 128),
                                             hT[:, dc * T + ib * 512: dc * T + (ib + 1) * 512],
                                             start=(dc == 0), stop=False)
                            for ib in range(4):
                                m_ = t.matmul(
                                    ps[:, ib * 512:(ib + 1) * 512],
                                    abias[0:1, fc * 128:(fc + 1) * 128],
                                    ones[0:1, 0:512], start=False, stop=True)
                                if ib == 3:
                                    mm.inc(m_, 1)

                    @block.scalar
                    def _(s):
                        for fc in range(NC):
                            s.wait_ge(mm.h, mm2 + fc + 1)
                            act.inc(s.activation(
                                qT[:, fc * T:(fc + 1) * T],
                                (pk0, pk1)[fc % 2][:, :], AF.Identity, scale=ASC), 1)

                _bg.__exit__(None, None, None)
                _Wg.__exit__(None, None, None)

                if not AP2:
                    return ret_st[0]
                with nc.sbuf_tensor(_nm("a_V"), [128, NJ * 1024], BF16) as Vsb:
                    with (
                        nc.sbuf_tensor(_nm("a_kT"), [128, NC * S], BF16) as ktT,
                        nc.sbuf_tensor(_nm("a_A"), [128, 2 * 2048], BF16) as Asl,
                        nc.psum_tensor(_nm("a_ps0"), [128, 2048], F32) as ps0,
                        nc.psum_tensor(_nm("a_ps1"), [128, 2048], F32) as ps1,
                    ):
                        ldx2 = ldx.n
                        qdone = act.n

                        @block.sync
                        def _(sy):
                            sy.wait_ge(act.h, qdone)
                            sy.wait_ge(cc.h, cc0 + 1)
                            for r in range(2):
                                for dc in range(NC):
                                    ldx.inc(sy.dma_start(
                                        out=ktT[:, dc * S + r * T: dc * S + r * T + T],
                                        in_=ag_k[D * r + dc * 128: D * r + (dc + 1) * 128, :]), 16)

                        mm3 = mm.n
                        act6 = act.n
                        sp3 = sp.n

                        @block.tensor
                        def _(t):
                            t.wait_ge(act.h, qdone)
                            t.wait_ge(ldx.h, ldx2 + 256)
                            for jt in range(NJ):
                                ps = (ps0, ps1)[jt % 2]
                                if jt >= 2:
                                    t.wait_ge(act.h, act6 + jt - 1)
                                for dc in range(NC):
                                    for ib in range(4):
                                        m_ = t.matmul(
                                            ps[:, ib * 512:(ib + 1) * 512],
                                            ktT[:, dc * S + jt * 128: dc * S + (jt + 1) * 128],
                                            qT[:, dc * T + ib * 512: dc * T + (ib + 1) * 512],
                                            start=(dc == 0), stop=(dc == NC - 1))
                                        if dc == NC - 1 and ib == 3:
                                            mm.inc(m_, 1)

                        @block.scalar
                        def _(s):
                            for jt in range(NJ):
                                s.wait_ge(mm.h, mm3 + jt + 1)
                                if jt >= 2:
                                    s.wait_ge(sp.h, sp3 + (jt - 1) * 16)
                                act.inc(s.activation(
                                    Asl[:, (jt % 2) * 2048:(jt % 2) * 2048 + 2048],
                                    (ps0, ps1)[jt % 2][:, :], AF.Exp,
                                    accum_out=zst[:, jt:jt + 1]), 1)
                            act.inc(s.activation(stats[:, 80:81], stats[:, 80:81], AF.Square), 1)

                        @block.sync
                        def _(sy):
                            for jt in range(NJ):
                                sy.wait_ge(act.h, act6 + jt + 1)
                                sp.inc(sy.dma_start(
                                    out=a_spill[jt * 128:(jt + 1) * 128, :],
                                    in_=Asl[:, (jt % 2) * 2048:(jt % 2) * 2048 + 2048]), 16)
                            sy.wait_ge(act.h, act6 + NJ + 1)
                            sp.inc(sy.dma_start(out=azb[:, :], in_=zst[:, 0:NJ]), 16)

                        cc1 = cc.n
                        z_done = sp.n

                        @block.gpsimd
                        def _(g):
                            g.wait_ge(sp.h, z_done)
                            cc.inc(g.collective_compute(
                                "AllGather", ALU.bypass, replica_groups=rg,
                                ins=[azb[:, :].opt()], outs=[ag_az[:, :].opt()]), 1)

                        ldx3 = ldx.n

                        @block.sync
                        def _(sy):
                            sy.wait_ge(cc.h, cc1 + 1)
                            ldx.inc(sy.dma_start(
                                out=zst[:, NJ:3 * NJ].rearrange("p (r j) -> p r j", r=2),
                                in_=ag_az.rearrange("(r p) j -> p r j", p=128)), 16)

                        dve3 = dve.n

                        @block.vector
                        def _(v):
                            v.wait_ge(ldx.h, ldx3 + 16)
                            v.tensor_add(zst[:, 0:NJ], zst[:, NJ:2 * NJ], zst[:, 2 * NJ:3 * NJ])
                            dve.inc(v.reciprocal(zst[:, 0:NJ], zst[:, 0:NJ]))

                    if not AP3:
                        return
                    _ngrp = 1 if aphase == 5 else (2 if aphase == 6 else (3 if aphase == 7 else 4))
                    with (
                        nc.sbuf_tensor(_nm("a_Ar"), [128, NJ * 512], BF16) as Ar,
                        nc.sbuf_tensor(_nm("a_xr"), [128, 4 * 1024], F32) as xr,
                        nc.sbuf_tensor(_nm("a_xo"), [128, 4 * 1024], F32) as xo,
                        nc.psum_tensor(_nm("a_po0"), [128, 1024], F32) as po0,
                        nc.psum_tensor(_nm("a_po1"), [128, 1024], F32) as po1,
                        nc.psum_tensor(_nm("a_po2"), [128, 1024], F32) as po2,
                        nc.psum_tensor(_nm("a_po3"), [128, 1024], F32) as po3,
                    ):
                        pos = (po0, po1, po2, po3)
                        ldx4 = ldx.n

                        @block.sync
                        def _(sy):
                            sy.wait_ge(act.h, qdone)
                            sy.wait_ge(cc.h, cc0 + 2)
                            for half in range(2):
                                ldx.inc(sy.dma_start(
                                    out=Vsb[:, half * 16 * 1024:(half + 1) * 16 * 1024]
                                        .rearrange("p (j d) -> p j d", d=1024),
                                    in_=AP(ag_v2.tensor, half * T * D,
                                           [[1024, 128], [128 * 1024, 16], [1, 1024]])), 16)

                        dve4 = dve.n

                        @block.vector
                        def _(v):
                            v.wait_ge(dve.h, dve3 + 1)
                            v.wait_ge(ldx.h, ldx4 + 32)
                            for jt in range(NJ):
                                dve.inc(v.tensor_scalar_mul(
                                    Vsb[:, jt * 1024:(jt + 1) * 1024],
                                    Vsb[:, jt * 1024:(jt + 1) * 1024],
                                    zst[:, jt:jt + 1]))

                        if aphase == 4:
                            return
                        ldx5 = ldx.n
                        mm4 = mm.n
                        dve5 = dve.n

                        st0 = st.n
                        ret_st[0] = st0

                        @block.sync
                        def _(sy):
                            for grp in range(_ngrp):
                                if grp >= 1:
                                    sy.wait_ge(mm.h, mm4 + grp * 4)
                                for q in range(4):
                                    ldx.inc(sy.dma_start(
                                        out=Ar[:, q * 8 * 512:(q + 1) * 8 * 512]
                                            .rearrange("p (j i) -> p j i", i=512),
                                        in_=AP(a_spill.tensor,
                                               q * 8 * 128 * T + grp * 512,
                                               [[T, 128], [128 * T, 8], [1, 512]])), 16)
                                if grp >= 1:
                                    sy.wait_ge(dve.h, dve5 + grp * 4)
                                for it in range(4):
                                    tt = grp * 4 + it
                                    ldx.inc(sy.dma_start(
                                        out=xr[:, it * 1024:(it + 1) * 1024],
                                        in_=xin[tt * 128:(tt + 1) * 128, :]), 16)
                                if grp >= 1:
                                    pg = grp - 1
                                    for it in range(4):
                                        tt = pg * 4 + it
                                        sy.wait_ge(dve.h, dve5 + pg * 4 + it + 1)
                                        st.inc(sy.dma_start(
                                            out=xout[tt * 128:(tt + 1) * 128, :],
                                            in_=xo[:, it * 1024:(it + 1) * 1024]), 16)
                            pg = _ngrp - 1
                            for it in range(4):
                                tt = pg * 4 + it
                                sy.wait_ge(dve.h, dve5 + pg * 4 + it + 1)
                                st.inc(sy.dma_start(
                                    out=xout[tt * 128:(tt + 1) * 128, :],
                                    in_=xo[:, it * 1024:(it + 1) * 1024]), 16)

                        @block.tensor
                        def _(t):
                            t.wait_ge(dve.h, dve4 + NJ)
                            for grp in range(_ngrp):
                                t.wait_ge(ldx.h, ldx5 + grp * 8 * 16 + 4 * 16)
                                if grp >= 1:
                                    t.wait_ge(dve.h, dve5 + grp * 4)
                                for jt in range(NJ):
                                    for it in range(4):
                                        a_sl = Ar[:, jt * 512 + it * 128:
                                                  jt * 512 + (it + 1) * 128]
                                        for fb in range(2):
                                            m_ = t.matmul(
                                                pos[it][:, fb * 512:(fb + 1) * 512],
                                                a_sl,
                                                Vsb[:, jt * 1024 + fb * 512: jt * 1024 + (fb + 1) * 512],
                                                start=(jt == 0), stop=(jt == NJ - 1))
                                            if jt == NJ - 1 and fb == 1:
                                                mm.inc(m_, 1)

                        @block.vector
                        def _(v):
                            for grp in range(_ngrp):
                                if grp >= 1:
                                    v.wait_ge(st.h, st0 + grp * 4 * 16)
                                for it in range(4):
                                    v.wait_ge(mm.h, mm4 + grp * 4 + it + 1)
                                    v.wait_ge(ldx.h, ldx5 + (grp + 1) * 8 * 16)
                                    dve.inc(v.tensor_add(
                                        xo[:, it * 1024:(it + 1) * 1024],
                                        xr[:, it * 1024:(it + 1) * 1024],
                                        pos[it][:, :]))

            return ret_st[0]

        def emit_ffn(xin, xout, st_base=None):
            ret_st = [None]
            NH2 = NT // 2
            TH = NH2 * 128
            with (
                nc.sbuf_tensor(_nm("f_W"), [128, 3 * NC * 1024], BF16) as W,
                nc.sbuf_tensor(_nm("f_prevA"), [8, 128], BF16) as prevA,
                nc.sbuf_tensor(_nm("f_prev"), [128, 16], BF16) as prevT,
                nc.psum_tensor(_nm("f_pb"), [128, 8], BF16) as pbP,
                nc.psum_tensor(_nm("f_pp0"), [128, 1024], F32) as pp0,
                nc.psum_tensor(_nm("f_pp1"), [128, 1024], F32) as pp1,
            ):
                f_mm, f_dve, f_act, f_st, f_sp = mm.n, dve.n, act.n, st.n, sp.n
                ldw0 = ldw.n

                @block.gpsimd
                def _(g):
                    g.wait_ge(mm.h, f_mm)
                    g.wait_ge(dve.h, f_dve)
                    g.wait_ge(act.h, f_act)
                    g.wait_ge(st.h, f_st)
                    for wi, w in enumerate((fWrT, fWkT, fWvT)):
                        ldw.inc(g.dma_start(
                            out=W[:, wi * NC * 1024:(wi + 1) * NC * 1024]
                                .rearrange("p (c f) -> p c f", f=1024),
                            in_=w.rearrange("(c p) f -> p c f", p=128)), 16)

                with (
                    nc.sbuf_tensor(_nm("f_xf"), [128, NT * 1024], F32) as xf,
                    nc.sbuf_tensor(_nm("f_sc"), [128, 1024], F32) as hsc,
                    nc.sbuf_tensor(_nm("f_ev"), [128, 2 * 1024], BF16) as ev2,
                ):
                    sp0 = emit_ln(xin, xf, hsc, ev2, 2, 1024, st_base=st_base,
                                  fence=((mm, f_mm), (dve, f_dve), (act, f_act)))

                sp_b = sp.n

                @block.sync
                def _(sy):
                    sy.wait_ge(sp.h, sp0 + NT * 16)
                    sp.inc(sy.dma_start(out=bnd_in[0:1, :], in_=h_b[T - 1:T, :]), 16)

                cc_b = cc.n

                @block.gpsimd
                def _(g):
                    g.wait_ge(sp.h, sp_b + 16)
                    cc.inc(g.collective_compute(
                        "AllGather", ALU.bypass, replica_groups=rg,
                        ins=[bnd_in[:, :].opt()], outs=[ag_bnd[:, :].opt()]), 1)

                ldxb = ldx.n

                @block.sync
                def _(sy):
                    sy.wait_ge(cc.h, cc_b + 1)
                    ldx.inc(sy.dma_start(
                        out=prevA[:, :],
                        in_=AP(ag_bnd.tensor, 0, [[128, 8], [1, 128]])), 16)

                mm_b = mm.n

                @block.tensor
                def _(t):
                    t.wait_ge(ldx.h, ldxb + 16)
                    t.wait_ge(idm.h, idm.n)
                    mm.inc(t.transpose(pbP[:, :], prevA[:, :], ident[0:8, 0:8]), 1)

                dve_b = dve.n

                @block.vector
                def _(v):
                    v.wait_ge(mm.h, mm_b + 1)
                    v.wait_ge(sp.h, 16)
                    dve.inc(v.tensor_scalar_mul(prevT[:, 0:8], pbP[:, :], msl[:, 0:1]))

                for hh in range(2):
                    with (
                        nc.sbuf_tensor(_nm("f_hT"), [128, NC * 1024], BF16) as hT,
                        nc.sbuf_tensor(_nm("f_xrT"), [128, NC * 1024], BF16) as xrT,
                        nc.sbuf_tensor(_nm("f_xkT"), [128, NC * 1024], BF16) as xkT,
                        nc.sbuf_tensor(_nm("f_r"), [128, NH2 * 1024], BF16) as rti,
                        nc.sbuf_tensor(_nm("f_ka"), [128, NC * 1024], BF16) as kaT,
                        nc.sbuf_tensor(_nm("f_rs"), [128, 2 * 1024], F32) as rsc,
                        nc.sbuf_tensor(_nm("f_tmp"), [128, 1024], BF16) as tmpc,
                        nc.sbuf_tensor(_nm("f_t1"), [128, 1024], F32) as t1,
                        nc.sbuf_tensor(_nm("f_xr"), [128, 2 * 1024], F32) as xrr,
                        nc.sbuf_tensor(_nm("f_xo"), [128, 2 * 1024], F32) as xor_,
                    ):
                        pcol = 0 if hh == 0 else 8
                        ldxh = ldx.n
                        if hh == 1:
                            h0_mm, h0_dve, h0_act, h0_st = mm.n, dve.n, act.n, st.n
                        else:
                            h0_mm = h0_dve = h0_act = h0_st = 0

                        @block.sync
                        def _(sy):
                            if hh == 0:
                                sy.wait_ge(sp.h, sp0 + NT * 16)
                            else:
                                sy.wait_ge(dve.h, h0_dve)
                                sy.wait_ge(act.h, h0_act)
                            for dc in range(NC):
                                ldx.inc(sy.dma_start_transpose(
                                    hT[:, dc * 1024:(dc + 1) * 1024],
                                    h_b[hh * TH:(hh + 1) * TH,
                                        dc * 128:(dc + 1) * 128]), 16)

                        dve_m = dve.n

                        @block.vector
                        def _(v):
                            v.wait_ge(dve.h, dve_b + 1)
                            if hh == 1:
                                v.wait_ge(dve.h, dve_sv)
                                v.wait_ge(mm.h, h0_mm)
                            for dc in range(NC):
                                v.wait_ge(ldx.h, ldxh + (dc + 1) * 16)
                                hc = hT[:, dc * 1024:(dc + 1) * 1024]
                                pv = prevT[:, pcol + dc:pcol + dc + 1]
                                v.tensor_sub(tmpc[:, 1:1024], hc[:, 1:1024], hc[:, 0:1023])
                                v.tensor_sub(tmpc[:, 0:1], hc[:, 0:1], pv)
                                v.scalar_tensor_tensor(
                                    xrT[:, dc * 1024 + 1:(dc + 1) * 1024],
                                    tmpc[:, 1:1024], mur[:, dc:dc + 1], hc[:, 0:1023],
                                    ALU.mult, ALU.add)
                                v.scalar_tensor_tensor(
                                    xrT[:, dc * 1024:dc * 1024 + 1],
                                    tmpc[:, 0:1], mur[:, dc:dc + 1], pv,
                                    ALU.mult, ALU.add)
                                v.scalar_tensor_tensor(
                                    xkT[:, dc * 1024 + 1:(dc + 1) * 1024],
                                    tmpc[:, 1:1024], muk[:, dc:dc + 1], hc[:, 0:1023],
                                    ALU.mult, ALU.add)
                                dve.inc(v.scalar_tensor_tensor(
                                    xkT[:, dc * 1024:dc * 1024 + 1],
                                    tmpc[:, 0:1], muk[:, dc:dc + 1], pv,
                                    ALU.mult, ALU.add))
                            if hh == 0:
                                dve.inc(v.tensor_copy(
                                    prevT[:, 8:16],
                                    hT[:, :].rearrange("p (c t) -> p c t", t=1024)[:, :, 1023]))

                        if hh == 0:
                            dve_sv = dve.n

                        mm_r = mm.n
                        act_r = act.n

                        @block.tensor
                        def _(t):
                            t.wait_ge(ldw.h, ldw0 + 16)
                            for tt in range(NH2):
                                ps = (pp0, pp1)[tt % 2]
                                if tt == 0:
                                    t.wait_ge(dve.h, dve_m + NC)
                                if tt >= 2:
                                    t.wait_ge(act.h, act_r + tt - 1)
                                for dc in range(NC):
                                    for fb in range(2):
                                        m_ = t.matmul(
                                            ps[:, fb * 512:(fb + 1) * 512],
                                            xrT[:, dc * 1024 + tt * 128: dc * 1024 + (tt + 1) * 128],
                                            W[:, dc * 1024 + fb * 512: dc * 1024 + (fb + 1) * 512],
                                            start=(dc == 0), stop=(dc == NC - 1))
                                        if dc == NC - 1 and fb == 1:
                                            mm.inc(m_, 1)

                        @block.scalar
                        def _(s):
                            if hh == 1:
                                s.wait_ge(dve.h, h0_dve)
                            for tt in range(NH2):
                                s.wait_ge(mm.h, mm_r + tt + 1)
                                act.inc(s.activation(
                                    rti[:, tt * 1024:(tt + 1) * 1024],
                                    (pp0, pp1)[tt % 2][:, :], AF.Sigmoid), 1)

                        mm_k = mm.n
                        act_k = act.n
                        dve_k = dve.n

                        @block.tensor
                        def _(t):
                            t.wait_ge(ldw.h, ldw0 + 32)
                            for fc in range(NC):
                                ps = (pp0, pp1)[fc % 2]
                                if fc == 0:
                                    t.wait_ge(act.h, act_r + NH2)
                                if fc >= 2:
                                    t.wait_ge(act.h, act_k + fc - 1)
                                for dc in range(NC):
                                    for ib in range(2):
                                        m_ = t.matmul(
                                            ps[:, ib * 512:(ib + 1) * 512],
                                            W[:, 8192 + dc * 1024 + fc * 128: 8192 + dc * 1024 + fc * 128 + 128],
                                            xkT[:, dc * 1024 + ib * 512: dc * 1024 + (ib + 1) * 512],
                                            start=(dc == 0), stop=(dc == NC - 1))
                                        if dc == NC - 1 and ib == 1:
                                            mm.inc(m_, 1)

                        @block.scalar
                        def _(s):
                            for fc in range(NC):
                                s.wait_ge(mm.h, mm_k + fc + 1)
                                if fc >= 2:
                                    s.wait_ge(dve.h, dve_k + fc - 1)
                                act.inc(s.activation(
                                    rsc[:, (fc % 2) * 1024:(fc % 2) * 1024 + 1024],
                                    (pp0, pp1)[fc % 2][:, :], AF.Relu), 1)

                        @block.vector
                        def _(v):
                            for fc in range(NC):
                                v.wait_ge(act.h, act_k + fc + 1)
                                dve.inc(v.tensor_mul(
                                    kaT[:, fc * 1024:(fc + 1) * 1024],
                                    rsc[:, (fc % 2) * 1024:(fc % 2) * 1024 + 1024],
                                    rsc[:, (fc % 2) * 1024:(fc % 2) * 1024 + 1024]))

                        mm_v = mm.n
                        dve_v = dve.n
                        ldxv = ldx.n
                        st_v = st.n
                        if hh == 0:
                            ret_st[0] = st_v

                        @block.sync
                        def _(sy):
                            if hh == 1:
                                sy.wait_ge(dve.h, h0_dve)
                            for tt in range(NH2):
                                if tt >= 2:
                                    sy.wait_ge(dve.h, dve_v + tt - 1)
                                ldx.inc(sy.dma_start(
                                    out=xrr[:, (tt % 2) * 1024:(tt % 2) * 1024 + 1024],
                                    in_=xin[hh * TH + tt * 128: hh * TH + (tt + 1) * 128, :]), 16)
                                if tt >= 1:
                                    pt = tt - 1
                                    sy.wait_ge(dve.h, dve_v + pt + 1)
                                    st.inc(sy.dma_start(
                                        out=xout[hh * TH + pt * 128: hh * TH + (pt + 1) * 128, :],
                                        in_=xor_[:, (pt % 2) * 1024:(pt % 2) * 1024 + 1024]), 16)
                            pt = NH2 - 1
                            sy.wait_ge(dve.h, dve_v + pt + 1)
                            st.inc(sy.dma_start(
                                out=xout[hh * TH + pt * 128: hh * TH + (pt + 1) * 128, :],
                                in_=xor_[:, (pt % 2) * 1024:(pt % 2) * 1024 + 1024]), 16)

                        @block.tensor
                        def _(t):
                            t.wait_ge(ldw.h, ldw0 + 48)
                            for tt in range(NH2):
                                ps = (pp0, pp1)[tt % 2]
                                if tt == 0:
                                    t.wait_ge(act.h, act_k + NC)
                                    t.wait_ge(dve.h, dve_k + NC)
                                if tt >= 2:
                                    t.wait_ge(dve.h, dve_v + tt - 1)
                                for dc in range(NC):
                                    for fb in range(2):
                                        m_ = t.matmul(
                                            ps[:, fb * 512:(fb + 1) * 512],
                                            kaT[:, dc * 1024 + tt * 128: dc * 1024 + (tt + 1) * 128],
                                            W[:, 2 * 8192 + dc * 1024 + fb * 512: 2 * 8192 + dc * 1024 + (fb + 1) * 512],
                                            start=(dc == 0), stop=(dc == NC - 1))
                                        if dc == NC - 1 and fb == 1:
                                            mm.inc(m_, 1)

                        @block.vector
                        def _(v):
                            for tt in range(NH2):
                                v.wait_ge(mm.h, mm_v + tt + 1)
                                v.wait_ge(ldx.h, ldxv + (tt + 1) * 16)
                                if tt >= 2:
                                    v.wait_ge(st.h, st_v + (tt - 1) * 16)
                                v.tensor_mul(t1[:, :],
                                             (pp0, pp1)[tt % 2][:, :],
                                             rti[:, tt * 1024:(tt + 1) * 1024])
                                dve.inc(v.tensor_add(
                                    xor_[:, (tt % 2) * 1024:(tt % 2) * 1024 + 1024],
                                    xrr[:, (tt % 2) * 1024:(tt % 2) * 1024 + 1024],
                                    t1[:, :]), 1)

            return ret_st[0]

        def emit_attn_b(xin, xout, st_base=None):
            ret_stb = [None]
            HPC = MYH // 2

            with (
                nc.sbuf_tensor(_nm("b_W"), [128, 3 * NC * 1024], BF16) as W,
                nc.sbuf_tensor(_nm("b_bias"), [1, 4 * D], BF16) as bbias,
            ):
                f_mm, f_dve, f_act, f_st = mm.n, dve.n, act.n, st.n
                ldw0 = ldw.n

                @block.gpsimd
                def _(g):
                    g.wait_ge(mm.h, f_mm)
                    g.wait_ge(dve.h, f_dve)
                    g.wait_ge(act.h, f_act)
                    g.wait_ge(st.h, f_st)
                    for wi, w in enumerate((bWq, bWk, bWv)):
                        ldw.inc(g.dma_start(
                            out=W[:, wi * NC * 1024:(wi + 1) * NC * 1024]
                                .rearrange("p (c f) -> p c f", f=1024),
                            in_=w.rearrange("(c p) f -> p c f", p=128)), 16)
                    for bi, b_ in enumerate((bqb, bkb, bvb, bob)):
                        ldw.inc(g.dma_start(
                            out=bbias[:, bi * D:(bi + 1) * D], in_=b_[:, :]), 16)

                with (
                    nc.sbuf_tensor(_nm("b_xf"), [128, NT * 1024], F32) as xf,
                    nc.sbuf_tensor(_nm("b_sc"), [128, 1024], F32) as hsc,
                    nc.sbuf_tensor(_nm("b_ev"), [128, 2 * 1024], BF16) as ev2,
                ):
                    sp0 = emit_ln(xin, xf, hsc, ev2, 2, 1024, st_base=st_base,
                                  fence=((mm, f_mm), (dve, f_dve), (act, f_act)))

                if bphase < 2:
                    return
                roff = {}
                cc_x = cc.n
                with (
                    nc.sbuf_tensor(_nm("b_hT"), [128, NC * T], BF16) as hT,
                    nc.sbuf_tensor(_nm("b_rt1"), [128, 1024], F32) as rt1,
                    nc.sbuf_tensor(_nm("b_rt2"), [128, 1024], F32) as rt2,
                    nc.sbuf_tensor(_nm("b_qr"), [128, 2 * 1024], BF16) as qring,
                    nc.sbuf_tensor(_nm("b_kr"), [128, 2 * 1024], BF16) as kring,
                    nc.sbuf_tensor(_nm("b_vr"), [128, 2 * 1024], BF16) as vring,
                    nc.psum_tensor(_nm("b_pp0"), [128, 1024], F32) as pp0,
                    nc.psum_tensor(_nm("b_pp1"), [128, 1024], F32) as pp1,
                ):
                    ldx1 = ldx.n

                    @block.sync
                    def _(sy):
                        pair = pid_pair(sy)
                        roff["own"] = sy.snap(pair * (T * 512), donate=True)
                        roff["oth"] = sy.snap((1 - pair) * (T * 512), donate=True)
                        sy.wait_ge(sp.h, sp0 + NT * 16)
                        if dbg:
                            st.inc(sy.dma_start(out=dhb[:, :], in_=h_b[:, :]), 16)
                        for dc in range(NC):
                            ldx.inc(sy.dma_start_transpose(
                                hT[:, dc * T:(dc + 1) * T],
                                h_b[:, dc * 128:(dc + 1) * 128]), 16)

                    def rope(v, src, ring, tt):
                        v.tensor_mul(
                            AP(rt1, 0, [[1024, 128], [64, 16], [2, 32]]),
                            AP(src, 1, [[1024, 128], [64, 16], [2, 32]]),
                            AP(ropeSs, tt * 64, [[NT * 64, 128], [0, 16], [2, 32]]))
                        v.tensor_mul(
                            AP(rt1, 1, [[1024, 128], [64, 16], [2, 32]]),
                            AP(src, 0, [[1024, 128], [64, 16], [2, 32]]),
                            AP(ropeSs, tt * 64 + 1, [[NT * 64, 128], [0, 16], [2, 32]]))
                        v.tensor_mul(
                            AP(rt2, 0, [[1024, 128], [64, 16], [1, 64]]),
                            AP(src, 0, [[1024, 128], [64, 16], [1, 64]]),
                            AP(ropeCs, tt * 64, [[NT * 64, 128], [0, 16], [1, 64]]))
                        dve.inc(v.tensor_add(
                            ring[:, (tt % 2) * 1024:(tt % 2) * 1024 + 1024],
                            rt2[:, :], rt1[:, :]))

                    for wi, ring in ((0, qring), (1, kring), (2, vring)):
                        mm_e, dve_e, act_e, sp_e = mm.n, dve.n, act.n, sp.n

                        @block.tensor
                        def _(t, wi=wi, mm_e=mm_e, dve_e=dve_e, act_e=act_e):
                            if wi == 0:
                                t.wait_ge(ldw.h, ldw0 + 7 * 16)
                                t.wait_ge(ldx.h, ldx1 + NC * 16)
                            else:
                                t.wait_ge(dve.h, dve_e)
                                t.wait_ge(act.h, act_e)
                            for tt in range(NT):
                                ps = (pp0, pp1)[tt % 2]
                                if tt >= 2:
                                    if wi < 2:
                                        t.wait_ge(dve.h, dve_e + tt - 1)
                                    else:
                                        t.wait_ge(act.h, act_e + tt - 1)
                                for dc in range(NC):
                                    for fb in range(2):
                                        t.matmul(
                                            ps[:, fb * 512:(fb + 1) * 512],
                                            hT[:, dc * T + tt * 128: dc * T + (tt + 1) * 128],
                                            W[:, wi * 8192 + dc * 1024 + fb * 512:
                                              wi * 8192 + dc * 1024 + (fb + 1) * 512],
                                            start=(dc == 0), stop=False)
                                for fb in range(2):
                                    m_ = t.matmul(
                                        ps[:, fb * 512:(fb + 1) * 512],
                                        onesc[0:1, 0:128],
                                        bbias[0:1, wi * D + fb * 512: wi * D + (fb + 1) * 512],
                                        start=False, stop=True)
                                    if fb == 1:
                                        mm.inc(m_, 1)

                        if wi < 2:
                            @block.vector
                            def _(v, ring=ring, mm_e=mm_e, sp_e=sp_e):
                                for tt in range(NT):
                                    v.wait_ge(mm.h, mm_e + tt + 1)
                                    if tt >= 2:
                                        v.wait_ge(sp.h, sp_e + (tt - 2) * 32 + 32)
                                    rope(v, (pp0, pp1)[tt % 2], ring, tt)
                        else:
                            @block.scalar
                            def _(s, ring=ring, mm_e=mm_e, sp_e=sp_e):
                                for tt in range(NT):
                                    s.wait_ge(mm.h, mm_e + tt + 1)
                                    if tt >= 2:
                                        s.wait_ge(sp.h, sp_e + (tt - 2) * 32 + 32)
                                    act.inc(s.activation(
                                        ring[:, (tt % 2) * 1024:(tt % 2) * 1024 + 1024],
                                        (pp0, pp1)[tt % 2][:, :], AF.Identity), 1)

                        @block.sync
                        def _(sy, wi=wi, ring=ring, dve_e=dve_e, act_e=act_e):
                            for tt in range(NT):
                                if wi < 2:
                                    sy.wait_ge(dve.h, dve_e + tt + 1)
                                else:
                                    sy.wait_ge(act.h, act_e + tt + 1)
                                sp.inc(sy.dma_start(
                                    out=qkvo[wi * T + tt * 128: wi * T + (tt + 1) * 128, :],
                                    in_=ring[:, (tt % 2) * 1024:(tt % 2) * 1024 + 512]), 16)
                                sp.inc(sy.dma_start(
                                    out=bqkv[wi * T + tt * 128: wi * T + (tt + 1) * 128, :],
                                    in_=ring[:, (tt % 2) * 1024 + 512:(tt % 2) * 1024 + 1024]), 16)
                            sy.wait_ge(sp.h, sp.n)
                            sp.inc(sy.dma_start(
                                out=AP(qkvg.tensor, roff["own"] + wi * (S * 512),
                                       [[512, T], [1, 512]]),
                                in_=AP(qkvo.tensor, wi * T * 512,
                                       [[512, T], [1, 512]])), 16)

                        @block.gpsimd
                        def _(g, wi=wi, spv=sp_e + 32 * 16):
                            g.wait_ge(sp.h, spv)
                            cc.inc(g.collective_compute(
                                "AllGather", ALU.bypass, replica_groups=rg,
                                ins=[AP(bqkv.tensor, wi * T * 512,
                                        [[512, T], [1, 512]]).opt()],
                                outs=[AP(ag_bqkv.tensor, 2 * wi * T * 512,
                                         [[512, 2 * T], [1, 512]]).opt()]), 1)

                if bphase < 3:
                    return
                if bphase < 4:
                    return
                _Wog = nc.sbuf_tensor(_nm("b_Wo"), [128, NC * 1024], BF16)
                Wo = _Wog.__enter__()
                ldw1 = ldw.n

                @block.gpsimd
                def _(g):
                    ldw.inc(g.dma_start(
                        out=Wo[:, :].rearrange("p (c f) -> p c f", f=1024),
                        in_=bWo.rearrange("(c p) f -> p c f", p=128)), 16)

                with (
                    nc.sbuf_tensor(_nm("b_qT"), [128, HPC * S], BF16) as qT,
                    nc.sbuf_tensor(_nm("b_kT"), [128, HPC * S], BF16) as kT,
                    nc.sbuf_tensor(_nm("b_va"), [128, NJ * 520], BF16) as Vau,
                    nc.sbuf_tensor(_nm("b_ea"), [128, 3 * 1024], BF16) as eA,
                    nc.sbuf_tensor(_nm("b_yt"), [128, 2 * 2048], BF16) as ytk,
                    nc.sbuf_tensor(_nm("b_rz"), [128, 8], F32) as rz,
                    nc.sbuf_tensor(_nm("b_dz"), [128, 2 * 32], F32) as dzsb,
                    nc.psum_tensor(_nm("b_ps0"), [128, 1024], F32) as bs0,
                    nc.psum_tensor(_nm("b_ps1"), [128, 1024], F32) as bs1,
                    nc.psum_tensor(_nm("b_av0"), [128, 1024], F32) as av0,
                    nc.psum_tensor(_nm("b_av1"), [128, 1024], F32) as av1,
                ):
                    ldx2 = ldx.n
                    prj_dve, prj_act, prj_mm, prj_sp = dve.n, act.n, mm.n, sp.n
                    dve_o = dve.n

                    @block.vector
                    def _(v):
                        v.wait_ge(mm.h, prj_mm)
                        v.wait_ge(sp.h, prj_sp)
                        dve.inc(v.memset(
                            Vau[:, :].rearrange("p (j h c) -> p j h c", h=8, c=65)
                               [:, :, :, 64], 1.0))

                    @block.sync
                    def _(sy):
                        sy.wait_ge(dve.h, prj_dve)
                        sy.wait_ge(act.h, prj_act)
                        for wi in range(3):
                            sy.wait_ge(cc.h, cc_x + wi + 1)
                            sp.inc(sy.dma_start(
                                out=AP(qkvg.tensor, roff["oth"] + wi * (S * 512),
                                       [[512, T], [1, 512]]),
                                in_=AP(ag_bqkv.tensor, roff["oth"] + 2 * wi * (T * 512),
                                       [[512, T], [1, 512]])), 16)
                            sy.wait_ge(sp.h, sp.n)
                            if wi < 2:
                                dst = (qT, kT)[wi]
                                for hp in range(HPC):
                                    ldx.inc(sy.dma_start_transpose(
                                        dst[:, hp * S:(hp + 1) * S],
                                        AP(qkvg.tensor, wi * S * 512 + hp * 128,
                                           [[512, S], [1, 128]])), 16)
                            else:
                                if dbg:
                                    st.inc(sy.dma_start(out=dqb[0:3*T, 0:512], in_=qkvo[:, :]), 16)
                                    st.inc(sy.dma_start(out=dqkv[:, :],
                                                        in_=AP(qkvg.tensor, 0, [[512, 3 * S], [1, 512]])), 16)
                                sy.wait_ge(ldx.h, ldx2 + 8 * 16)
                                sy.wait_ge(dve.h, dve_o + 1)
                                for jt in range(NJ):
                                    ldx.inc(sy.dma_start(
                                        out=Vau[:, jt * 520:(jt + 1) * 520]
                                            .rearrange("p (h c) -> p h c", c=65)[:, :, 0:64],
                                        in_=AP(qkvg.tensor, 2 * S * 512 + jt * 128 * 512,
                                               [[512, 128], [64, 8], [1, 64]])), 16)

                    sched = []
                    for ib in range(NIB):
                        for hp in range(HPC):
                            jt_hi = 4 * (ib + 1)
                            for jt in range(jt_hi):
                                sched.append((ib, hp, jt, jt == jt_hi - 1))
                    NU = len(sched)
                    NG = NIB * HPC
                    grp_of = []
                    g = -1
                    for u, (ib, hp, jt, lastj) in enumerate(sched):
                        if jt == 0:
                            g += 1
                        grp_of.append(g)
                    diag = [jt >= 4 * ib for (ib, hp, jt, _l) in sched]

                    mm_s = mm.n
                    act_s = act.n
                    dve_s = dve.n
                    st_s = st.n

                    dve_after_mask = [0] * NU
                    dve_after_evac = [0] * NG
                    c = dve_s + 0
                    for u in range(NU):
                        if diag[u]:
                            c += 1
                        dve_after_mask[u] = c
                        if sched[u][3]:
                            c += 1
                            dve_after_evac[grp_of[u]] = c

                    sc_mm = [0] * NU
                    av_mm = [0] * NU

                    @block.tensor
                    def _(t):
                        t.wait_ge(act.h, prj_act)
                        t.wait_ge(ldx.h, ldx2 + 8 * 16)

                        def emit_scores(u):
                            ib, hp, jt, _l = sched[u]
                            ps = (bs0, bs1)[u % 2]
                            if u >= 2:
                                t.wait_ge(act.h, act_s + u - 1)
                            for h2 in range(2):
                                m_ = t.matmul(
                                    ps[:, h2 * 512:(h2 + 1) * 512],
                                    kT[64 * h2:64 * h2 + 64,
                                       hp * S + jt * 128: hp * S + (jt + 1) * 128],
                                    qT[64 * h2:64 * h2 + 64,
                                       hp * S + ib * 512: hp * S + (ib + 1) * 512],
                                    start=True, stop=True,
                                    tile_position=(64 * h2, 0))
                            mm.inc(m_, 1)
                            sc_mm[u] = mm.n

                        def emit_av(u):
                            ib, hp, jt, lastj = sched[u]
                            g_ = grp_of[u]
                            av = (av0, av1)[g_ % 2]
                            if u == 0:
                                t.wait_ge(ldx.h, ldx2 + (8 + NJ) * 16)
                            t.wait_ge(act.h, act_s + u + 1)
                            if diag[u]:
                                t.wait_ge(dve.h, dve_after_mask[u])
                            if jt == 0 and g_ >= 2:
                                t.wait_ge(dve.h, dve_after_evac[g_ - 2])
                            sl = (u % 3) * 1024
                            for h2 in range(2):
                                for q in range(4):
                                    m_ = t.matmul(
                                        av[:, (h2 * 4 + q) * 128:(h2 * 4 + q) * 128 + 65],
                                        eA[:, sl + h2 * 512 + q * 128: sl + h2 * 512 + (q + 1) * 128],
                                        Vau[:, jt * 520 + (hp * 2 + h2) * 65:
                                            jt * 520 + (hp * 2 + h2) * 65 + 65],
                                        start=(jt == 0 and q == 0),
                                        stop=(lastj and q == 3))
                            mm.inc(m_, 1)
                            av_mm[u] = mm.n

                        for u in range(NU):
                            emit_scores(u)
                            if u >= 1:
                                emit_av(u - 1)
                        emit_av(NU - 1)

                    @block.scalar
                    def _(s):
                        for u in range(NU):
                            s.wait_ge(mm.h, sc_mm[u])
                            if u >= 3:
                                s.wait_ge(mm.h, av_mm[u - 3])
                            act.inc(s.activation(
                                eA[:, (u % 3) * 1024:(u % 3) * 1024 + 1024],
                                (bs0, bs1)[u % 2][:, :], AF.Exp, scale=BSC), 1)

                    @block.vector
                    def _(v):
                        for u in range(NU):
                            ib, hp, jt, lastj = sched[u]
                            if diag[u]:
                                v.wait_ge(act.h, act_s + u + 1)
                                k_ = jt - 4 * ib
                                sl = (u % 3) * 1024
                                v.tensor_mul(eA[:, sl:sl + 512], eA[:, sl:sl + 512],
                                             masks[:, k_ * 512:(k_ + 1) * 512])
                                dve.inc(v.tensor_mul(
                                    eA[:, sl + 512:sl + 1024], eA[:, sl + 512:sl + 1024],
                                    masks[:, k_ * 512:(k_ + 1) * 512]))
                            if lastj:
                                g_ = grp_of[u]
                                av = (av0, av1)[g_ % 2]
                                v.wait_ge(mm.h, av_mm[u])
                                if ib >= 2 and hp == 0:
                                    v.wait_ge(st.h, st_s + (ib - 1) * 4 * 16)
                                for h2 in range(2):
                                    for q in range(4):
                                        base = (h2 * 4 + q) * 128
                                        if dbg:
                                            v.tensor_copy(
                                                dzsb[:, (ib % 2) * 32 + q * 8 + hp * 2 + h2:
                                                     (ib % 2) * 32 + q * 8 + hp * 2 + h2 + 1],
                                                av[:, base + 64:base + 65])
                                        v.reciprocal(rz[:, h2 * 4 + q:h2 * 4 + q + 1],
                                                     av[:, base + 64:base + 65])
                                for h2 in range(2):
                                    for q in range(4):
                                        base = (h2 * 4 + q) * 128
                                        m2 = v.tensor_scalar_mul(
                                            ytk[:, (ib % 2) * 2048 + q * 512 +
                                                (hp * 2 + h2) * 64:
                                                (ib % 2) * 2048 + q * 512 +
                                                (hp * 2 + h2) * 64 + 64],
                                            av[:, base:base + 64],
                                            rz[:, h2 * 4 + q:h2 * 4 + q + 1])
                                dve.inc(m2)

                    @block.sync
                    def _(sy):
                        for ib in range(NIB):
                            gend = dve_after_evac[ib * HPC + HPC - 1]
                            sy.wait_ge(dve.h, gend)
                            for q in range(4):
                                st.inc(sy.dma_start(
                                    out=yb_tok[(ib * 4 + q) * 128:(ib * 4 + q + 1) * 128, :],
                                    in_=ytk[:, (ib % 2) * 2048 + q * 512:
                                            (ib % 2) * 2048 + (q + 1) * 512]), 16)
                                if dbg:
                                    st.inc(sy.dma_start(
                                        out=dzt[(ib * 4 + q) * 128:(ib * 4 + q + 1) * 128, :],
                                        in_=dzsb[:, (ib % 2) * 32 + q * 8:
                                                 (ib % 2) * 32 + (q + 1) * 8]), 16)

                if bphase < 5:
                    return
                sp_ye = sp.n
                st_att = st.n

                @block.sync
                def _(sy):
                    sy.wait_ge(st.h, st_att)
                    if dbg:
                        st.inc(sy.dma_start(out=dyt[:, :],
                                            in_=AP(yb_tok.tensor, 0, [[512, S], [1, 512]])), 16)
                    sp.inc(sy.dma_start(
                        out=AP(ybx.tensor, 0, [[512, T], [1, 512]]),
                        in_=AP(yb_tok.tensor, roff["oth"], [[512, T], [1, 512]])), 16)
                    sp.inc(sy.dma_start(
                        out=AP(yg2.tensor, 0, [[1024, T], [1, 512]]),
                        in_=AP(yb_tok.tensor, roff["own"], [[512, T], [1, 512]])), 16)

                cc_y = cc.n

                @block.gpsimd
                def _(g):
                    g.wait_ge(sp.h, sp_ye + 16)
                    cc.inc(g.collective_compute(
                        "AllGather", ALU.bypass, replica_groups=rg,
                        ins=[AP(ybx.tensor, 0, [[512, T], [1, 512]]).opt()],
                        outs=[AP(ag_y2.tensor, 0, [[512, 2 * T], [1, 512]]).opt()]), 1)

                sp_ya = sp.n

                @block.sync
                def _(sy):
                    sy.wait_ge(cc.h, cc_y + 1)
                    sp.inc(sy.dma_start(
                        out=AP(yg2.tensor, 512, [[1024, T], [1, 512]]),
                        in_=AP(ag_y2.tensor, roff["oth"], [[512, T], [1, 512]])), 16)

                with (
                    nc.sbuf_tensor(_nm("b_yTo"), [128, NC * T], BF16) as yTo,
                    nc.sbuf_tensor(_nm("b_xr"), [128, 2 * 1024], F32) as xrr,
                    nc.sbuf_tensor(_nm("b_xo"), [128, 2 * 1024], F32) as xor_,
                    nc.psum_tensor(_nm("b_po0"), [128, 1024], F32) as po0,
                    nc.psum_tensor(_nm("b_po1"), [128, 1024], F32) as po1,
                ):
                    att_mm, att_dve = mm.n, dve.n

                    ldx3 = ldx.n

                    @block.sync
                    def _(sy):
                        sy.wait_ge(mm.h, att_mm)
                        sy.wait_ge(dve.h, att_dve)
                        sy.wait_ge(sp.h, sp_ye + 2 * 16)
                        for dc in range(4):
                            ldx.inc(sy.dma_start_transpose(
                                yTo[:, dc * T:(dc + 1) * T],
                                AP(yg2.tensor, dc * 128, [[1024, T], [1, 128]])), 16)
                        sy.wait_ge(sp.h, sp_ya + 16)
                        for dc in range(4, NC):
                            ldx.inc(sy.dma_start_transpose(
                                yTo[:, dc * T:(dc + 1) * T],
                                AP(yg2.tensor, dc * 128, [[1024, T], [1, 128]])), 16)

                    mm_o = mm.n
                    dve_v = dve.n
                    ldxv = ldx.n
                    st_v = st.n
                    ret_stb[0] = st_v

                    @block.sync
                    def _(sy):
                        for tt in range(NT):
                            if tt >= 2:
                                sy.wait_ge(dve.h, dve_v + tt - 1)
                            ldx.inc(sy.dma_start(
                                out=xrr[:, (tt % 2) * 1024:(tt % 2) * 1024 + 1024],
                                in_=xin[tt * 128:(tt + 1) * 128, :]), 16)
                            if tt >= 1:
                                pt = tt - 1
                                sy.wait_ge(dve.h, dve_v + pt + 1)
                                st.inc(sy.dma_start(
                                    out=xout[pt * 128:(pt + 1) * 128, :],
                                    in_=xor_[:, (pt % 2) * 1024:(pt % 2) * 1024 + 1024]), 16)
                        pt = NT - 1
                        sy.wait_ge(dve.h, dve_v + pt + 1)
                        st.inc(sy.dma_start(
                            out=xout[pt * 128:(pt + 1) * 128, :],
                            in_=xor_[:, (pt % 2) * 1024:(pt % 2) * 1024 + 1024]), 16)

                    @block.tensor
                    def _(t):
                        t.wait_ge(ldw.h, ldw1 + 16)
                        for tt in range(NT):
                            ps = (po0, po1)[tt % 2]
                            if tt >= 2:
                                t.wait_ge(dve.h, dve_v + tt - 1)
                            for dc in range(NC):
                                if tt == 0:
                                    t.wait_ge(ldx.h, ldx3 + (dc + 1) * 16)
                                for fb in range(2):
                                    t.matmul(
                                        ps[:, fb * 512:(fb + 1) * 512],
                                        yTo[:, dc * T + tt * 128: dc * T + (tt + 1) * 128],
                                        Wo[:, dc * 1024 + fb * 512: dc * 1024 + (fb + 1) * 512],
                                        start=(dc == 0), stop=False)
                            for fb in range(2):
                                m_ = t.matmul(
                                    ps[:, fb * 512:(fb + 1) * 512],
                                    onesc[0:1, 0:128],
                                    bbias[0:1, 3 * D + fb * 512: 3 * D + (fb + 1) * 512],
                                    start=False, stop=True)
                                if fb == 1:
                                    mm.inc(m_, 1)

                    @block.vector
                    def _(v):
                        for tt in range(NT):
                            v.wait_ge(mm.h, mm_o + tt + 1)
                            v.wait_ge(ldx.h, ldxv + (tt + 1) * 16)
                            if tt >= 2:
                                v.wait_ge(st.h, st_v + (tt - 1) * 16)
                            dve.inc(v.tensor_add(
                                xor_[:, (tt % 2) * 1024:(tt % 2) * 1024 + 1024],
                                xrr[:, (tt % 2) * 1024:(tt % 2) * 1024 + 1024],
                                (po0, po1)[tt % 2][:, :]))

                _Wog.__exit__(None, None, None)
            return ret_stb[0]

        cur = x0
        stb = None
        last = max(sublayers)
        _stb = lambda s: s if lnchase else None
        if 0 in sublayers:
            nxt = yout if last == 0 else xb0
            stb = emit_attn_a(cur, nxt)
            cur = nxt
        if 1 in sublayers:
            nxt = yout if last == 1 else xb1
            stb = emit_ffn(cur, nxt, st_base=_stb(stb))
            cur = nxt
        if 2 in sublayers:
            nxt = yout if last == 2 else xb0
            stb = emit_attn_b(cur, nxt, st_base=_stb(stb))
            cur = nxt
        if 3 in sublayers:
            emit_ffn(cur, yout, st_base=_stb(stb))

        @block.sync
        def _(sy):
            sy.wait_ge(st.h, st.n)
            sy.wait_ge(sp.h, sp.n)
            sy.wait_ge(ldx.h, ldx.n)
            sy.wait_ge(ldw.h, ldw.n)

    return nc



def make_inputs(full):
    x = np.asarray(full["L0_attn_a_x"], np.float32)
    W = np.asarray(full["L0_attn_a_W"], np.float32)
    b = np.asarray(full["L0_attn_a_b"], np.float32)
    Wr = np.asarray(full["L0_ffn_W_r"], np.float32)
    Wk = np.asarray(full["L0_ffn_W_k"], np.float32)
    Wv = np.asarray(full["L0_ffn_W_v"], np.float32)
    mur = np.asarray(full["L0_ffn_mu_r"], np.float32)
    muk = np.asarray(full["L0_ffn_mu_k"], np.float32)
    Wq = np.asarray(full["L0_attn_b_Wq"], np.float32)
    Wkb = np.asarray(full["L0_attn_b_Wk"], np.float32)
    Wvb = np.asarray(full["L0_attn_b_Wv"], np.float32)
    Wo = np.asarray(full["L0_attn_b_Wo"], np.float32)
    bq = np.asarray(full["L0_attn_b_bq"], np.float32)
    bk = np.asarray(full["L0_attn_b_bk"], np.float32)
    bv = np.asarray(full["L0_attn_b_bv"], np.float32)
    bo = np.asarray(full["L0_attn_b_bo"], np.float32)

    aWqT = np.ascontiguousarray(W[0:D].T)
    aWkT = np.ascontiguousarray(W[D:2 * D].T)
    aWvT = np.ascontiguousarray(W[2 * D:3 * D].T)
    ab = np.ascontiguousarray(b.reshape(1, 3 * D))
    fWrT = np.ascontiguousarray(Wr.T)
    fWkT = np.ascontiguousarray(Wk.T)
    fWvT = np.ascontiguousarray(Wv.T)
    fmur = np.ascontiguousarray(mur.reshape(NC, 128).T)
    fmuk = np.ascontiguousarray(muk.reshape(NC, 128).T)

    pos = np.arange(S, dtype=np.float32)
    freq = np.arange(0, DH, 2, dtype=np.float32) / DH
    inv_freq = (1.0 / 10000.0 ** freq).astype(np.float32)
    ang = np.outer(pos, inv_freq)
    cosT = np.repeat(np.cos(ang).astype(np.float32), 2, axis=1)
    sinT = np.sin(ang).astype(np.float32)
    sgn = np.empty((S, DH), np.float32)
    sgn[:, 0::2] = -sinT
    sgn[:, 1::2] = sinT

    m = np.zeros((128, 4 * 512), np.float32)
    for k in range(4):
        jj = 128 * k + np.arange(128)[:, None]
        ii = np.arange(512)[None, :]
        m[:, k * 512:(k + 1) * 512] = (jj <= ii).astype(np.float32)

    ins = []
    for c in range(8):
        bsel, hh = c // 2, c % 2
        t0 = hh * T
        myh = np.arange(hh * MYH, hh * MYH + MYH)
        oth = np.arange((1 - hh) * MYH, (1 - hh) * MYH + MYH)
        perm = np.concatenate([myh, oth])
        colp = (perm[:, None] * DH + np.arange(DH)[None, :]).reshape(-1)
        d = {
            "x0": np.ascontiguousarray(x[bsel, t0:t0 + T]),
            "aWqT": aWqT, "aWkT": aWkT, "aWvT": aWvT, "ab": ab,
            "fWrT": fWrT, "fWkT": fWkT, "fWvT": fWvT,
            "fmur": fmur, "fmuk": fmuk,
            "bWq": np.ascontiguousarray(Wq[:, colp]),
            "bWk": np.ascontiguousarray(Wkb[:, colp]),
            "bWv": np.ascontiguousarray(Wvb[:, colp]),
            "bWo": np.ascontiguousarray(Wo[colp, :]),
            "bqb": np.ascontiguousarray(bq[colp].reshape(1, D)),
            "bkb": np.ascontiguousarray(bk[colp].reshape(1, D)),
            "bvb": np.ascontiguousarray(bv[colp].reshape(1, D)),
            "bob": np.ascontiguousarray(bo.reshape(1, D)),
            "ropeC": np.ascontiguousarray(cosT[t0:t0 + T]),
            "ropeS": np.ascontiguousarray(sgn[t0:t0 + T]),
            "cmask": m,
            "msel": np.full((128, 1), float(hh), np.float32),
        }
        ins.append(d)
    return ins


def kernel(**inputs):
    nc = build_program()
    in_maps = make_inputs(inputs)
    res = run_bass_kernel_spmd(nc, in_maps, core_ids=list(range(8)))
    out = np.empty((B, S, D), np.float32)
    for c in range(8):
        bsel, hh = c // 2, c % 2
        out[bsel, hh * T:(hh + 1) * T] = res.results[c]["y"]
    return out
